# version 20
# baseline (speedup 1.0000x reference)
import numpy as np
import concourse.bass as bass
import concourse.mybir as mybir
from concourse.bass_utils import run_bass_kernel_spmd

F32 = mybir.dt.float32
BF16 = mybir.dt.bfloat16
AF = mybir.ActivationFunctionType
ALU = mybir.AluOpType
AX = mybir.AxisListType

B, S, D = 4, 2048, 2048
H, HD = 16, 128
M = 4
NCORES = 8
T_OWN = 1024
NC_OWN = 8
NC_ALL = 16
ND = 16
SCALE = HD ** -0.5
CGROUPS = [[0, 1], [2, 3], [4, 5], [6, 7]]
LN_EPS = 1e-5
ATT_EPS = 1e-6
FFN_EPS = 1e-8
KI = 8192
I_DIM = 1024
E_EXP = 128


class Rec:
    __slots__ = ("eng", "fn", "is_dma", "is_coll", "deps", "marked", "val", "dbg")

    def __init__(self, eng, fn, is_dma, dbg, is_coll=False):
        self.eng = eng
        self.fn = fn
        self.is_dma = is_dma
        self.is_coll = is_coll
        self.deps = []
        self.marked = is_dma or is_coll
        self.val = None
        self.dbg = dbg


class Prog:
    CENG = ("t", "v", "s", "g", "y")

    def __init__(self, nc):
        self.nc = nc
        self.streams = {e: [] for e in self.CENG}
        self.regions = {}
        self.free_barrier = []
        self.birth = {}
        self.stacks = {"L": [], "R": [], "P": []}

    def sb(self, name, shape, dtype, side="left", **kw):
        g = self.nc.sbuf_tensor(name, shape, dtype, side=side, **kw)
        h = g.__enter__()
        self.stacks["R" if side == "right" else "L"].append([name, g, False])
        self.birth[name] = list(self.free_barrier)
        return h

    def ps(self, name, shape, dtype=F32):
        bankw = 512 if dtype == F32 else 1024
        shape = list(shape)
        shape[-1] = ((shape[-1] + bankw - 1) // bankw) * bankw
        g = self.nc.psum_tensor(name, shape, dtype)
        h = g.__enter__()
        self.stacks["P"].append([name, g, False])
        self.birth[name] = list(self.free_barrier)
        return h

    def free(self, h):
        name = h.name
        for stack in self.stacks.values():
            for ent in stack:
                if ent[0] == name:
                    ent[2] = True
                    for (n, k), st in list(self.regions.items()):
                        if n == name:
                            if st[0] is not None:
                                self.free_barrier.append(st[0])
                            self.free_barrier.extend(st[1])
                            del self.regions[(n, k)]
                    while stack and stack[-1][2]:
                        _, g, _ = stack.pop()
                        g.__exit__(None, None, None)
                    return
        raise KeyError(name)

    def _region(self, tensor_name, key):
        st = self.regions.get((tensor_name, key))
        if st is None:
            st = [None, []]
            self.regions[(tensor_name, key)] = st
        return st

    def op(self, eng, fn, reads=(), writes=(), dma=False, dbg="", coll=False):
        rec = Rec(eng, fn, dma, dbg, is_coll=coll)
        deps = []
        for t, k in reads:
            st = self._region(t, k)
            if st[0] is not None:
                deps.append(st[0])
        for t, k in writes:
            st = self._region(t, k)
            if st[0] is not None:
                deps.append(st[0])
            deps.extend(st[1])
        for t, k in writes:
            if self.regions.get((t, k), [None])[0] is None:
                bd = self.birth.get(t)
                if bd:
                    deps.extend(bd)
        seen = set()
        for d in deps:
            if id(d) in seen or d is rec:
                continue
            seen.add(id(d))
            if (d.eng == "t" and eng == "t" and not d.is_dma and not dma):
                continue
            rec.deps.append(d)
            d.marked = True
        for t, k in reads:
            self._region(t, k)[1].append(rec)
        for t, k in writes:
            st = self._region(t, k)
            st[0] = rec
            st[1] = []
        self.streams[eng].append(rec)
        return rec

    def dma(self, q, out, in_, reads=(), writes=(), transpose=False, dbg=""):
        def fn(e):
            return e.dma_start(out=out, in_=in_, transpose=transpose)
        return self.op(q, fn, reads, writes, dma=True, dbg=dbg)

    def emit(self, final_outputs=()):
        nc = self.nc
        sink_deps = []
        for t, k in final_outputs:
            st = self.regions.get((t, k))
            if st and st[0] is not None:
                sink_deps.append(st[0])
                st[0].marked = True

        NSLOT = 8
        NCOLL = 4
        csem = {e: nc.alloc_semaphore(name=f"prog_{e}") for e in self.CENG}
        dsem = {(e, s): nc.alloc_semaphore(name=f"dma_{e}{s}")
                for e in ("y", "g", "s") for s in range(NSLOT)}
        for s in range(NCOLL):
            dsem[("o", s)] = nc.alloc_semaphore(name=f"coll{s}")
        cnt = {e: 0 for e in self.CENG}
        dcnt = {e: 0 for e in ("y", "g", "s")}
        ocnt = [0]
        for e in self.CENG:
            for rec in self.streams[e]:
                if not rec.marked:
                    continue
                if rec.is_coll:
                    slot = ocnt[0] % NCOLL
                    use = ocnt[0] // NCOLL + 1
                    ocnt[0] += 1
                    rec.val = ("d", ("o", slot), use)
                elif rec.is_dma:
                    slot = dcnt[e] % NSLOT
                    use = dcnt[e] // NSLOT + 1
                    dcnt[e] += 1
                    rec.val = ("d", (e, slot), 16 * use)
                else:
                    cnt[e] += 1
                    rec.val = ("c", e, cnt[e])

        hiwater = {e: {} for e in self.CENG}

        def get_sem(kind, de):
            return dsem[de] if kind == "d" else csem[de]

        def do_wait(engobj, hw, kind, de, v):
            key = (kind, de)
            if hw.get(key, 0) >= v:
                return
            engobj.wait_ge(get_sem(kind, de), v)
            hw[key] = v

        def emit_stream(e, engobj):
            hw = hiwater[e]
            for rec in self.streams[e]:
                for d in rec.deps:
                    kind, de, v = d.val
                    do_wait(engobj, hw, kind, de, v)
                if rec.marked and (rec.is_dma or rec.is_coll):
                    kind, de, v = rec.val
                    step = 1 if rec.is_coll else 16
                    if v > step:
                        do_wait(engobj, hw, kind, de, v - step)
                ins = rec.fn(engobj)
                if rec.marked:
                    kind, de, v = rec.val
                    amt = 16 if (kind == "d" and not rec.is_coll) else 1
                    ins.then_inc(get_sem(kind, de), amt)
            if e == "g":
                for d in sink_deps:
                    kind, de, v = d.val
                    do_wait(engobj, hw, kind, de, v)

        with nc.Block() as block:
            @block.tensor
            def _(t):
                emit_stream("t", t)

            @block.vector
            def _(v):
                emit_stream("v", v)

            @block.scalar
            def _(s):
                emit_stream("s", s)

            @block.gpsimd
            def _(g):
                emit_stream("g", g)

            @block.sync
            def _(y):
                emit_stream("y", y)


def build_nc():
    nc = bass.Bass("TRN2", target_bir_lowering=False)
    P = Prog(nc)

    def din(name, shape):
        return nc.dram_tensor(name, shape, F32, kind="ExternalInput").ap()

    xo = din("xo", [T_OWN, D])
    bmask = din("bmask", [128, 1])
    wqT = din("wqT", [D, D])
    wkT = din("wkT", [D, D])
    wvT = din("wvT", [D, D])
    woT = din("woT", [D, D])
    memT = din("memT", [128, M * H * 128])
    mnT = din("mnT", [128, H * M])
    gate1 = din("gate1", [1, H])
    wrt = din("wrt", [D, E_EXP])
    wg = din("wg", [D, KI])
    wu = din("wu", [D, KI])
    wd = din("wd", [I_DIM, D])
    amps = din("amps", [128, 16])
    escale = din("escale", [128, 1])
    swg = din("swg", [D, 2 * I_DIM])
    swu = din("swu", [D, 2 * I_DIM])
    swd = din("swd", [2 * I_DIM, D])

    yout = nc.dram_tensor("y", [T_OWN, D], F32, kind="ExternalOutput").ap()
    ln1_dram = nc.dram_tensor("ln1_dram", [T_OWN, D], BF16, kind="Internal").ap()
    cx_in = nc.dram_tensor("cx_in", [128, 16], F32, kind="Internal")
    cx_sum = nc.dram_tensor("cx_sum", [128, 16], F32, kind="Internal")
    cs_in = nc.dram_tensor("cs_in", [8, 128, 258], F32, kind="Internal")
    cs_sum = nc.dram_tensor("cs_sum", [8, 128, 258], F32, kind="Internal")
    attn_dram = nc.dram_tensor("attn_dram", [T_OWN, D], BF16, kind="Internal").ap()
    ln2_dram = nc.dram_tensor("ln2_dram", [T_OWN, D], BF16, kind="Internal").ap()
    u_dram = nc.dram_tensor("u_dram", [T_OWN, I_DIM], BF16, kind="Internal").ap()
    x1_dram = nc.dram_tensor("x1_dram", [T_OWN, D], F32, kind="Internal").ap()

    R = dict(side="right")
    ones_col = P.sb("ones_col", [128, 1], BF16, **R)
    ones_row = P.sb("ones_row", [1, 128], BF16, **R)
    ident_bf = P.sb("ident_bf", [128, 128], BF16, **R)
    mask01 = P.sb("mask01", [128, 256], BF16, **R)
    bmask_sb = P.sb("bmask_sb", [128, 1], F32, **R)
    stat = P.sb("stat", [128, 24], F32, **R)
    mv = P.sb("mv", [128, 2], F32, **R)
    rstd = P.sb("rstd", [128, 1], F32, **R)
    nmean = P.sb("nmean", [128, 1], F32, **R)
    xbar = P.sb("xbar", [128, ND], BF16, **R)
    xbf = P.sb("xbf", [128, 1], F32, **R)
    qbar_sb = P.sb("qbar_sb", [128, 16], BF16, **R)
    mn_sb = P.sb("mn_sb", [128, H * M], BF16, **R)
    sc1 = P.sb("sc1", [1, 256], F32, **R)
    sc1b = P.sb("sc1b", [1, 80], BF16, **R)
    eps_sb = P.sb("eps_sb", [128, 1], F32, **R)
    csb = P.sb("csb", [128, 80], F32, **R)

    P.op("g", lambda e: e.memset(ones_col[:], 1.0), writes=[("ones_col", 0)])
    P.op("g", lambda e: e.memset(ones_row[:], 1.0), writes=[("ones_row", 0)])
    P.op("g", lambda e: e.memset(ident_bf[:], 0.0), writes=[("ident_bf", 0)])
    P.op("g", lambda e: e.affine_select(
        out=ident_bf[:], in_=ident_bf[:], compare_op=ALU.not_equal, fill=1.0,
        base=0, pattern=[[-1, 128]], channel_multiplier=1),
        reads=[("ident_bf", 0)], writes=[("ident_bf", 0)])
    P.op("g", lambda e: e.memset(mask01[:], 1.0), writes=[("mask01", 0)])
    P.op("g", lambda e: e.affine_select(
        out=mask01[:].rearrange("p (q i) -> p q i", i=128),
        in_=mask01[:].rearrange("p (q i) -> p q i", i=128),
        compare_op=ALU.is_ge, fill=0.0,
        base=0, pattern=[[0, 2], [1, 128]], channel_multiplier=-1),
        reads=[("mask01", 0)], writes=[("mask01", 0)])
    P.dma("y", bmask_sb[:], bmask, writes=[("bmask_sb", 0)])
    P.op("g", lambda e: e.memset(eps_sb[:], LN_EPS), writes=[("eps_sb", 0)])
    P.dma("g", mn_sb[:], mnT, writes=[("mn_sb", 0)])

    aT = P.sb("aT", [128, ND, T_OWN], BF16)
    sigq = P.sb("sigq", [128, H, T_OWN], BF16)
    attn_fin = P.sb("attn_fin", [128, NC_OWN, D], BF16)
    ln1T = P.sb("ln1T", [128, ND, T_OWN], BF16, is_transpose_dest=True)
    elu_t2 = [P.sb(f"elu_t{i}", [128, 512], BF16) for i in range(2)]
    elu_e2 = [P.sb(f"elu_e{i}", [128, 512], BF16) for i in range(2)]
    elu_ctr = [0]

    def emit_elu(src_psum, src_regs, dst_ap, dst_regs, n):
        i = elu_ctr[0] % 2
        elu_ctr[0] += 1
        elu_t = elu_t2[i]
        elu_e = elu_e2[i]
        P.op("v", lambda e, elu_t=elu_t: e.tensor_scalar_min(elu_t[:, :n], src_psum, 0.0),
             reads=src_regs, writes=[(elu_t.name, 0)])
        P.op("s", lambda e, elu_t=elu_t, elu_e=elu_e: e.activation(elu_e[:, :n], elu_t[:, :n], AF.Exp),
             reads=[(elu_t.name, 0)], writes=[(elu_e.name, 0)])
        P.op("v", lambda e, elu_e=elu_e: e.scalar_tensor_tensor(
            dst_ap, src_psum, 1.0, elu_e[:, :n], ALU.add, ALU.max),
            reads=list(src_regs) + [(elu_e.name, 0)], writes=dst_regs)


    xbuf = [P.sb(f"xbuf{i}", [128, D], F32) for i in range(2)]
    lnbuf = [P.sb(f"lnbuf{i}", [128, D], BF16) for i in range(2)]

    def ln_tile(src_ap, xb, lb, store_ap, wkey, xsrc_key):
        P.dma("y", xb[:], src_ap, reads=[xsrc_key] if xsrc_key else [],
              writes=[(xb.name, 0)])
        for j in range(4):
            P.op("v", lambda e, j=j, xb=xb: e.bn_stats(
                stat[:, 6 * j:6 * j + 6], xb[:, 512 * j:512 * (j + 1)]),
                reads=[(xb.name, 0)], writes=[("stat", j)])
        P.op("v", lambda e: e.bn_aggr(mv[:], stat[:]),
             reads=[("stat", j) for j in range(4)], writes=[("mv", 0)])
        P.op("s", lambda e: e.activation(rstd[:], mv[:, 1:2], AF.Sqrt, bias=eps_sb[:]),
             reads=[("mv", 0), ("eps_sb", 0)], writes=[("rstd", 0)])
        P.op("v", lambda e: e.reciprocal(rstd[:], rstd[:]),
             reads=[("rstd", 0)], writes=[("rstd", 0)])
        P.op("v", lambda e: e.tensor_tensor(nmean[:], mv[:, 0:1], rstd[:], ALU.mult),
             reads=[("mv", 0), ("rstd", 0)], writes=[("nmean", 0)])
        P.op("v", lambda e: e.tensor_scalar_mul(nmean[:], nmean[:], -1.0),
             reads=[("nmean", 0)], writes=[("nmean", 0)])
        P.op("s", lambda e, xb=xb, lb=lb: e.activation(
            lb[:], xb[:], AF.Identity, bias=nmean[:], scale=rstd[:]),
            reads=[(xb.name, 0), ("rstd", 0), ("nmean", 0)], writes=[(lb.name, 0)])
        P.dma("s", store_ap, lb[:], reads=[(lb.name, 0)], writes=[wkey])

    for i in range(NC_OWN):
        ln_tile(xo[128 * i:128 * (i + 1), :], xbuf[i % 2], lnbuf[i % 2],
                ln1_dram[128 * i:128 * (i + 1), :], ("ln1_dram", i), None)
    for b in xbuf + lnbuf:
        P.free(b)

    ln_own = [("ln1_dram", i) for i in range(NC_OWN)]
    for j in range(ND):
        P.dma("y", ln1T[:, j, :],
              ln1_dram[:, 128 * j:128 * (j + 1)],
              transpose=True, reads=ln_own, writes=[("ln1T", (j, 0))])
    xpart = P.sb("xpart", [128, ND], F32, side="right")
    for j in range(ND):
        P.op("v", lambda e, j=j: e.reduce_sum(xbf[:], ln1T[:, j, :], axis=AX.X),
             reads=[("ln1T", (j, 0))], writes=[("xbf", 0)])
        P.op("v", lambda e, j=j: e.tensor_scalar(
            xpart[:, j:j + 1], xbf[:], SCALE / (2.0 * T_OWN), None, ALU.mult),
            reads=[("xbf", 0)], writes=[("xpart", j)])
    P.dma("y", cx_in.ap(), xpart[:],
          reads=[("xpart", j) for j in range(ND)], writes=[("cx_in", 0)])
    P.op("g", lambda e: e.collective_compute(
        "AllReduce", ALU.add, replica_groups=CGROUPS,
        ins=[cx_in.ap().opt()], outs=[cx_sum.ap().opt()]),
        reads=[("cx_in", 0)], writes=[("cx_sum", 0)], coll=True, dbg="cx")
    P.dma("y", xpart[:], cx_sum.ap(),
          reads=[("cx_sum", 0)], writes=[("xpart", j) for j in range(ND)])
    P.op("v", lambda e: e.tensor_copy(xbar[:], xpart[:]),
         reads=[("xpart", j) for j in range(ND)],
         writes=[("xbar", j) for j in range(ND)])

    HGW = 2
    NHG = H // HGW
    BW = HGW * 128

    wk_d = [P.sb("wk_b0", [128, ND, BW], BF16)]
    wv_d = [P.sb("wv_b0", [128, ND, BW], BF16)]
    kp_hg = P.sb("kp_hg", [128, NC_OWN, BW], BF16)
    kpT_g = [P.sb(f"kpT_g{i}", [128, NC_OWN, BW], BF16) for i in range(2)]
    v_g = [P.sb(f"v_g{i}", [128, NC_OWN, BW], BF16) for i in range(2)]
    spall_g = [P.sb(f"spall{i}", [128, NC_OWN, HGW, 132], BF16) for i in range(2)]
    exch_g = [P.sb(f"exch{i}", [128, 258], F32) for i in range(2)]
    recv_g = [P.sb(f"recv{i}", [128, 258], F32) for i in range(2)]
    sbase_g = [P.sb(f"sbase{i}", [128, HGW, 132], BF16) for i in range(2)]
    scm = P.sb("scm", [128, BW], BF16)
    den_sb = P.sb("den_sb", [128, HGW], F32)

    pk1 = P.ps("pk1", [128, BW])
    pk2 = P.ps("pk2", [128, BW])
    pk3 = P.ps("pk3", [128, BW])
    pkv = P.ps("pkv", [128, 272])
    psc = P.ps("psc", [128, BW])
    pnum = P.ps("pnum", [128, 272])
    pkt2 = [P.ps(f"pkt{i}", [128, 128], BF16) for i in range(2)]
    pk_rot = [pk1, pk2, pk3]

    def produce_hg(hg, mid=None):
        g3 = hg % 2
        kpT_hg = kpT_g[g3]
        v_hg = v_g[g3]
        spall = spall_g[g3]
        exch = exch_g[g3]

        def state_copies(tc):
            for hh in range(HGW):
                if tc > 0:
                    P.op("s", lambda e, tc=tc, hh=hh: e.copy(
                        spall[:, tc, hh, 0:128], pkv[:, 128 * hh:128 * (hh + 1)]),
                        reads=[("pkv", 0)], writes=[(spall.name, (tc, hh))])
                    P.op("s", lambda e, tc=tc, hh=hh: e.copy(
                        spall[:, tc, hh, 128:129], pkv[:, 256 + hh:257 + hh]),
                        reads=[("pkv", 0)], writes=[(spall.name, (tc, hh))])
                else:
                    P.op("s", lambda e, hh=hh: e.memzero(
                        spall[:, 0, hh, 0:130]),
                        writes=[(spall.name, (0, hh))])

        def state_mms(tc):
            for hh in range(HGW):
                P.op("t", lambda e, tc=tc, hh=hh: e.matmul(
                    pkv[:, 128 * hh:128 * (hh + 1)],
                    kp_hg[:, tc, 128 * hh:128 * (hh + 1)],
                    v_hg[:, tc, 128 * hh:128 * (hh + 1)],
                    start=(tc == 0 and hh == 0), stop=True,
                    skip_group_check=(tc != 0 or hh != 0)),
                    reads=[("kp_hg", tc), (v_hg.name, tc)],
                    writes=[("pkv", 0)], dbg=f"upd{hg}")
                P.op("t", lambda e, tc=tc, hh=hh: e.matmul(
                    pkv[:, 256 + hh:257 + hh],
                    kp_hg[:, tc, 128 * hh:128 * (hh + 1)],
                    ones_col[:, 0:1],
                    start=False, stop=True, skip_group_check=True),
                    reads=[("kp_hg", tc), ("ones_col", 0)],
                    writes=[("pkv", 0)], dbg=f"updz{hg}")
        wk_b = wk_d[0]
        wv_b = wv_d[0]
        for j in range(ND):
            P.dma("g", wk_b[:, j, :],
                  wkT[128 * j:128 * (j + 1), BW * hg:BW * (hg + 1)],
                  writes=[(wk_b.name, j)])
            P.dma("g", wv_b[:, j, :],
                  wvT[128 * j:128 * (j + 1), BW * hg:BW * (hg + 1)],
                  writes=[(wv_b.name, j)])
        for tc in range(NC_OWN):
            pk = pk_rot[tc % 3]
            for j in range(ND):
                P.op("t", lambda e, tc=tc, j=j, pk=pk, wk_b=wk_b: e.matmul(
                    pk[:, 0:BW], ln1T[:, j, 128 * tc:128 * (tc + 1)], wk_b[:, j, :],
                    start=(j == 0), stop=(j == ND - 1)),
                    reads=[("ln1T", (j, 0)), (wk_b.name, j)],
                    writes=[(pk.name, 0)], dbg=f"k{hg}t{tc}")
            emit_elu(pk[:, 0:BW], [(pk.name, 0)], kp_hg[:, tc, :], [("kp_hg", tc)], BW)
        if mid is not None:
            mid()
        for tc in range(NC_OWN):
            if tc > 0:
                state_copies(tc - 1)
            pk = pk_rot[tc % 3]
            for j in range(ND):
                P.op("t", lambda e, tc=tc, j=j, pk=pk, wv_b=wv_b: e.matmul(
                    pk[:, 0:BW], ln1T[:, j, 128 * tc:128 * (tc + 1)], wv_b[:, j, :],
                    start=(j == 0), stop=(j == ND - 1)),
                    reads=[("ln1T", (j, 0)), (wv_b.name, j)],
                    writes=[(pk.name, 0)], dbg=f"v{hg}t{tc}")
            P.op("s", lambda e, tc=tc, pk=pk, v_hg=v_hg: e.copy(v_hg[:, tc, :], pk[:, 0:BW]),
                 reads=[(pk.name, 0)], writes=[(v_hg.name, tc)])
            for hh in range(HGW):
                pkt = pkt2[hh]
                P.op("t", lambda e, tc=tc, hh=hh, pkt=pkt: e.transpose(
                    pkt[:, 0:128], kp_hg[:, tc, 128 * hh:128 * (hh + 1)],
                    ident_bf[:]),
                    reads=[("kp_hg", tc), ("ident_bf", 0)],
                    writes=[(pkt.name, 0)], dbg=f"kpt{hg}")
                P.op("s", lambda e, tc=tc, hh=hh, pkt=pkt, kpT_hg=kpT_hg: e.copy(
                    kpT_hg[:, tc, 128 * hh:128 * (hh + 1)], pkt[:, 0:128]),
                    reads=[(pkt.name, 0)], writes=[(kpT_hg.name, tc)])
            if tc > 0:
                state_mms(tc - 1)
            if tc == NC_OWN - 1:
                state_copies(tc)
                state_mms(tc)
        for hh in range(HGW):
            P.op("s", lambda e, hh=hh, exch=exch: e.copy(
                exch[:, 129 * hh:129 * hh + 128], pkv[:, 128 * hh:128 * (hh + 1)]),
                reads=[("pkv", 0)], writes=[(exch.name, hh)])
            P.op("s", lambda e, hh=hh, exch=exch: e.copy(
                exch[:, 129 * hh + 128:129 * hh + 129], pkv[:, 256 + hh:257 + hh]),
                reads=[("pkv", 0)], writes=[(exch.name, hh)])
        P.dma("y", cs_in.ap()[hg], exch[:, 0:258],
              reads=[(exch.name, hh) for hh in range(HGW)],
              writes=[("cs_in", hg)])
        P.op("g", lambda e, hg=hg: e.collective_compute(
            "AllReduce", ALU.add, replica_groups=CGROUPS,
            ins=[cs_in.ap()[hg].opt()], outs=[cs_sum.ap()[hg].opt()]),
            reads=[("cs_in", hg)], writes=[("cs_sum", hg)], coll=True,
            dbg=f"cs{hg}")

    def prep_hg(hg):
        g3 = hg % 2
        exch = exch_g[g3]
        recv = recv_g[g3]
        sbase = sbase_g[g3]
        P.dma("y", recv[:, 0:258], cs_sum.ap()[hg],
              reads=[("cs_sum", hg)], writes=[(recv.name, 0)])
        P.op("v", lambda e, recv=recv, exch=exch: e.tensor_tensor(
            recv[:, 0:258], recv[:, 0:258], exch[:, 0:258], ALU.subtract),
            reads=[(recv.name, 0)] + [(exch.name, hh) for hh in range(HGW)],
            writes=[(recv.name, 0)])
        for hh in range(HGW):
            P.op("s", lambda e, hh=hh, recv=recv, sbase=sbase: e.activation(
                sbase[:, hh, 0:129], recv[:, 129 * hh:129 * hh + 129],
                AF.Identity, scale=bmask_sb[:]),
                reads=[(recv.name, 0), ("bmask_sb", 0)],
                writes=[(sbase.name, hh)])

    def local_hg(hg):
        g3 = hg % 2
        kpT_hg = kpT_g[g3]
        v_hg = v_g[g3]
        spall = spall_g[g3]
        sbase = sbase_g[g3]
        for tc in range(NC_OWN):
            for hh in range(HGW):
                h = HGW * hg + hh
                P.op("t", lambda e, tc=tc, h=h, hh=hh, kpT_hg=kpT_hg: e.matmul(
                    psc[:, 128 * hh:128 * (hh + 1)],
                    kpT_hg[:, tc, 128 * hh:128 * (hh + 1)],
                    sigq[:, h, 128 * tc:128 * (tc + 1)], start=True, stop=True),
                    reads=[(kpT_hg.name, tc), ("sigq", h)],
                    writes=[("psc", hh)] + ([("psc", "qb")] if (hg == 0 and tc == 0 and hh == 0) else []),
                    dbg=f"sc{h}")
            P.op("s", lambda e: e.copy(scm[:], psc[:, 0:BW]),
                 reads=[("psc", hh2) for hh2 in range(HGW)],
                 writes=[("scm", 0)], dbg=f"mcp{hg}t{tc}")
            P.op("g", lambda e: e.affine_select(
                out=scm[:].rearrange("p (q e) -> p q e", e=128),
                in_=scm[:].rearrange("p (q e) -> p q e", e=128),
                compare_op=ALU.is_ge, fill=0.0,
                base=0, pattern=[[0, HGW], [1, 128]], channel_multiplier=-1),
                reads=[("scm", 0)], writes=[("scm", 0)], dbg=f"mask{hg}t{tc}")
            nmm = [0]
            NTOT = HGW * 6

            def flg():
                nmm[0] += 1
                return dict(start=(nmm[0] == 1), stop=(nmm[0] == NTOT))
            for hh in range(HGW):
                h = HGW * hg + hh
                P.op("t", lambda e, tc=tc, h=h, hh=hh, spall=spall, f=flg(): e.matmul(
                    pnum[:, 128 * hh:128 * (hh + 1)],
                    sigq[:, h, 128 * tc:128 * (tc + 1)],
                    spall[:, tc, hh, 0:128], **f),
                    reads=[("sigq", h), (spall.name, (tc, hh))],
                    writes=[("pnum", 0)] + ([("pnum", "rel")] if (hg == 0 and tc == 0 and hh == 0) else []),
                    dbg=f"num{h}")
                P.op("t", lambda e, tc=tc, h=h, hh=hh, sbase=sbase, f=flg(): e.matmul(
                    pnum[:, 128 * hh:128 * (hh + 1)],
                    sigq[:, h, 128 * tc:128 * (tc + 1)],
                    sbase[:, hh, 0:128], **f),
                    reads=[("sigq", h), (sbase.name, hh)],
                    writes=[("pnum", 0)], dbg=f"numb{h}")
                P.op("t", lambda e, tc=tc, h=h, hh=hh, spall=spall, f=flg(): e.matmul(
                    pnum[:, 256 + hh:257 + hh],
                    sigq[:, h, 128 * tc:128 * (tc + 1)],
                    spall[:, tc, hh, 128:129], **f),
                    reads=[("sigq", h), (spall.name, (tc, hh))],
                    writes=[("pnum", 0)], dbg=f"den{h}")
                P.op("t", lambda e, tc=tc, h=h, hh=hh, sbase=sbase, f=flg(): e.matmul(
                    pnum[:, 256 + hh:257 + hh],
                    sigq[:, h, 128 * tc:128 * (tc + 1)],
                    sbase[:, hh, 128:129], **f),
                    reads=[("sigq", h), (sbase.name, hh)],
                    writes=[("pnum", 0)], dbg=f"denb{h}")
            for hh in range(HGW):
                h = HGW * hg + hh
                P.op("t", lambda e, tc=tc, hh=hh, v_hg=v_hg, f=flg(): e.matmul(
                    pnum[:, 128 * hh:128 * (hh + 1)],
                    scm[:, 128 * hh:128 * (hh + 1)],
                    v_hg[:, tc, 128 * hh:128 * (hh + 1)], **f),
                    reads=[("scm", 0), (v_hg.name, tc)],
                    writes=[("pnum", 0)], dbg=f"numi{h}")
                P.op("t", lambda e, tc=tc, hh=hh, f=flg(): e.matmul(
                    pnum[:, 256 + hh:257 + hh],
                    scm[:, 128 * hh:128 * (hh + 1)],
                    ones_col[:, 0:1], **f),
                    reads=[("scm", 0), ("ones_col", 0)],
                    writes=[("pnum", 0)], dbg=f"deni{h}")
            P.op("v", lambda e: e.tensor_scalar_max(
                den_sb[:], pnum[:, 256:256 + HGW], ATT_EPS),
                reads=[("pnum", 0)], writes=[("den_sb", 0)])
            P.op("v", lambda e: e.reciprocal(den_sb[:], den_sb[:]),
                 reads=[("den_sb", 0)], writes=[("den_sb", 0)])
            P.op("v", lambda e, hg=hg: e.tensor_tensor(
                den_sb[:], den_sb[:],
                csb[:, 64 + HGW * hg:64 + HGW * (hg + 1)], ALU.mult),
                reads=[("den_sb", 0), ("csb", 0)], writes=[("den_sb", 0)])
            P.op("v", lambda e, tc=tc, hg=hg: e.tensor_tensor(
                attn_fin[:, tc, 256 * hg:256 * (hg + 1)]
                .rearrange("p (q e) -> p q e", e=128),
                pnum[:, 0:256].rearrange("p (q e) -> p q e", e=128),
                den_sb[:, 0:HGW].rearrange("p q -> p q ()")
                .to_broadcast([128, HGW, 128]), ALU.mult),
                reads=[("pnum", 0), ("den_sb", 0)],
                writes=[("attn_fin", (tc, HGW * hg)),
                        ("attn_fin", (tc, HGW * hg + 1))], dbg=f"loc{hg}")

    produce_hg(0)
    produce_hg(1)

    wq_q2 = [P.sb(f"wq_q{i}", [128, ND, 256], BF16) for i in range(2)]
    qctr = [0]
    for j in range(ND):
        P.dma("g", wq_q2[0][:, j, :], wqT[128 * j:128 * (j + 1), 0:256],
              writes=[(wq_q2[0].name, j)])
    for qq in range(8):
        wq_q = wq_q2[qq % 2]
        if qq + 1 < 8:
            wn = wq_q2[(qq + 1) % 2]
            for j in range(ND):
                P.dma("g", wn[:, j, :],
                      wqT[128 * j:128 * (j + 1), 256 * (qq + 1):256 * (qq + 2)],
                      writes=[(wn.name, j)])
        for hh in range(2):
            h = 2 * qq + hh
            for t2 in range(2):
                pk = pk_rot[qctr[0] % 3]
                qctr[0] += 1
                for j in range(ND):
                    P.op("t", lambda e, h=h, hh=hh, j=j, t2=t2, pk=pk, wq_q=wq_q: e.matmul(
                        pk[:, 0:512],
                        wq_q[:, j, 128 * hh:128 * (hh + 1)],
                        ln1T[:, j, 512 * t2:512 * (t2 + 1)],
                        start=(j == 0), stop=(j == ND - 1)),
                        reads=[(wq_q.name, j), ("ln1T", (j, 0))],
                        writes=[(pk.name, 0)], dbg=f"q{h}")
                emit_elu(pk[:, 0:512], [(pk.name, 0)],
                         sigq[:, h, 512 * t2:512 * (t2 + 1)], [("sigq", h)], 512)
            for j in range(ND):
                P.op("t", lambda e, h=h, hh=hh, j=j, wq_q=wq_q: e.matmul(
                    psc[0:128, h:h + 1], wq_q[:, j, 128 * hh:128 * (hh + 1)],
                    xbar[:, j:j + 1], start=(j == 0), stop=(j == ND - 1)),
                    reads=[(wq_q.name, j), ("xbar", j)],
                    writes=[("psc", "qb")])
    P.op("s", lambda e: e.copy(qbar_sb[:], psc[:, 0:16]),
         reads=[("psc", "qb")], writes=[("qbar_sb", 0)])
    P.free(wq_q2[1]); P.free(wq_q2[0])

    for h in range(H):
        P.op("t", lambda e, h=h: e.matmul(
            pnum[0:1, 4 * h:4 * h + 4], qbar_sb[:, h:h + 1],
            mn_sb[:, 4 * h:4 * h + 4], start=True, stop=True),
            reads=[("qbar_sb", 0), ("mn_sb", 0)], writes=[("pnum", "rel")])
    P.op("t", lambda e: e.matmul(pnum[0:1, 64:128], ones_col[:, 0:1], mn_sb[:],
                                 start=True, stop=True),
         reads=[("ones_col", 0), ("mn_sb", 0)], writes=[("pnum", "rel")])

    P.op("v", lambda e: e.tensor_reduce(
        sc1[0:1, 112:116], pnum[0:1, 64:128].rearrange("p (m h) -> p m h", h=16),
        axis=AX.X, op=ALU.add),
        reads=[("pnum", "rel")], writes=[("sc1", "sm")])
    P.op("v", lambda e: e.tensor_scalar(
        sc1[0:1, 121:125], sc1[0:1, 112:116], 1e-6, None, ALU.is_lt),
        reads=[("sc1", "sm")], writes=[("sc1", "empty")])
    P.op("v", lambda e: e.scalar_tensor_tensor(
        sc1[0:1, 0:64].rearrange("p (h m) -> p h m", m=4),
        sc1[0:1, 121:125].rearrange("p m -> p () m").to_broadcast([1, 16, 4]),
        -1e30, pnum[0:1, 0:64].rearrange("p (h m) -> p h m", m=4),
        ALU.mult, ALU.add),
        reads=[("sc1", "empty"), ("pnum", "rel")], writes=[("sc1", "rel")])
    P.op("v", lambda e: e.tensor_reduce(
        sc1[0:1, 64:80], sc1[0:1, 0:64].rearrange("p (h m) -> p h m", m=4),
        axis=AX.X, op=ALU.max),
        reads=[("sc1", "rel")], writes=[("sc1", "hmax")])
    P.op("v", lambda e: e.tensor_tensor(
        sc1[0:1, 0:64].rearrange("p (h m) -> p h m", m=4),
        sc1[0:1, 0:64].rearrange("p (h m) -> p h m", m=4),
        sc1[0:1, 64:80].rearrange("p h -> p h ()").to_broadcast([1, 16, 4]),
        ALU.subtract),
        reads=[("sc1", "rel"), ("sc1", "hmax")], writes=[("sc1", "rel")])
    P.op("s", lambda e: e.activation(sc1[0:1, 0:64], sc1[0:1, 0:64], AF.Exp),
         reads=[("sc1", "rel")], writes=[("sc1", "rel")])
    P.op("v", lambda e: e.tensor_reduce(
        sc1[0:1, 80:96], sc1[0:1, 0:64].rearrange("p (h m) -> p h m", m=4),
        axis=AX.X, op=ALU.add),
        reads=[("sc1", "rel")], writes=[("sc1", "hsum")])
    P.op("v", lambda e: e.reciprocal(sc1[0:1, 80:96], sc1[0:1, 80:96]),
         reads=[("sc1", "hsum")], writes=[("sc1", "hsum")])
    P.op("v", lambda e: e.tensor_tensor(
        sc1[0:1, 0:64].rearrange("p (h m) -> p h m", m=4),
        sc1[0:1, 0:64].rearrange("p (h m) -> p h m", m=4),
        sc1[0:1, 80:96].rearrange("p h -> p h ()").to_broadcast([1, 16, 4]),
        ALU.mult),
        reads=[("sc1", "rel"), ("sc1", "hsum")], writes=[("sc1", "rel")])
    P.dma("y", sc1[0:1, 96:112], gate1, writes=[("sc1", "g")])
    P.op("s", lambda e: e.activation(sc1[0:1, 96:112], sc1[0:1, 96:112], AF.Sigmoid),
         reads=[("sc1", "g")], writes=[("sc1", "g")])
    P.op("v", lambda e: e.tensor_tensor(
        sc1[0:1, 120:121], sc1[0:1, 121:122], sc1[0:1, 122:123], ALU.mult),
        reads=[("sc1", "empty")], writes=[("sc1", "nae")])
    P.op("v", lambda e: e.tensor_tensor(
        sc1[0:1, 120:121], sc1[0:1, 120:121], sc1[0:1, 123:124], ALU.mult),
        reads=[("sc1", "empty"), ("sc1", "nae")], writes=[("sc1", "nae")])
    P.op("v", lambda e: e.tensor_tensor(
        sc1[0:1, 120:121], sc1[0:1, 120:121], sc1[0:1, 124:125], ALU.mult),
        reads=[("sc1", "empty"), ("sc1", "nae")], writes=[("sc1", "nae")])
    P.op("v", lambda e: e.tensor_scalar(
        sc1[0:1, 120:121], sc1[0:1, 120:121], -1.0, 1.0, ALU.mult, ALU.add),
        reads=[("sc1", "nae")], writes=[("sc1", "nae")])
    P.op("v", lambda e: e.tensor_scalar(
        sc1[0:1, 116:120], sc1[0:1, 121:125], -1.0, 1.0, ALU.mult, ALU.add),
        reads=[("sc1", "empty")], writes=[("sc1", "nem")])
    P.op("v", lambda e: e.tensor_tensor(
        sc1[0:1, 128:192].rearrange("p (h m) -> p h m", m=4),
        sc1[0:1, 0:64].rearrange("p (h m) -> p h m", m=4),
        sc1[0:1, 96:112].rearrange("p h -> p h ()").to_broadcast([1, 16, 4]),
        ALU.mult),
        reads=[("sc1", "rel"), ("sc1", "g")], writes=[("sc1", "wgg")])
    P.op("v", lambda e: e.tensor_tensor(
        sc1[0:1, 128:192].rearrange("p (h m) -> p h m", m=4),
        sc1[0:1, 128:192].rearrange("p (h m) -> p h m", m=4),
        sc1[0:1, 116:120].rearrange("p m -> p () m").to_broadcast([1, 16, 4]),
        ALU.mult),
        reads=[("sc1", "wgg"), ("sc1", "nem")], writes=[("sc1", "wgg")])
    P.op("v", lambda e: e.tensor_tensor(
        sc1[0:1, 128:192], sc1[0:1, 128:192],
        sc1[0:1, 120:121].to_broadcast([1, 64]), ALU.mult),
        reads=[("sc1", "wgg"), ("sc1", "nae")], writes=[("sc1", "wgg")])
    P.op("v", lambda e: e.tensor_scalar(
        sc1[0:1, 192:208], sc1[0:1, 96:112], -1.0, 1.0, ALU.mult, ALU.add),
        reads=[("sc1", "g")], writes=[("sc1", "gloc")])
    P.op("v", lambda e: e.tensor_copy(sc1b[0:1, 0:64], sc1[0:1, 128:192]),
         reads=[("sc1", "wgg")], writes=[("sc1b", 0)])
    P.op("v", lambda e: e.tensor_copy(sc1b[0:1, 64:80], sc1[0:1, 192:208]),
         reads=[("sc1", "gloc")], writes=[("sc1b", 0)])
    P.op("t", lambda e: e.matmul(psc[:, 0:80], ones_row[0:1, :], sc1b[0:1, :],
                                 start=True, stop=True),
         reads=[("ones_row", 0), ("sc1b", 0)], writes=[("psc", "qb")])
    P.op("s", lambda e: e.copy(csb[:], psc[:, 0:80]),
         reads=[("psc", "qb")], writes=[("csb", 0)])

    prep_hg(0)
    local_hg(0)
    for hg in range(2, NHG):
        produce_hg(hg, mid=lambda hgm=hg - 1: prep_hg(hgm))
        local_hg(hg - 1)
    prep_hg(NHG - 1)
    local_hg(NHG - 1)

    P.free(pkt2[1]); P.free(pkt2[0])
    P.free(pnum); P.free(psc); P.free(pkv)
    P.free(pk3); P.free(pk2); P.free(pk1)
    P.free(den_sb); P.free(scm)
    for b in reversed(sbase_g + recv_g + exch_g + spall_g + v_g + kpT_g):
        P.free(b)
    P.free(kp_hg)
    P.free(wv_d[0]); P.free(wk_d[0])
    P.free(elu_e2[1]); P.free(elu_e2[0]); P.free(elu_t2[1]); P.free(elu_t2[0])
    P.free(ln1T)

    mem_sb = P.sb("mem_sb", [128, M * H * 128], BF16)
    P.dma("g", mem_sb[:], memT, writes=[("mem_sb", 0)])
    pnall = P.ps("pnall", [128, 512])
    nmm2 = [0]
    for tc in range(NC_OWN):
        for h in range(H):
            nmm2[0] += 1
            P.op("t", lambda e, tc=tc, h=h,
                 f=dict(start=(nmm2[0] == 1), stop=(nmm2[0] == 128)): e.matmul(
                pnall[:, 64 * tc + 4 * h:64 * tc + 4 * (h + 1)],
                sigq[:, h, 128 * tc:128 * (tc + 1)],
                mn_sb[:, 4 * h:4 * (h + 1)], **f),
                reads=[("sigq", h), ("mn_sb", 0)],
                writes=[("pnall", 0)], dbg="nall")
    csn_all = P.sb("csn_all", [128, 512], F32)
    P.op("v", lambda e: e.tensor_scalar_max(csn_all[:], pnall[:, 0:512], ATT_EPS),
         reads=[("pnall", 0)], writes=[("csn_all", 0)])
    P.op("v", lambda e: e.reciprocal(csn_all[:], csn_all[:]),
         reads=[("csn_all", 0)], writes=[("csn_all", 0)])
    P.op("v", lambda e: e.tensor_tensor(
        csn_all[:].rearrange("p (tc q) -> p tc q", q=64),
        csn_all[:].rearrange("p (tc q) -> p tc q", q=64),
        csb[:, 0:64].rearrange("p q -> p () q").to_broadcast([128, 8, 64]),
        ALU.mult),
        reads=[("csn_all", 0), ("csb", 0)], writes=[("csn_all", 0)])
    P.free(pnall)

    dg = P.sb("dg", [128, 3, 8, 128], BF16)
    am_sb = P.sb("am_sb", [128, 2, 2, 4, 128], BF16)
    pam2 = [P.ps(f"pamb{i}", [128, 1024]) for i in range(2)]
    pmon2 = [P.ps(f"pmon{i}", [128, 264]) for i in range(2)]
    pat2 = [P.ps(f"pat{i}", [128, 128], BF16) for i in range(2)]
    pairs = [(tc, pp) for tc in range(NC_OWN) for pp in range(8)]

    def dg_emit(i):
        tc, pp = pairs[i]
        P.op("g", lambda e, tc=tc, pp=pp, st=i % 3: e.affine_select(
            out=dg[:, st, :, :],
            in_=csn_all[:, 64 * tc + 8 * pp:64 * tc + 8 * (pp + 1)]
                .rearrange("p q -> p q ()").to_broadcast([128, 8, 128]),
            compare_op=ALU.is_equal, fill=0.0,
            base=0, pattern=[[0, 8], [1, 128]], channel_multiplier=-1),
            reads=[("csn_all", 0)], writes=[("dg", i % 3)], dbg=f"dg{i}")

    def mem_front(i):
        tc, pp = pairs[i]
        st_i = i % 2
        pa = pam2[st_i]
        for hh in range(2):
            h = 2 * pp + hh
            for m in range(M):
                P.op("t", lambda e, tc=tc, h=h, hh=hh, m=m, pa=pa: e.matmul(
                    pa[:, 512 * hh + 128 * m:512 * hh + 128 * (m + 1)],
                    sigq[:, h, 128 * tc:128 * (tc + 1)],
                    mem_sb[:, 128 * (m * H + h):128 * (m * H + h + 1)],
                    start=(m == 0), stop=(m == M - 1)),
                    reads=[("sigq", h), ("mem_sb", 0)],
                    writes=[(pa.name, 0)], dbg=f"am{h}m{m}")
        P.op("s", lambda e, st_i=st_i, pa=pa: e.copy(
            am_sb[:, st_i, :, :, :], pa[:, 0:1024]),
            reads=[(pa.name, 0)], writes=[("am_sb", st_i)])

    def mem_back1(i):
        tc, pp = pairs[i]
        st_i = i % 2
        dst = i % 3
        pmon = pmon2[st_i]
        for hh in range(2):
            h = 2 * pp + hh
            for m in range(M):
                P.op("t", lambda e, st_i=st_i, dst=dst, hh=hh, m=m, pmon=pmon: e.matmul(
                    pmon[:, 128 * hh:128 * (hh + 1)],
                    dg[:, dst, 4 * hh + m, :],
                    am_sb[:, st_i, hh, m, :],
                    start=(m == 0), stop=(m == M - 1)),
                    reads=[("dg", dst), ("am_sb", st_i)],
                    writes=[(pmon.name, 0)], dbg=f"dmm{h}")
        P.op("v", lambda e, tc=tc, pp=pp, pmon=pmon: e.tensor_tensor(
            attn_fin[:, tc, 256 * pp:256 * (pp + 1)],
            pmon[:, 0:256],
            attn_fin[:, tc, 256 * pp:256 * (pp + 1)], ALU.add),
            reads=[(pmon.name, 0), ("attn_fin", (tc, 2 * pp)),
                   ("attn_fin", (tc, 2 * pp + 1))],
            writes=[("attn_fin", (tc, 2 * pp)), ("attn_fin", (tc, 2 * pp + 1))],
            dbg=f"mfin{pp}")

    def mem_back2(i):
        tc, pp = pairs[i]
        for hh in range(2):
            h = 2 * pp + hh
            pat = pat2[hh]
            P.op("t", lambda e, tc=tc, h=h, pat=pat: e.transpose(
                pat[:, 0:128], attn_fin[:, tc, 128 * h:128 * (h + 1)],
                ident_bf[:]),
                reads=[("attn_fin", (tc, h)), ("ident_bf", 0)],
                writes=[(pat.name, 0)], dbg=f"aT{h}")
            P.op("s", lambda e, tc=tc, h=h, pat=pat: e.copy(
                aT[:, h, 128 * tc:128 * (tc + 1)], pat[:, 0:128]),
                reads=[(pat.name, 0)], writes=[("aT", (h, tc))], dbg=f"aTc{h}")

    NP_ = len(pairs)
    dg_emit(0)
    dg_emit(1)
    mem_front(0)
    mem_front(1)
    mem_back1(0)
    for i in range(2, NP_):
        mem_front(i)
        dg_emit(i)
        mem_back1(i - 1)
        mem_back2(i - 2)
    mem_back1(NP_ - 1)
    mem_back2(NP_ - 2)
    mem_back2(NP_ - 1)

    P.free(pat2[1]); P.free(pat2[0])
    P.free(pmon2[1]); P.free(pmon2[0])
    P.free(pam2[1]); P.free(pam2[0])
    P.free(am_sb); P.free(dg)
    P.free(csn_all)
    P.free(attn_fin); P.free(mem_sb)
    P.free(sigq)

    x1 = P.sb("x1", [128, NC_OWN, D], F32)
    wo_b = [P.sb(f"wo_b{i}", [128, ND, 1024], BF16) for i in range(2)]
    xre = [P.sb(f"xre{i}", [128, 2048], F32) for i in range(2)]
    ln2buf = [P.sb(f"ln2buf{i}", [128, D], BF16) for i in range(2)]
    pw1 = P.ps("pw1", [128, 2048])
    pw2 = P.ps("pw2", [128, 2048])
    pw_rot = [pw1, pw2]
    for half in range(2):
        wob = wo_b[half]
        for j in range(ND):
            P.dma("g", wob[:, j, :],
                  woT[128 * j:128 * (j + 1), 1024 * half:1024 * (half + 1)],
                  writes=[(wob.name, j)])
    for tc in range(NC_OWN):
        pw = pw_rot[tc % 2]
        xr_b = xre[tc % 2]
        P.dma("y", xr_b[:], xo[128 * tc:128 * (tc + 1), :],
              writes=[(xr_b.name, 0)])
        for half in range(2):
            wob = wo_b[half]
            for j in range(ND):
                for n in range(2):
                    P.op("t", lambda e, tc=tc, j=j, n=n, half=half, pw=pw, wob=wob: e.matmul(
                        pw[:, 1024 * half + 512 * n:1024 * half + 512 * (n + 1)],
                        aT[:, j, 128 * tc:128 * (tc + 1)],
                        wob[:, j, 512 * n:512 * (n + 1)],
                        start=(j == 0), stop=(j == ND - 1)),
                        reads=[("aT", (j, tc)), (wob.name, j)],
                        writes=[(pw.name, 2 * half + n)], dbg=f"wo{tc}")
        P.op("v", lambda e, tc=tc, pw=pw, xr_b=xr_b: e.tensor_tensor(
            x1[:, tc, :], pw[:], xr_b[:], ALU.add),
            reads=[(pw.name, i) for i in range(4)] + [(xr_b.name, 0)],
            writes=[("x1", tc)], dbg=f"res1_{tc}")
        lb = ln2buf[tc % 2]
        for j in range(4):
            P.op("v", lambda e, j=j, tc=tc: e.bn_stats(
                stat[:, 6 * j:6 * j + 6], x1[:, tc, 512 * j:512 * (j + 1)]),
                reads=[("x1", tc)], writes=[("stat", j)])
        P.op("v", lambda e: e.bn_aggr(mv[:], stat[:]),
             reads=[("stat", j) for j in range(4)], writes=[("mv", 0)])
        P.op("s", lambda e: e.activation(rstd[:], mv[:, 1:2], AF.Sqrt, bias=eps_sb[:]),
             reads=[("mv", 0), ("eps_sb", 0)], writes=[("rstd", 0)])
        P.op("v", lambda e: e.reciprocal(rstd[:], rstd[:]),
             reads=[("rstd", 0)], writes=[("rstd", 0)])
        P.op("v", lambda e: e.tensor_tensor(nmean[:], mv[:, 0:1], rstd[:], ALU.mult),
             reads=[("mv", 0), ("rstd", 0)], writes=[("nmean", 0)])
        P.op("v", lambda e: e.tensor_scalar_mul(nmean[:], nmean[:], -1.0),
             reads=[("nmean", 0)], writes=[("nmean", 0)])
        P.op("s", lambda e, tc=tc, lb=lb: e.activation(
            lb[:], x1[:, tc, :], AF.Identity, bias=nmean[:], scale=rstd[:]),
            reads=[("x1", tc), ("rstd", 0), ("nmean", 0)], writes=[(lb.name, 0)])
        P.dma("s", ln2_dram[128 * tc:128 * (tc + 1), :], lb[:],
              reads=[(lb.name, 0)], writes=[("ln2_dram", tc)])
        P.dma("y", x1_dram[128 * tc:128 * (tc + 1), :], x1[:, tc, :],
              reads=[("x1", tc)], writes=[("x1_dram", tc)])
    P.free(pw2); P.free(pw1)
    P.free(ln2buf[1]); P.free(ln2buf[0])
    P.free(xre[1]); P.free(xre[0]); P.free(wo_b[1]); P.free(wo_b[0])
    P.free(aT)
    P.free(x1)

    sh_basis = P.sb("sh_basis", [128, 16, T_OWN], BF16)
    uT = P.sb("uT", [128, 8, T_OWN], BF16, is_transpose_dest=True)
    c_sb = P.sb("c_sb", [128, NC_OWN, 8], F32)
    ln2T = P.sb("ln2T", [128, ND, T_OWN], BF16, is_transpose_dest=True)
    l2_all = [("ln2_dram", tc) for tc in range(NC_OWN)]
    for j in range(ND):
        P.dma("y", ln2T[:, j, :],
              ln2_dram[:, 128 * j:128 * (j + 1)],
              transpose=True, reads=l2_all, writes=[("ln2T", j)])

    wr_sb = P.sb("wr_sb", [128, ND, E_EXP], BF16)
    for j in range(ND):
        P.dma("g", wr_sb[:, j, :], wrt[128 * j:128 * (j + 1), :],
              writes=[("wr_sb", j)])
    amps_sb = P.sb("amps_sb", [128, 16], F32)
    esc_sb = P.sb("esc_sb", [128, 1], F32)
    mix_sb = P.sb("mix_sb", [128, 8], BF16)
    mixf = P.sb("mixf", [128, 9], F32)
    P.dma("y", amps_sb[:], amps, writes=[("amps_sb", 0)])
    P.dma("y", esc_sb[:], escale, writes=[("esc_sb", 0)])
    P.op("s", lambda e: e.activation(amps_sb[:], amps_sb[:], AF.Square),
         reads=[("amps_sb", 0)], writes=[("amps_sb", 0)])
    P.op("v", lambda e: e.tensor_tensor(
        mixf[:, 0:8],
        amps_sb[:].rearrange("p (k two) -> p k two", two=2)[:, :, 0],
        amps_sb[:].rearrange("p (k two) -> p k two", two=2)[:, :, 1],
        ALU.add),
        reads=[("amps_sb", 0)], writes=[("mixf", 0)])
    P.op("v", lambda e: e.reduce_sum(mixf[:, 8:9], mixf[:, 0:8], axis=AX.X),
         reads=[("mixf", 0)], writes=[("mixf", 1)])
    P.op("v", lambda e: e.tensor_scalar_add(mixf[:, 8:9], mixf[:, 8:9], FFN_EPS),
         reads=[("mixf", 1)], writes=[("mixf", 1)])
    P.op("v", lambda e: e.reciprocal(mixf[:, 8:9], mixf[:, 8:9]),
         reads=[("mixf", 1)], writes=[("mixf", 1)])
    P.op("v", lambda e: e.tensor_tensor(
        mixf[:, 8:9], mixf[:, 8:9], esc_sb[:], ALU.mult),
        reads=[("mixf", 1), ("esc_sb", 0)], writes=[("mixf", 1)])
    P.op("v", lambda e: e.tensor_scalar(
        mix_sb[:], mixf[:, 0:8], mixf[:, 8:9], None, ALU.mult),
        reads=[("mixf", 0), ("mixf", 1)], writes=[("mix_sb", 0)])

    probs2 = [P.sb(f"probs{i}", [128, 128], F32) for i in range(2)]
    rt12 = [P.sb(f"rt1_{i}", [128, 8], F32) for i in range(2)]
    rtv2 = [P.sb(f"rtv{i}", [128, 3], F32) for i in range(2)]
    wmask2 = [P.sb(f"wmask{i}", [128, 128], BF16) for i in range(2)]
    prt2 = [P.ps(f"prt{i}", [128, 128]) for i in range(3)]
    pc2 = [P.ps(f"pc{i}", [128, 8]) for i in range(2)]
    pwt2 = [P.ps(f"pwt{i}", [128, 128], BF16) for i in range(2)]

    def rt_front(tc):
        prt = prt2[tc % 3]
        probs = probs2[tc % 2]
        rt1 = rt12[tc % 2]
        rtv = rtv2[tc % 2]
        wmask = wmask2[tc % 2]
        for j in range(ND):
            P.op("t", lambda e, tc=tc, j=j, prt=prt: e.matmul(
                prt[:, 0:128], ln2T[:, j, 128 * tc:128 * (tc + 1)], wr_sb[:, j, :],
                start=(j == 0), stop=(j == ND - 1)),
                reads=[("ln2T", j), ("wr_sb", j)], writes=[(prt.name, 0)],
                dbg=f"rt{tc}")
        P.op("v", lambda e, prt=prt, rtv=rtv: e.reduce_max(
            rtv[:, 0:1], prt[:, 0:128], axis=AX.X, negate=True),
            reads=[(prt.name, 0)], writes=[(rtv.name, 0)])
        P.op("s", lambda e, prt=prt, probs=probs, rtv=rtv: e.activation(
            probs[:], prt[:, 0:128], AF.Exp, bias=rtv[:, 0:1], accum_out=rtv[:, 1:2]),
            reads=[(prt.name, 0), (rtv.name, 0)],
            writes=[(probs.name, 0), (rtv.name, 1)])
        P.op("v", lambda e, probs=probs, rt1=rt1: e.max(out=rt1[:], in_=probs[:]),
             reads=[(probs.name, 0)], writes=[(rt1.name, 0)])
        P.op("v", lambda e, probs=probs, rt1=rt1, wmask=wmask: e.scalar_tensor_tensor(
            wmask[:], probs[:], rt1[:, 3:4], probs[:], ALU.is_ge, ALU.mult),
            reads=[(probs.name, 0), (rt1.name, 0)], writes=[(wmask.name, 0)])
        P.op("v", lambda e, wmask=wmask, rtv=rtv: e.reduce_sum(
            rtv[:, 2:3], wmask[:], axis=AX.X),
            reads=[(wmask.name, 0)], writes=[(rtv.name, 2)])
        P.op("v", lambda e, rtv=rtv: e.scalar_tensor_tensor(
            rtv[:, 2:3], rtv[:, 1:2], FFN_EPS, rtv[:, 2:3], ALU.mult, ALU.add),
            reads=[(rtv.name, 1), (rtv.name, 2)], writes=[(rtv.name, 2)])
        P.op("v", lambda e, rtv=rtv: e.reciprocal(rtv[:, 2:3], rtv[:, 2:3]),
             reads=[(rtv.name, 2)], writes=[(rtv.name, 2)])

    def rt_back(tc):
        rtv = rtv2[tc % 2]
        wmask = wmask2[tc % 2]
        pwt = pwt2[tc % 2]
        pc = pc2[tc % 2]
        P.op("t", lambda e, pwt=pwt, wmask=wmask: e.transpose(
            pwt[:, 0:128], wmask[:], ident_bf[:]),
            reads=[(wmask.name, 0), ("ident_bf", 0)], writes=[(pwt.name, 0)])
        P.op("s", lambda e, pwt=pwt, wmask=wmask: e.copy(wmask[:], pwt[:, 0:128]),
             reads=[(pwt.name, 0)], writes=[(wmask.name, 0)])
        P.op("t", lambda e, pc=pc, wmask=wmask: e.matmul(
            pc[:, 0:8], wmask[:], mix_sb[:], start=True, stop=True),
            reads=[(wmask.name, 0), ("mix_sb", 0)], writes=[(pc.name, 0)])
        P.op("v", lambda e, tc=tc, pc=pc, rtv=rtv: e.tensor_scalar(
            c_sb[:, tc, :], pc[:, 0:8], rtv[:, 2:3], None, ALU.mult),
            reads=[(pc.name, 0), (rtv.name, 2)], writes=[("c_sb", tc)])

    rt_front(0)
    for tc in range(1, NC_OWN):
        rt_front(tc)
        rt_back(tc - 1)
    rt_back(NC_OWN - 1)
    P.free(pwt2[1]); P.free(pwt2[0]); P.free(pc2[1]); P.free(pc2[0])
    P.free(prt2[2]); P.free(prt2[1]); P.free(prt2[0])
    P.free(wmask2[1]); P.free(wmask2[0]); P.free(rtv2[1]); P.free(rtv2[0])
    P.free(rt12[1]); P.free(rt12[0]); P.free(probs2[1]); P.free(probs2[0])
    P.free(mixf); P.free(mix_sb); P.free(esc_sb); P.free(amps_sb)
    P.free(wr_sb)

    u_acc = P.sb("u_acc", [128, NC_OWN, I_DIM], F32)
    wgq = [P.sb(f"wgq{i}", [128, ND, 512], BF16) for i in range(2)]
    wuq = [P.sb(f"wuq{i}", [128, ND, 512], BF16) for i in range(2)]
    silu_g = P.sb("silu_g", [128, NC_OWN, 512], BF16)
    ut2 = [P.sb(f"ut{i}", [128, 512], F32) for i in range(2)]
    pg_rot = [P.ps(f"pg{i}", [128, 512]) for i in range(3)]
    pu_rot = [P.ps(f"pu{i}", [128, 512]) for i in range(3)]

    def ffn_loads(q):
        wgb = wgq[q % 2]
        wub = wuq[q % 2]
        for j in range(ND):
            P.dma("g", wgb[:, j, :],
                  wg[128 * j:128 * (j + 1), 512 * q:512 * (q + 1)],
                  writes=[(wgb.name, j)])
            P.dma("g", wub[:, j, :],
                  wu[128 * j:128 * (j + 1), 512 * q:512 * (q + 1)],
                  writes=[(wub.name, j)])

    ffn_loads(0)
    ffn_loads(1)
    for q in range(16):
        k_idx = q // 2
        ihalf = q % 2
        wgb = wgq[q % 2]
        wub = wuq[q % 2]
        for tc in range(NC_OWN):
            pg = pg_rot[tc % 3]
            pu = pu_rot[tc % 3]
            ut = ut2[tc % 2]
            for j in range(ND):
                P.op("t", lambda e, tc=tc, j=j, pg=pg, wgb=wgb: e.matmul(
                    pg[:], ln2T[:, j, 128 * tc:128 * (tc + 1)], wgb[:, j, :],
                    start=(j == 0), stop=(j == ND - 1)),
                    reads=[("ln2T", j), (wgb.name, j)],
                    writes=[(pg.name, 0)], dbg=f"g{q}t{tc}")
            P.op("s", lambda e, tc=tc, pg=pg: e.activation(
                silu_g[:, tc, :], pg[:], AF.Silu),
                reads=[(pg.name, 0)], writes=[("silu_g", tc)])
            for j in range(ND):
                P.op("t", lambda e, tc=tc, j=j, pu=pu, wub=wub: e.matmul(
                    pu[:], ln2T[:, j, 128 * tc:128 * (tc + 1)], wub[:, j, :],
                    start=(j == 0), stop=(j == ND - 1)),
                    reads=[("ln2T", j), (wub.name, j)],
                    writes=[(pu.name, 0)], dbg=f"u{q}t{tc}")
            P.op("v", lambda e, tc=tc, k_idx=k_idx, pu=pu, ut=ut: e.scalar_tensor_tensor(
                ut[:], pu[:], c_sb[:, tc, k_idx:k_idx + 1], silu_g[:, tc, :],
                ALU.mult, ALU.mult),
                reads=[(pu.name, 0), ("c_sb", tc), ("silu_g", tc)],
                writes=[(ut.name, 0)], dbg=f"ut{q}t{tc}")
            if k_idx == 0:
                P.op("v", lambda e, tc=tc, ihalf=ihalf, ut=ut: e.tensor_copy(
                    u_acc[:, tc, 512 * ihalf:512 * (ihalf + 1)], ut[:]),
                    reads=[(ut.name, 0)], writes=[("u_acc", (tc, ihalf))])
            else:
                P.op("v", lambda e, tc=tc, ihalf=ihalf, ut=ut: e.tensor_tensor(
                    u_acc[:, tc, 512 * ihalf:512 * (ihalf + 1)],
                    u_acc[:, tc, 512 * ihalf:512 * (ihalf + 1)], ut[:], ALU.add),
                    reads=[(ut.name, 0), ("u_acc", (tc, ihalf))],
                    writes=[("u_acc", (tc, ihalf))])
        if q + 2 < 16:
            ffn_loads(q + 2)
    for tc in range(NC_OWN):
        P.dma("g", u_dram[128 * tc:128 * (tc + 1), :], u_acc[:, tc, :],
              reads=[("u_acc", (tc, 0)), ("u_acc", (tc, 1))],
              writes=[("u_dram", tc)])
    for p in reversed(pu_rot + pg_rot):
        P.free(p)
    P.free(ut2[1]); P.free(ut2[0]); P.free(silu_g)
    P.free(wuq[1]); P.free(wuq[0]); P.free(wgq[1]); P.free(wgq[0])
    P.free(u_acc)

    u_all = [("u_dram", tc) for tc in range(NC_OWN)]
    for j in range(8):
        P.dma("y", uT[:, j, :],
              u_dram[:, 128 * j:128 * (j + 1)],
              transpose=True, reads=u_all, writes=[("uT", j)])

    swgq = [P.sb(f"swgq{i}", [128, ND, 512], BF16) for i in range(2)]
    swuq = [P.sb(f"swuq{i}", [128, ND, 512], BF16) for i in range(2)]
    ssilu = P.sb("ssilu", [128, T_OWN], BF16)
    psh = [P.ps(f"psh{i}", [128, 1024]) for i in range(4)]

    def sh_loads(q):
        sgb = swgq[q % 2]
        sub = swuq[q % 2]
        for j in range(ND):
            P.dma("g", sgb[:, j, :],
                  swg[128 * j:128 * (j + 1), 512 * q:512 * (q + 1)],
                  writes=[(sgb.name, j)])
            P.dma("g", sub[:, j, :],
                  swu[128 * j:128 * (j + 1), 512 * q:512 * (q + 1)],
                  writes=[(sub.name, j)])

    sh_loads(0)
    sh_loads(1)
    for q in range(4):
        sgb = swgq[q % 2]
        sub = swuq[q % 2]
        for fc in range(4):
            of = 4 * q + fc
            pga = psh[2 * (fc % 2)]
            pua = psh[2 * (fc % 2) + 1]
            for j in range(ND):
                for n in range(2):
                    P.op("t", lambda e, j=j, n=n, fc=fc, pga=pga, sgb=sgb: e.matmul(
                        pga[:, 512 * n:512 * (n + 1)],
                        sgb[:, j, 128 * fc:128 * (fc + 1)],
                        ln2T[:, j, 512 * n:512 * (n + 1)],
                        start=(j == 0), stop=(j == ND - 1)),
                        reads=[(sgb.name, j), ("ln2T", j)],
                        writes=[(pga.name, n)], dbg=f"sg{of}")
            P.op("s", lambda e, pga=pga: e.activation(ssilu[:], pga[:], AF.Silu),
                 reads=[(pga.name, 0), (pga.name, 1)], writes=[("ssilu", 0)])
            for j in range(ND):
                for n in range(2):
                    P.op("t", lambda e, j=j, n=n, fc=fc, pua=pua, sub=sub: e.matmul(
                        pua[:, 512 * n:512 * (n + 1)],
                        sub[:, j, 128 * fc:128 * (fc + 1)],
                        ln2T[:, j, 512 * n:512 * (n + 1)],
                        start=(j == 0), stop=(j == ND - 1)),
                        reads=[(sub.name, j), ("ln2T", j)],
                        writes=[(pua.name, n)], dbg=f"su{of}")
            P.op("v", lambda e, of=of, pua=pua: e.tensor_tensor(
                sh_basis[:, of, :], pua[:], ssilu[:], ALU.mult),
                reads=[(pua.name, 0), (pua.name, 1), ("ssilu", 0)],
                writes=[("sh_basis", of)], dbg=f"sb{of}")
        if q + 2 < 4:
            sh_loads(q + 2)
    for p in reversed(psh):
        P.free(p)
    P.free(ssilu)
    P.free(swuq[1]); P.free(swuq[0]); P.free(swgq[1]); P.free(swgq[0])
    P.free(ln2T)

    wd_sb = P.sb("wd_sb", [128, 8, D], BF16)
    for j in range(8):
        P.dma("g", wd_sb[:, j, :], wd[128 * j:128 * (j + 1), :],
              writes=[("wd_sb", j)])
    swd_sb = P.sb("swd_sb", [128, 16, D], BF16)
    for j in range(16):
        P.dma("g", swd_sb[:, j, :], swd[128 * j:128 * (j + 1), :],
              writes=[("swd_sb", j)])
    ybuf = [P.sb(f"ybuf{i}", [128, D], F32) for i in range(2)]
    x1b = [P.sb(f"x1b{i}", [128, D], F32) for i in range(2)]
    pf = [P.ps(f"pf{i}", [128, 2048]) for i in range(2)]

    for tc in range(NC_OWN):
        pfb = pf[tc % 2]
        yb = ybuf[tc % 2]
        xb = x1b[tc % 2]
        P.dma("y", xb[:], x1_dram[128 * tc:128 * (tc + 1), :],
              reads=[("x1_dram", tc)], writes=[(xb.name, 0)])
        for j in range(8):
            for n in range(4):
                P.op("t", lambda e, tc=tc, j=j, n=n, pfb=pfb: e.matmul(
                    pfb[:, 512 * n:512 * (n + 1)],
                    uT[:, j, 128 * tc:128 * (tc + 1)],
                    wd_sb[:, j, 512 * n:512 * (n + 1)],
                    start=(j == 0), stop=False),
                    reads=[("uT", j), ("wd_sb", j)],
                    writes=[(pfb.name, n)], dbg=f"dn{tc}")
        for j in range(16):
            for n in range(4):
                P.op("t", lambda e, tc=tc, j=j, n=n, pfb=pfb: e.matmul(
                    pfb[:, 512 * n:512 * (n + 1)],
                    sh_basis[:, j, 128 * tc:128 * (tc + 1)],
                    swd_sb[:, j, 512 * n:512 * (n + 1)],
                    start=False, stop=(j == 15)),
                    reads=[("sh_basis", j), ("swd_sb", j)],
                    writes=[(pfb.name, n)], dbg=f"sd{tc}")
        P.op("v", lambda e, pfb=pfb, yb=yb, xb=xb: e.tensor_tensor(
            yb[:], pfb[:], xb[:], ALU.add),
            reads=[(pfb.name, i) for i in range(4)] + [(xb.name, 0)],
            writes=[(yb.name, 0)], dbg=f"res2_{tc}")
        P.dma("y", yout[128 * tc:128 * (tc + 1), :], yb[:],
              reads=[(yb.name, 0)], writes=[("y", tc)])

    P.emit(final_outputs=[("y", tc) for tc in range(NC_OWN)])
    return nc


_NC_CACHE = [None]


def kernel(L0_attn_hidden_states, L0_attn_memories, L0_attn_memory_norms,
           L0_attn_w_q, L0_attn_w_k, L0_attn_w_v, L0_attn_w_o, L0_attn_gate,
           L0_ffn_w_router, L0_ffn_w_gate, L0_ffn_w_up, L0_ffn_w_down,
           L0_ffn_amplitudes, L0_ffn_expert_scale, L0_ffn_shared_w_gate,
           L0_ffn_shared_w_up, L0_ffn_shared_w_down):
    ct = np.ascontiguousarray
    f32 = np.float32
    x = ct(np.asarray(L0_attn_hidden_states, f32))
    shared = dict(
        wqT=ct(np.asarray(L0_attn_w_q, f32).T),
        wkT=ct(np.asarray(L0_attn_w_k, f32).T),
        wvT=ct(np.asarray(L0_attn_w_v, f32).T),
        woT=ct(np.asarray(L0_attn_w_o, f32).T),
        memT=ct(np.asarray(L0_attn_memories, f32).transpose(2, 0, 1, 3)
                .reshape(128, M * H * 128)),
        mnT=ct(np.asarray(L0_attn_memory_norms, f32).transpose(2, 1, 0)
               .reshape(128, H * M)),
        gate1=ct(np.asarray(L0_attn_gate, f32).reshape(1, H)),
        wrt=ct(np.asarray(L0_ffn_w_router, f32)),
        wg=ct(np.asarray(L0_ffn_w_gate, f32)),
        wu=ct(np.asarray(L0_ffn_w_up, f32)),
        wd=ct(np.asarray(L0_ffn_w_down, f32)),
        amps=ct(np.asarray(L0_ffn_amplitudes, f32).reshape(128, 16)),
        escale=ct(np.asarray(L0_ffn_expert_scale, f32).reshape(128, 1)),
        swg=ct(np.asarray(L0_ffn_shared_w_gate, f32)),
        swu=ct(np.asarray(L0_ffn_shared_w_up, f32)),
        swd=ct(np.asarray(L0_ffn_shared_w_down, f32)),
    )
    if _NC_CACHE[0] is None:
        _NC_CACHE[0] = build_nc()
    nc = _NC_CACHE[0]

    in_maps = []
    for core in range(NCORES):
        seq, half = divmod(core, 2)
        m = dict(shared)
        m["xo"] = ct(x[seq, 1024 * half:1024 * (half + 1), :])
        m["bmask"] = np.full((128, 1), 1.0 if half == 1 else 0.0, f32)
        in_maps.append(m)

    res = run_bass_kernel_spmd(nc, in_maps, core_ids=list(range(NCORES)))
    out = np.empty((B, S, D), f32)
    for core in range(NCORES):
        seq, half = divmod(core, 2)
        out[seq, 1024 * half:1024 * (half + 1), :] = res.results[core]["y"]
    return out


# version 21
# speedup vs baseline: 1.3027x; 1.3027x over previous
import numpy as np
import concourse.bass as bass
import concourse.mybir as mybir
from concourse.bass_utils import run_bass_kernel_spmd

F32 = mybir.dt.float32
BF16 = mybir.dt.bfloat16
AF = mybir.ActivationFunctionType
ALU = mybir.AluOpType
AX = mybir.AxisListType

B, S, D = 4, 2048, 2048
H, HD = 16, 128
M = 4
NCORES = 8
T_OWN = 1024
NC_OWN = 8
NC_ALL = 16
ND = 16
SCALE = HD ** -0.5
CGROUPS = [[0, 1], [2, 3], [4, 5], [6, 7]]
LN_EPS = 1e-5
ATT_EPS = 1e-6
FFN_EPS = 1e-8
KI = 8192
I_DIM = 1024
E_EXP = 128


class Rec:
    __slots__ = ("eng", "fn", "is_dma", "is_coll", "deps", "marked", "val", "dbg")

    def __init__(self, eng, fn, is_dma, dbg, is_coll=False):
        self.eng = eng
        self.fn = fn
        self.is_dma = is_dma
        self.is_coll = is_coll
        self.deps = []
        self.marked = is_dma or is_coll
        self.val = None
        self.dbg = dbg


class Prog:
    CENG = ("t", "v", "s", "g", "y")

    def __init__(self, nc):
        self.nc = nc
        self.streams = {e: [] for e in self.CENG}
        self.regions = {}
        self.free_barrier = []
        self.birth = {}
        self.stacks = {"L": [], "R": [], "P": []}

    def sb(self, name, shape, dtype, side="left", **kw):
        g = self.nc.sbuf_tensor(name, shape, dtype, side=side, **kw)
        h = g.__enter__()
        self.stacks["R" if side == "right" else "L"].append([name, g, False])
        self.birth[name] = list(self.free_barrier)
        return h

    def ps(self, name, shape, dtype=F32):
        bankw = 512 if dtype == F32 else 1024
        shape = list(shape)
        shape[-1] = ((shape[-1] + bankw - 1) // bankw) * bankw
        g = self.nc.psum_tensor(name, shape, dtype)
        h = g.__enter__()
        self.stacks["P"].append([name, g, False])
        self.birth[name] = list(self.free_barrier)
        return h

    def free(self, h):
        name = h.name
        for stack in self.stacks.values():
            for ent in stack:
                if ent[0] == name:
                    ent[2] = True
                    for (n, k), st in list(self.regions.items()):
                        if n == name:
                            if st[0] is not None:
                                self.free_barrier.append(st[0])
                            self.free_barrier.extend(st[1])
                            del self.regions[(n, k)]
                    while stack and stack[-1][2]:
                        _, g, _ = stack.pop()
                        g.__exit__(None, None, None)
                    return
        raise KeyError(name)

    def _region(self, tensor_name, key):
        st = self.regions.get((tensor_name, key))
        if st is None:
            st = [None, []]
            self.regions[(tensor_name, key)] = st
        return st

    def op(self, eng, fn, reads=(), writes=(), dma=False, dbg="", coll=False):
        rec = Rec(eng, fn, dma, dbg, is_coll=coll)
        deps = []
        for t, k in reads:
            st = self._region(t, k)
            if st[0] is not None:
                deps.append(st[0])
        for t, k in writes:
            st = self._region(t, k)
            if st[0] is not None:
                deps.append(st[0])
            deps.extend(st[1])
        for t, k in writes:
            if self.regions.get((t, k), [None])[0] is None:
                bd = self.birth.get(t)
                if bd:
                    deps.extend(bd)
        seen = set()
        for d in deps:
            if id(d) in seen or d is rec:
                continue
            seen.add(id(d))
            if (d.eng == "t" and eng == "t" and not d.is_dma and not dma):
                continue
            rec.deps.append(d)
            d.marked = True
        for t, k in reads:
            self._region(t, k)[1].append(rec)
        for t, k in writes:
            st = self._region(t, k)
            st[0] = rec
            st[1] = []
        self.streams[eng].append(rec)
        return rec

    def dma(self, q, out, in_, reads=(), writes=(), transpose=False, dbg=""):
        def fn(e):
            return e.dma_start(out=out, in_=in_, transpose=transpose)
        return self.op(q, fn, reads, writes, dma=True, dbg=dbg)

    def emit(self, final_outputs=()):
        nc = self.nc
        sink_deps = []
        for t, k in final_outputs:
            st = self.regions.get((t, k))
            if st and st[0] is not None:
                sink_deps.append(st[0])
                st[0].marked = True

        NSLOT = 8
        NCOLL = 4
        csem = {e: nc.alloc_semaphore(name=f"prog_{e}") for e in self.CENG}
        dsem = {(e, s): nc.alloc_semaphore(name=f"dma_{e}{s}")
                for e in ("y", "g", "s") for s in range(NSLOT)}
        for s in range(NCOLL):
            dsem[("o", s)] = nc.alloc_semaphore(name=f"coll{s}")
        cnt = {e: 0 for e in self.CENG}
        dcnt = {e: 0 for e in ("y", "g", "s")}
        ocnt = [0]
        for e in self.CENG:
            for rec in self.streams[e]:
                if not rec.marked:
                    continue
                if rec.is_coll:
                    slot = ocnt[0] % NCOLL
                    use = ocnt[0] // NCOLL + 1
                    ocnt[0] += 1
                    rec.val = ("d", ("o", slot), use)
                elif rec.is_dma:
                    slot = dcnt[e] % NSLOT
                    use = dcnt[e] // NSLOT + 1
                    dcnt[e] += 1
                    rec.val = ("d", (e, slot), 16 * use)
                else:
                    cnt[e] += 1
                    rec.val = ("c", e, cnt[e])

        hiwater = {e: {} for e in self.CENG}

        def get_sem(kind, de):
            return dsem[de] if kind == "d" else csem[de]

        def do_wait(engobj, hw, kind, de, v):
            key = (kind, de)
            if hw.get(key, 0) >= v:
                return
            engobj.wait_ge(get_sem(kind, de), v)
            hw[key] = v

        def emit_stream(e, engobj):
            hw = hiwater[e]
            for rec in self.streams[e]:
                for d in rec.deps:
                    kind, de, v = d.val
                    do_wait(engobj, hw, kind, de, v)
                if rec.marked and (rec.is_dma or rec.is_coll):
                    kind, de, v = rec.val
                    step = 1 if rec.is_coll else 16
                    if v > step:
                        do_wait(engobj, hw, kind, de, v - step)
                ins = rec.fn(engobj)
                if rec.marked:
                    kind, de, v = rec.val
                    amt = 16 if (kind == "d" and not rec.is_coll) else 1
                    ins.then_inc(get_sem(kind, de), amt)
            if e == "g":
                for d in sink_deps:
                    kind, de, v = d.val
                    do_wait(engobj, hw, kind, de, v)

        with nc.Block() as block:
            @block.tensor
            def _(t):
                emit_stream("t", t)

            @block.vector
            def _(v):
                emit_stream("v", v)

            @block.scalar
            def _(s):
                emit_stream("s", s)

            @block.gpsimd
            def _(g):
                emit_stream("g", g)

            @block.sync
            def _(y):
                emit_stream("y", y)


def build_nc():
    nc = bass.Bass("TRN2", target_bir_lowering=False)
    P = Prog(nc)

    def din(name, shape):
        return nc.dram_tensor(name, shape, F32, kind="ExternalInput").ap()

    xo = din("xo", [T_OWN, D])
    bmask = din("bmask", [128, 1])
    wqT = din("wqT", [D, D])
    wkT = din("wkT", [D, D])
    wvT = din("wvT", [D, D])
    woT = din("woT", [D, D])
    memT = din("memT", [128, M * H * 128])
    mnT = din("mnT", [128, H * M])
    gate1 = din("gate1", [1, H])
    wrt = din("wrt", [D, E_EXP])
    wg = din("wg", [D, KI])
    wu = din("wu", [D, KI])
    wd = din("wd", [I_DIM, D])
    amps = din("amps", [128, 16])
    escale = din("escale", [128, 1])
    swg = din("swg", [D, 2 * I_DIM])
    swu = din("swu", [D, 2 * I_DIM])
    swd = din("swd", [2 * I_DIM, D])

    yout = nc.dram_tensor("y", [T_OWN, D], F32, kind="ExternalOutput").ap()
    ln1_dram = nc.dram_tensor("ln1_dram", [T_OWN, D], BF16, kind="Internal").ap()
    cx_in = nc.dram_tensor("cx_in", [128, 16], F32, kind="Internal")
    cx_sum = nc.dram_tensor("cx_sum", [128, 16], F32, kind="Internal")
    cs_in = nc.dram_tensor("cs_in", [8, 128, 258], F32, kind="Internal")
    cs_sum = nc.dram_tensor("cs_sum", [8, 128, 258], F32, kind="Internal")
    attn_dram = nc.dram_tensor("attn_dram", [T_OWN, D], BF16, kind="Internal").ap()
    ln2_dram = nc.dram_tensor("ln2_dram", [T_OWN, D], BF16, kind="Internal").ap()
    u_dram = nc.dram_tensor("u_dram", [T_OWN, I_DIM], BF16, kind="Internal").ap()
    x1_dram = nc.dram_tensor("x1_dram", [T_OWN, D], F32, kind="Internal").ap()

    R = dict(side="right")
    ones_col = P.sb("ones_col", [128, 1], BF16, **R)
    ones_row = P.sb("ones_row", [1, 128], BF16, **R)
    ident_bf = P.sb("ident_bf", [128, 128], BF16, **R)
    mask01 = P.sb("mask01", [128, 256], BF16, **R)
    bmask_sb = P.sb("bmask_sb", [128, 1], F32, **R)
    stat = P.sb("stat", [128, 24], F32, **R)
    mv = P.sb("mv", [128, 2], F32, **R)
    rstd = P.sb("rstd", [128, 1], F32, **R)
    nmean = P.sb("nmean", [128, 1], F32, **R)
    xbar = P.sb("xbar", [128, ND], BF16, **R)
    xbf = P.sb("xbf", [128, 1], F32, **R)
    qbar_sb = P.sb("qbar_sb", [128, 16], BF16, **R)
    mn_sb = P.sb("mn_sb", [128, H * M], BF16, **R)
    sc1 = P.sb("sc1", [1, 256], F32, **R)
    sc1b = P.sb("sc1b", [1, 80], BF16, **R)
    eps_sb = P.sb("eps_sb", [128, 1], F32, **R)
    csb = P.sb("csb", [128, 80], F32, **R)

    P.op("g", lambda e: e.memset(ones_col[:], 1.0), writes=[("ones_col", 0)])
    P.op("g", lambda e: e.memset(ones_row[:], 1.0), writes=[("ones_row", 0)])
    P.op("g", lambda e: e.memset(ident_bf[:], 0.0), writes=[("ident_bf", 0)])
    P.op("g", lambda e: e.affine_select(
        out=ident_bf[:], in_=ident_bf[:], compare_op=ALU.not_equal, fill=1.0,
        base=0, pattern=[[-1, 128]], channel_multiplier=1),
        reads=[("ident_bf", 0)], writes=[("ident_bf", 0)])
    P.op("g", lambda e: e.memset(mask01[:], 1.0), writes=[("mask01", 0)])
    P.op("g", lambda e: e.affine_select(
        out=mask01[:].rearrange("p (q i) -> p q i", i=128),
        in_=mask01[:].rearrange("p (q i) -> p q i", i=128),
        compare_op=ALU.is_ge, fill=0.0,
        base=0, pattern=[[0, 2], [1, 128]], channel_multiplier=-1),
        reads=[("mask01", 0)], writes=[("mask01", 0)])
    P.dma("y", bmask_sb[:], bmask, writes=[("bmask_sb", 0)])
    P.op("g", lambda e: e.memset(eps_sb[:], LN_EPS), writes=[("eps_sb", 0)])
    P.dma("g", mn_sb[:], mnT, writes=[("mn_sb", 0)])

    aT = P.sb("aT", [128, ND, T_OWN], BF16)
    sigq = P.sb("sigq", [128, H, T_OWN], BF16)
    attn_fin = P.sb("attn_fin", [128, NC_OWN, D], BF16)
    ln1T = P.sb("ln1T", [128, ND, T_OWN], BF16, is_transpose_dest=True)
    elu_t2 = [P.sb(f"elu_t{i}", [128, 512], BF16) for i in range(2)]
    elu_e2 = [P.sb(f"elu_e{i}", [128, 512], BF16) for i in range(2)]
    elu_ctr = [0]

    def emit_elu(src_psum, src_regs, dst_ap, dst_regs, n):
        i = elu_ctr[0] % 2
        elu_ctr[0] += 1
        elu_t = elu_t2[i]
        elu_e = elu_e2[i]
        P.op("v", lambda e, elu_t=elu_t: e.tensor_scalar_min(elu_t[:, :n], src_psum, 0.0),
             reads=src_regs, writes=[(elu_t.name, 0)])
        P.op("s", lambda e, elu_t=elu_t, elu_e=elu_e: e.activation(elu_e[:, :n], elu_t[:, :n], AF.Exp),
             reads=[(elu_t.name, 0)], writes=[(elu_e.name, 0)])
        P.op("v", lambda e, elu_e=elu_e: e.scalar_tensor_tensor(
            dst_ap, src_psum, 1.0, elu_e[:, :n], ALU.add, ALU.max),
            reads=list(src_regs) + [(elu_e.name, 0)], writes=dst_regs)


    xbuf = [P.sb(f"xbuf{i}", [128, D], F32) for i in range(2)]
    lnbuf = [P.sb(f"lnbuf{i}", [128, D], BF16) for i in range(2)]

    def ln_tile(src_ap, xb, lb, store_ap, wkey, xsrc_key):
        P.dma("y", xb[:], src_ap, reads=[xsrc_key] if xsrc_key else [],
              writes=[(xb.name, 0)])
        for j in range(4):
            P.op("v", lambda e, j=j, xb=xb: e.bn_stats(
                stat[:, 6 * j:6 * j + 6], xb[:, 512 * j:512 * (j + 1)]),
                reads=[(xb.name, 0)], writes=[("stat", j)])
        P.op("v", lambda e: e.bn_aggr(mv[:], stat[:]),
             reads=[("stat", j) for j in range(4)], writes=[("mv", 0)])
        P.op("s", lambda e: e.activation(rstd[:], mv[:, 1:2], AF.Sqrt, bias=eps_sb[:]),
             reads=[("mv", 0), ("eps_sb", 0)], writes=[("rstd", 0)])
        P.op("v", lambda e: e.reciprocal(rstd[:], rstd[:]),
             reads=[("rstd", 0)], writes=[("rstd", 0)])
        P.op("v", lambda e: e.tensor_tensor(nmean[:], mv[:, 0:1], rstd[:], ALU.mult),
             reads=[("mv", 0), ("rstd", 0)], writes=[("nmean", 0)])
        P.op("v", lambda e: e.tensor_scalar_mul(nmean[:], nmean[:], -1.0),
             reads=[("nmean", 0)], writes=[("nmean", 0)])
        P.op("s", lambda e, xb=xb, lb=lb: e.activation(
            lb[:], xb[:], AF.Identity, bias=nmean[:], scale=rstd[:]),
            reads=[(xb.name, 0), ("rstd", 0), ("nmean", 0)], writes=[(lb.name, 0)])
        P.dma("s", store_ap, lb[:], reads=[(lb.name, 0)], writes=[wkey])

    for i in range(NC_OWN):
        ln_tile(xo[128 * i:128 * (i + 1), :], xbuf[i % 2], lnbuf[i % 2],
                ln1_dram[128 * i:128 * (i + 1), :], ("ln1_dram", i), None)
    for b in xbuf + lnbuf:
        P.free(b)

    ln_own = [("ln1_dram", i) for i in range(NC_OWN)]
    for j in range(ND):
        P.dma("y", ln1T[:, j, :],
              ln1_dram[:, 128 * j:128 * (j + 1)],
              transpose=True, reads=ln_own, writes=[("ln1T", (j, 0))])
    xpart = P.sb("xpart", [128, ND], F32, side="right")
    for j in range(ND):
        P.op("v", lambda e, j=j: e.reduce_sum(xbf[:], ln1T[:, j, :], axis=AX.X),
             reads=[("ln1T", (j, 0))], writes=[("xbf", 0)])
        P.op("v", lambda e, j=j: e.tensor_scalar(
            xpart[:, j:j + 1], xbf[:], SCALE / (2.0 * T_OWN), None, ALU.mult),
            reads=[("xbf", 0)], writes=[("xpart", j)])
    P.dma("y", cx_in.ap(), xpart[:],
          reads=[("xpart", j) for j in range(ND)], writes=[("cx_in", 0)])
    P.op("g", lambda e: e.collective_compute(
        "AllReduce", ALU.add, replica_groups=CGROUPS,
        ins=[cx_in.ap().opt()], outs=[cx_sum.ap().opt()]),
        reads=[("cx_in", 0)], writes=[("cx_sum", 0)], coll=True, dbg="cx")
    P.dma("y", xpart[:], cx_sum.ap(),
          reads=[("cx_sum", 0)], writes=[("xpart", j) for j in range(ND)])
    P.op("v", lambda e: e.tensor_copy(xbar[:], xpart[:]),
         reads=[("xpart", j) for j in range(ND)],
         writes=[("xbar", j) for j in range(ND)])

    HGW = 2
    NHG = H // HGW
    BW = HGW * 128

    wk_d = [P.sb("wk_b0", [128, ND, BW], BF16)]
    wv_d = [P.sb("wv_b0", [128, ND, BW], BF16)]
    kp_hg = P.sb("kp_hg", [128, NC_OWN, BW], BF16)
    kpT_g = [P.sb(f"kpT_g{i}", [128, NC_OWN, BW], BF16) for i in range(2)]
    v_g = [P.sb(f"v_g{i}", [128, NC_OWN, BW], BF16) for i in range(2)]
    spall_g = [P.sb(f"spall{i}", [128, NC_OWN, HGW, 132], BF16) for i in range(2)]
    exch_g = [P.sb(f"exch{i}", [128, 258], F32) for i in range(2)]
    recv_g = [P.sb(f"recv{i}", [128, 258], F32) for i in range(2)]
    sbase_g = [P.sb(f"sbase{i}", [128, HGW, 132], BF16) for i in range(2)]
    scm = P.sb("scm", [128, BW], BF16)
    den_sb = P.sb("den_sb", [128, HGW], F32)

    pk1 = P.ps("pk1", [128, BW])
    pk2 = P.ps("pk2", [128, BW])
    pk3 = P.ps("pk3", [128, BW])
    pkv = P.ps("pkv", [128, 272])
    psc = P.ps("psc", [128, BW])
    pnum = P.ps("pnum", [128, 272])
    pkt2 = [P.ps(f"pkt{i}", [128, 128], BF16) for i in range(2)]
    pk_rot = [pk1, pk2, pk3]

    def produce_hg(hg, mid=None):
        g3 = hg % 2
        kpT_hg = kpT_g[g3]
        v_hg = v_g[g3]
        spall = spall_g[g3]
        exch = exch_g[g3]

        def state_copies(tc):
            for hh in range(HGW):
                if tc > 0:
                    P.op("s", lambda e, tc=tc, hh=hh: e.copy(
                        spall[:, tc, hh, 0:128], pkv[:, 128 * hh:128 * (hh + 1)]),
                        reads=[("pkv", 0)], writes=[(spall.name, (tc, hh))])
                    P.op("s", lambda e, tc=tc, hh=hh: e.copy(
                        spall[:, tc, hh, 128:129], pkv[:, 256 + hh:257 + hh]),
                        reads=[("pkv", 0)], writes=[(spall.name, (tc, hh))])
                else:
                    P.op("s", lambda e, hh=hh: e.memzero(
                        spall[:, 0, hh, 0:130]),
                        writes=[(spall.name, (0, hh))])

        def state_mms(tc):
            for hh in range(HGW):
                P.op("t", lambda e, tc=tc, hh=hh: e.matmul(
                    pkv[:, 128 * hh:128 * (hh + 1)],
                    kp_hg[:, tc, 128 * hh:128 * (hh + 1)],
                    v_hg[:, tc, 128 * hh:128 * (hh + 1)],
                    start=(tc == 0 and hh == 0), stop=True,
                    skip_group_check=(tc != 0 or hh != 0)),
                    reads=[("kp_hg", tc), (v_hg.name, tc)],
                    writes=[("pkv", 0)], dbg=f"upd{hg}")
                P.op("t", lambda e, tc=tc, hh=hh: e.matmul(
                    pkv[:, 256 + hh:257 + hh],
                    kp_hg[:, tc, 128 * hh:128 * (hh + 1)],
                    ones_col[:, 0:1],
                    start=False, stop=True, skip_group_check=True),
                    reads=[("kp_hg", tc), ("ones_col", 0)],
                    writes=[("pkv", 0)], dbg=f"updz{hg}")
        wk_b = wk_d[0]
        wv_b = wv_d[0]
        for j in range(ND):
            P.dma("g", wk_b[:, j, :],
                  wkT[128 * j:128 * (j + 1), BW * hg:BW * (hg + 1)],
                  writes=[(wk_b.name, j)])
            P.dma("g", wv_b[:, j, :],
                  wvT[128 * j:128 * (j + 1), BW * hg:BW * (hg + 1)],
                  writes=[(wv_b.name, j)])
        for tc in range(NC_OWN):
            pk = pk_rot[tc % 3]
            for j in range(ND):
                P.op("t", lambda e, tc=tc, j=j, pk=pk, wk_b=wk_b: e.matmul(
                    pk[:, 0:BW], ln1T[:, j, 128 * tc:128 * (tc + 1)], wk_b[:, j, :],
                    start=(j == 0), stop=(j == ND - 1)),
                    reads=[("ln1T", (j, 0)), (wk_b.name, j)],
                    writes=[(pk.name, 0)], dbg=f"k{hg}t{tc}")
            emit_elu(pk[:, 0:BW], [(pk.name, 0)], kp_hg[:, tc, :], [("kp_hg", tc)], BW)
        if mid is not None:
            mid()
        for tc in range(NC_OWN):
            if tc > 0:
                state_copies(tc - 1)
            pk = pk_rot[tc % 3]
            for j in range(ND):
                P.op("t", lambda e, tc=tc, j=j, pk=pk, wv_b=wv_b: e.matmul(
                    pk[:, 0:BW], ln1T[:, j, 128 * tc:128 * (tc + 1)], wv_b[:, j, :],
                    start=(j == 0), stop=(j == ND - 1)),
                    reads=[("ln1T", (j, 0)), (wv_b.name, j)],
                    writes=[(pk.name, 0)], dbg=f"v{hg}t{tc}")
            P.op("s", lambda e, tc=tc, pk=pk, v_hg=v_hg: e.copy(v_hg[:, tc, :], pk[:, 0:BW]),
                 reads=[(pk.name, 0)], writes=[(v_hg.name, tc)])
            for hh in range(HGW):
                pkt = pkt2[hh]
                P.op("t", lambda e, tc=tc, hh=hh, pkt=pkt: e.transpose(
                    pkt[:, 0:128], kp_hg[:, tc, 128 * hh:128 * (hh + 1)],
                    ident_bf[:]),
                    reads=[("kp_hg", tc), ("ident_bf", 0)],
                    writes=[(pkt.name, 0)], dbg=f"kpt{hg}")
                P.op("s", lambda e, tc=tc, hh=hh, pkt=pkt, kpT_hg=kpT_hg: e.copy(
                    kpT_hg[:, tc, 128 * hh:128 * (hh + 1)], pkt[:, 0:128]),
                    reads=[(pkt.name, 0)], writes=[(kpT_hg.name, tc)])
            if tc > 0:
                state_mms(tc - 1)
            if tc == NC_OWN - 1:
                state_copies(tc)
                state_mms(tc)
        for hh in range(HGW):
            P.op("s", lambda e, hh=hh, exch=exch: e.copy(
                exch[:, 129 * hh:129 * hh + 128], pkv[:, 128 * hh:128 * (hh + 1)]),
                reads=[("pkv", 0)], writes=[(exch.name, hh)])
            P.op("s", lambda e, hh=hh, exch=exch: e.copy(
                exch[:, 129 * hh + 128:129 * hh + 129], pkv[:, 256 + hh:257 + hh]),
                reads=[("pkv", 0)], writes=[(exch.name, hh)])
        P.dma("y", cs_in.ap()[hg], exch[:, 0:258],
              reads=[(exch.name, hh) for hh in range(HGW)],
              writes=[("cs_in", hg)])
        P.op("g", lambda e, hg=hg: e.collective_compute(
            "AllReduce", ALU.add, replica_groups=CGROUPS,
            ins=[cs_in.ap()[hg].opt()], outs=[cs_sum.ap()[hg].opt()]),
            reads=[("cs_in", hg)], writes=[("cs_sum", hg)], coll=True,
            dbg=f"cs{hg}")

    def prep_hg(hg):
        g3 = hg % 2
        exch = exch_g[g3]
        recv = recv_g[g3]
        sbase = sbase_g[g3]
        P.dma("y", recv[:, 0:258], cs_sum.ap()[hg],
              reads=[("cs_sum", hg)], writes=[(recv.name, 0)])
        P.op("v", lambda e, recv=recv, exch=exch: e.tensor_tensor(
            recv[:, 0:258], recv[:, 0:258], exch[:, 0:258], ALU.subtract),
            reads=[(recv.name, 0)] + [(exch.name, hh) for hh in range(HGW)],
            writes=[(recv.name, 0)])
        for hh in range(HGW):
            P.op("s", lambda e, hh=hh, recv=recv, sbase=sbase: e.activation(
                sbase[:, hh, 0:129], recv[:, 129 * hh:129 * hh + 129],
                AF.Identity, scale=bmask_sb[:]),
                reads=[(recv.name, 0), ("bmask_sb", 0)],
                writes=[(sbase.name, hh)])

    def local_hg(hg):
        g3 = hg % 2
        kpT_hg = kpT_g[g3]
        v_hg = v_g[g3]
        spall = spall_g[g3]
        sbase = sbase_g[g3]
        for tc in range(NC_OWN):
            for hh in range(HGW):
                h = HGW * hg + hh
                P.op("t", lambda e, tc=tc, h=h, hh=hh, kpT_hg=kpT_hg: e.matmul(
                    psc[:, 128 * hh:128 * (hh + 1)],
                    kpT_hg[:, tc, 128 * hh:128 * (hh + 1)],
                    sigq[:, h, 128 * tc:128 * (tc + 1)], start=True, stop=True),
                    reads=[(kpT_hg.name, tc), ("sigq", h)],
                    writes=[("psc", hh)] + ([("psc", "qb")] if (hg == 0 and tc == 0 and hh == 0) else []),
                    dbg=f"sc{h}")
            P.op("v", lambda e: e.tensor_tensor(scm[:], psc[:, 0:BW], mask01[:], ALU.mult),
                 reads=[("psc", hh2) for hh2 in range(HGW)] + [("mask01", 0)],
                 writes=[("scm", 0)], dbg=f"mask{hg}t{tc}")
            nmm = [0]
            NTOT = HGW * 6

            def flg():
                nmm[0] += 1
                return dict(start=(nmm[0] == 1), stop=(nmm[0] == NTOT))
            for hh in range(HGW):
                h = HGW * hg + hh
                P.op("t", lambda e, tc=tc, h=h, hh=hh, spall=spall, f=flg(): e.matmul(
                    pnum[:, 128 * hh:128 * (hh + 1)],
                    sigq[:, h, 128 * tc:128 * (tc + 1)],
                    spall[:, tc, hh, 0:128], **f),
                    reads=[("sigq", h), (spall.name, (tc, hh))],
                    writes=[("pnum", 0)] + ([("pnum", "rel")] if (hg == 0 and tc == 0 and hh == 0) else []),
                    dbg=f"num{h}")
                P.op("t", lambda e, tc=tc, h=h, hh=hh, sbase=sbase, f=flg(): e.matmul(
                    pnum[:, 128 * hh:128 * (hh + 1)],
                    sigq[:, h, 128 * tc:128 * (tc + 1)],
                    sbase[:, hh, 0:128], **f),
                    reads=[("sigq", h), (sbase.name, hh)],
                    writes=[("pnum", 0)], dbg=f"numb{h}")
                P.op("t", lambda e, tc=tc, h=h, hh=hh, spall=spall, f=flg(): e.matmul(
                    pnum[:, 256 + hh:257 + hh],
                    sigq[:, h, 128 * tc:128 * (tc + 1)],
                    spall[:, tc, hh, 128:129], **f),
                    reads=[("sigq", h), (spall.name, (tc, hh))],
                    writes=[("pnum", 0)], dbg=f"den{h}")
                P.op("t", lambda e, tc=tc, h=h, hh=hh, sbase=sbase, f=flg(): e.matmul(
                    pnum[:, 256 + hh:257 + hh],
                    sigq[:, h, 128 * tc:128 * (tc + 1)],
                    sbase[:, hh, 128:129], **f),
                    reads=[("sigq", h), (sbase.name, hh)],
                    writes=[("pnum", 0)], dbg=f"denb{h}")
            for hh in range(HGW):
                h = HGW * hg + hh
                P.op("t", lambda e, tc=tc, hh=hh, v_hg=v_hg, f=flg(): e.matmul(
                    pnum[:, 128 * hh:128 * (hh + 1)],
                    scm[:, 128 * hh:128 * (hh + 1)],
                    v_hg[:, tc, 128 * hh:128 * (hh + 1)], **f),
                    reads=[("scm", 0), (v_hg.name, tc)],
                    writes=[("pnum", 0)], dbg=f"numi{h}")
                P.op("t", lambda e, tc=tc, hh=hh, f=flg(): e.matmul(
                    pnum[:, 256 + hh:257 + hh],
                    scm[:, 128 * hh:128 * (hh + 1)],
                    ones_col[:, 0:1], **f),
                    reads=[("scm", 0), ("ones_col", 0)],
                    writes=[("pnum", 0)], dbg=f"deni{h}")
            P.op("v", lambda e: e.tensor_scalar_max(
                den_sb[:], pnum[:, 256:256 + HGW], ATT_EPS),
                reads=[("pnum", 0)], writes=[("den_sb", 0)])
            P.op("v", lambda e: e.reciprocal(den_sb[:], den_sb[:]),
                 reads=[("den_sb", 0)], writes=[("den_sb", 0)])
            P.op("v", lambda e, hg=hg: e.tensor_tensor(
                den_sb[:], den_sb[:],
                csb[:, 64 + HGW * hg:64 + HGW * (hg + 1)], ALU.mult),
                reads=[("den_sb", 0), ("csb", 0)], writes=[("den_sb", 0)])
            P.op("v", lambda e, tc=tc, hg=hg: e.tensor_tensor(
                attn_fin[:, tc, 256 * hg:256 * (hg + 1)]
                .rearrange("p (q e) -> p q e", e=128),
                pnum[:, 0:256].rearrange("p (q e) -> p q e", e=128),
                den_sb[:, 0:HGW].rearrange("p q -> p q ()")
                .to_broadcast([128, HGW, 128]), ALU.mult),
                reads=[("pnum", 0), ("den_sb", 0)],
                writes=[("attn_fin", (tc, HGW * hg)),
                        ("attn_fin", (tc, HGW * hg + 1))], dbg=f"loc{hg}")

    produce_hg(0)
    produce_hg(1)

    wq_q2 = [P.sb(f"wq_q{i}", [128, ND, 256], BF16) for i in range(2)]
    qctr = [0]
    for j in range(ND):
        P.dma("g", wq_q2[0][:, j, :], wqT[128 * j:128 * (j + 1), 0:256],
              writes=[(wq_q2[0].name, j)])
    for qq in range(8):
        wq_q = wq_q2[qq % 2]
        if qq + 1 < 8:
            wn = wq_q2[(qq + 1) % 2]
            for j in range(ND):
                P.dma("g", wn[:, j, :],
                      wqT[128 * j:128 * (j + 1), 256 * (qq + 1):256 * (qq + 2)],
                      writes=[(wn.name, j)])
        for hh in range(2):
            h = 2 * qq + hh
            for t2 in range(2):
                pk = pk_rot[qctr[0] % 3]
                qctr[0] += 1
                for j in range(ND):
                    P.op("t", lambda e, h=h, hh=hh, j=j, t2=t2, pk=pk, wq_q=wq_q: e.matmul(
                        pk[:, 0:512],
                        wq_q[:, j, 128 * hh:128 * (hh + 1)],
                        ln1T[:, j, 512 * t2:512 * (t2 + 1)],
                        start=(j == 0), stop=(j == ND - 1)),
                        reads=[(wq_q.name, j), ("ln1T", (j, 0))],
                        writes=[(pk.name, 0)], dbg=f"q{h}")
                emit_elu(pk[:, 0:512], [(pk.name, 0)],
                         sigq[:, h, 512 * t2:512 * (t2 + 1)], [("sigq", h)], 512)
            for j in range(ND):
                P.op("t", lambda e, h=h, hh=hh, j=j, wq_q=wq_q: e.matmul(
                    psc[0:128, h:h + 1], wq_q[:, j, 128 * hh:128 * (hh + 1)],
                    xbar[:, j:j + 1], start=(j == 0), stop=(j == ND - 1)),
                    reads=[(wq_q.name, j), ("xbar", j)],
                    writes=[("psc", "qb")])
    P.op("s", lambda e: e.copy(qbar_sb[:], psc[:, 0:16]),
         reads=[("psc", "qb")], writes=[("qbar_sb", 0)])
    P.free(wq_q2[1]); P.free(wq_q2[0])

    for h in range(H):
        P.op("t", lambda e, h=h: e.matmul(
            pnum[0:1, 4 * h:4 * h + 4], qbar_sb[:, h:h + 1],
            mn_sb[:, 4 * h:4 * h + 4], start=True, stop=True),
            reads=[("qbar_sb", 0), ("mn_sb", 0)], writes=[("pnum", "rel")])
    P.op("t", lambda e: e.matmul(pnum[0:1, 64:128], ones_col[:, 0:1], mn_sb[:],
                                 start=True, stop=True),
         reads=[("ones_col", 0), ("mn_sb", 0)], writes=[("pnum", "rel")])

    P.op("v", lambda e: e.tensor_reduce(
        sc1[0:1, 112:116], pnum[0:1, 64:128].rearrange("p (m h) -> p m h", h=16),
        axis=AX.X, op=ALU.add),
        reads=[("pnum", "rel")], writes=[("sc1", "sm")])
    P.op("v", lambda e: e.tensor_scalar(
        sc1[0:1, 121:125], sc1[0:1, 112:116], 1e-6, None, ALU.is_lt),
        reads=[("sc1", "sm")], writes=[("sc1", "empty")])
    P.op("v", lambda e: e.scalar_tensor_tensor(
        sc1[0:1, 0:64].rearrange("p (h m) -> p h m", m=4),
        sc1[0:1, 121:125].rearrange("p m -> p () m").to_broadcast([1, 16, 4]),
        -1e30, pnum[0:1, 0:64].rearrange("p (h m) -> p h m", m=4),
        ALU.mult, ALU.add),
        reads=[("sc1", "empty"), ("pnum", "rel")], writes=[("sc1", "rel")])
    P.op("v", lambda e: e.tensor_reduce(
        sc1[0:1, 64:80], sc1[0:1, 0:64].rearrange("p (h m) -> p h m", m=4),
        axis=AX.X, op=ALU.max),
        reads=[("sc1", "rel")], writes=[("sc1", "hmax")])
    P.op("v", lambda e: e.tensor_tensor(
        sc1[0:1, 0:64].rearrange("p (h m) -> p h m", m=4),
        sc1[0:1, 0:64].rearrange("p (h m) -> p h m", m=4),
        sc1[0:1, 64:80].rearrange("p h -> p h ()").to_broadcast([1, 16, 4]),
        ALU.subtract),
        reads=[("sc1", "rel"), ("sc1", "hmax")], writes=[("sc1", "rel")])
    P.op("s", lambda e: e.activation(sc1[0:1, 0:64], sc1[0:1, 0:64], AF.Exp),
         reads=[("sc1", "rel")], writes=[("sc1", "rel")])
    P.op("v", lambda e: e.tensor_reduce(
        sc1[0:1, 80:96], sc1[0:1, 0:64].rearrange("p (h m) -> p h m", m=4),
        axis=AX.X, op=ALU.add),
        reads=[("sc1", "rel")], writes=[("sc1", "hsum")])
    P.op("v", lambda e: e.reciprocal(sc1[0:1, 80:96], sc1[0:1, 80:96]),
         reads=[("sc1", "hsum")], writes=[("sc1", "hsum")])
    P.op("v", lambda e: e.tensor_tensor(
        sc1[0:1, 0:64].rearrange("p (h m) -> p h m", m=4),
        sc1[0:1, 0:64].rearrange("p (h m) -> p h m", m=4),
        sc1[0:1, 80:96].rearrange("p h -> p h ()").to_broadcast([1, 16, 4]),
        ALU.mult),
        reads=[("sc1", "rel"), ("sc1", "hsum")], writes=[("sc1", "rel")])
    P.dma("y", sc1[0:1, 96:112], gate1, writes=[("sc1", "g")])
    P.op("s", lambda e: e.activation(sc1[0:1, 96:112], sc1[0:1, 96:112], AF.Sigmoid),
         reads=[("sc1", "g")], writes=[("sc1", "g")])
    P.op("v", lambda e: e.tensor_tensor(
        sc1[0:1, 120:121], sc1[0:1, 121:122], sc1[0:1, 122:123], ALU.mult),
        reads=[("sc1", "empty")], writes=[("sc1", "nae")])
    P.op("v", lambda e: e.tensor_tensor(
        sc1[0:1, 120:121], sc1[0:1, 120:121], sc1[0:1, 123:124], ALU.mult),
        reads=[("sc1", "empty"), ("sc1", "nae")], writes=[("sc1", "nae")])
    P.op("v", lambda e: e.tensor_tensor(
        sc1[0:1, 120:121], sc1[0:1, 120:121], sc1[0:1, 124:125], ALU.mult),
        reads=[("sc1", "empty"), ("sc1", "nae")], writes=[("sc1", "nae")])
    P.op("v", lambda e: e.tensor_scalar(
        sc1[0:1, 120:121], sc1[0:1, 120:121], -1.0, 1.0, ALU.mult, ALU.add),
        reads=[("sc1", "nae")], writes=[("sc1", "nae")])
    P.op("v", lambda e: e.tensor_scalar(
        sc1[0:1, 116:120], sc1[0:1, 121:125], -1.0, 1.0, ALU.mult, ALU.add),
        reads=[("sc1", "empty")], writes=[("sc1", "nem")])
    P.op("v", lambda e: e.tensor_tensor(
        sc1[0:1, 128:192].rearrange("p (h m) -> p h m", m=4),
        sc1[0:1, 0:64].rearrange("p (h m) -> p h m", m=4),
        sc1[0:1, 96:112].rearrange("p h -> p h ()").to_broadcast([1, 16, 4]),
        ALU.mult),
        reads=[("sc1", "rel"), ("sc1", "g")], writes=[("sc1", "wgg")])
    P.op("v", lambda e: e.tensor_tensor(
        sc1[0:1, 128:192].rearrange("p (h m) -> p h m", m=4),
        sc1[0:1, 128:192].rearrange("p (h m) -> p h m", m=4),
        sc1[0:1, 116:120].rearrange("p m -> p () m").to_broadcast([1, 16, 4]),
        ALU.mult),
        reads=[("sc1", "wgg"), ("sc1", "nem")], writes=[("sc1", "wgg")])
    P.op("v", lambda e: e.tensor_tensor(
        sc1[0:1, 128:192], sc1[0:1, 128:192],
        sc1[0:1, 120:121].to_broadcast([1, 64]), ALU.mult),
        reads=[("sc1", "wgg"), ("sc1", "nae")], writes=[("sc1", "wgg")])
    P.op("v", lambda e: e.tensor_scalar(
        sc1[0:1, 192:208], sc1[0:1, 96:112], -1.0, 1.0, ALU.mult, ALU.add),
        reads=[("sc1", "g")], writes=[("sc1", "gloc")])
    P.op("v", lambda e: e.tensor_copy(sc1b[0:1, 0:64], sc1[0:1, 128:192]),
         reads=[("sc1", "wgg")], writes=[("sc1b", 0)])
    P.op("v", lambda e: e.tensor_copy(sc1b[0:1, 64:80], sc1[0:1, 192:208]),
         reads=[("sc1", "gloc")], writes=[("sc1b", 0)])
    P.op("t", lambda e: e.matmul(psc[:, 0:80], ones_row[0:1, :], sc1b[0:1, :],
                                 start=True, stop=True),
         reads=[("ones_row", 0), ("sc1b", 0)], writes=[("psc", "qb")])
    P.op("s", lambda e: e.copy(csb[:], psc[:, 0:80]),
         reads=[("psc", "qb")], writes=[("csb", 0)])

    prep_hg(0)
    local_hg(0)
    for hg in range(2, NHG):
        produce_hg(hg, mid=lambda hgm=hg - 1: prep_hg(hgm))
        local_hg(hg - 1)
    prep_hg(NHG - 1)
    local_hg(NHG - 1)

    P.free(pkt2[1]); P.free(pkt2[0])
    P.free(pnum); P.free(psc); P.free(pkv)
    P.free(pk3); P.free(pk2); P.free(pk1)
    P.free(den_sb); P.free(scm)
    for b in reversed(sbase_g + recv_g + exch_g + spall_g + v_g + kpT_g):
        P.free(b)
    P.free(kp_hg)
    P.free(wv_d[0]); P.free(wk_d[0])
    P.free(elu_e2[1]); P.free(elu_e2[0]); P.free(elu_t2[1]); P.free(elu_t2[0])
    P.free(ln1T)

    mem_sb = P.sb("mem_sb", [128, M * H * 128], BF16)
    P.dma("g", mem_sb[:], memT, writes=[("mem_sb", 0)])
    pnall = P.ps("pnall", [128, 512])
    nmm2 = [0]
    for tc in range(NC_OWN):
        for h in range(H):
            nmm2[0] += 1
            P.op("t", lambda e, tc=tc, h=h,
                 f=dict(start=(nmm2[0] == 1), stop=(nmm2[0] == 128)): e.matmul(
                pnall[:, 64 * tc + 4 * h:64 * tc + 4 * (h + 1)],
                sigq[:, h, 128 * tc:128 * (tc + 1)],
                mn_sb[:, 4 * h:4 * (h + 1)], **f),
                reads=[("sigq", h), ("mn_sb", 0)],
                writes=[("pnall", 0)], dbg="nall")
    csn_all = P.sb("csn_all", [128, 512], F32)
    P.op("v", lambda e: e.tensor_scalar_max(csn_all[:], pnall[:, 0:512], ATT_EPS),
         reads=[("pnall", 0)], writes=[("csn_all", 0)])
    P.op("v", lambda e: e.reciprocal(csn_all[:], csn_all[:]),
         reads=[("csn_all", 0)], writes=[("csn_all", 0)])
    P.op("v", lambda e: e.tensor_tensor(
        csn_all[:].rearrange("p (tc q) -> p tc q", q=64),
        csn_all[:].rearrange("p (tc q) -> p tc q", q=64),
        csb[:, 0:64].rearrange("p q -> p () q").to_broadcast([128, 8, 64]),
        ALU.mult),
        reads=[("csn_all", 0), ("csb", 0)], writes=[("csn_all", 0)])
    P.free(pnall)

    dg = P.sb("dg", [128, 3, 8, 128], BF16)
    am_sb = P.sb("am_sb", [128, 2, 2, 4, 128], BF16)
    pam2 = [P.ps(f"pamb{i}", [128, 1024]) for i in range(2)]
    pmon2 = [P.ps(f"pmon{i}", [128, 264]) for i in range(2)]
    pat2 = [P.ps(f"pat{i}", [128, 128], BF16) for i in range(2)]
    pairs = [(tc, pp) for tc in range(NC_OWN) for pp in range(8)]

    def dg_emit(i):
        tc, pp = pairs[i]
        P.op("g", lambda e, tc=tc, pp=pp, st=i % 3: e.affine_select(
            out=dg[:, st, :, :],
            in_=csn_all[:, 64 * tc + 8 * pp:64 * tc + 8 * (pp + 1)]
                .rearrange("p q -> p q ()").to_broadcast([128, 8, 128]),
            compare_op=ALU.is_equal, fill=0.0,
            base=0, pattern=[[0, 8], [1, 128]], channel_multiplier=-1),
            reads=[("csn_all", 0)], writes=[("dg", i % 3)], dbg=f"dg{i}")

    def mem_front(i):
        tc, pp = pairs[i]
        st_i = i % 2
        pa = pam2[st_i]
        for hh in range(2):
            h = 2 * pp + hh
            for m in range(M):
                P.op("t", lambda e, tc=tc, h=h, hh=hh, m=m, pa=pa: e.matmul(
                    pa[:, 512 * hh + 128 * m:512 * hh + 128 * (m + 1)],
                    sigq[:, h, 128 * tc:128 * (tc + 1)],
                    mem_sb[:, 128 * (m * H + h):128 * (m * H + h + 1)],
                    start=(m == 0), stop=(m == M - 1)),
                    reads=[("sigq", h), ("mem_sb", 0)],
                    writes=[(pa.name, 0)], dbg=f"am{h}m{m}")
        P.op("s", lambda e, st_i=st_i, pa=pa: e.copy(
            am_sb[:, st_i, :, :, :], pa[:, 0:1024]),
            reads=[(pa.name, 0)], writes=[("am_sb", st_i)])

    def mem_back1(i):
        tc, pp = pairs[i]
        st_i = i % 2
        dst = i % 3
        pmon = pmon2[st_i]
        for hh in range(2):
            h = 2 * pp + hh
            for m in range(M):
                P.op("t", lambda e, st_i=st_i, dst=dst, hh=hh, m=m, pmon=pmon: e.matmul(
                    pmon[:, 128 * hh:128 * (hh + 1)],
                    dg[:, dst, 4 * hh + m, :],
                    am_sb[:, st_i, hh, m, :],
                    start=(m == 0), stop=(m == M - 1)),
                    reads=[("dg", dst), ("am_sb", st_i)],
                    writes=[(pmon.name, 0)], dbg=f"dmm{h}")
        P.op("v", lambda e, tc=tc, pp=pp, pmon=pmon: e.tensor_tensor(
            attn_fin[:, tc, 256 * pp:256 * (pp + 1)],
            pmon[:, 0:256],
            attn_fin[:, tc, 256 * pp:256 * (pp + 1)], ALU.add),
            reads=[(pmon.name, 0), ("attn_fin", (tc, 2 * pp)),
                   ("attn_fin", (tc, 2 * pp + 1))],
            writes=[("attn_fin", (tc, 2 * pp)), ("attn_fin", (tc, 2 * pp + 1))],
            dbg=f"mfin{pp}")

    def mem_back2(i):
        tc, pp = pairs[i]
        for hh in range(2):
            h = 2 * pp + hh
            pat = pat2[hh]
            P.op("t", lambda e, tc=tc, h=h, pat=pat: e.transpose(
                pat[:, 0:128], attn_fin[:, tc, 128 * h:128 * (h + 1)],
                ident_bf[:]),
                reads=[("attn_fin", (tc, h)), ("ident_bf", 0)],
                writes=[(pat.name, 0)], dbg=f"aT{h}")
            P.op("s", lambda e, tc=tc, h=h, pat=pat: e.copy(
                aT[:, h, 128 * tc:128 * (tc + 1)], pat[:, 0:128]),
                reads=[(pat.name, 0)], writes=[("aT", (h, tc))], dbg=f"aTc{h}")

    NP_ = len(pairs)
    dg_emit(0)
    dg_emit(1)
    mem_front(0)
    mem_front(1)
    mem_back1(0)
    for i in range(2, NP_):
        mem_front(i)
        dg_emit(i)
        mem_back1(i - 1)
        mem_back2(i - 2)
    mem_back1(NP_ - 1)
    mem_back2(NP_ - 2)
    mem_back2(NP_ - 1)

    P.free(pat2[1]); P.free(pat2[0])
    P.free(pmon2[1]); P.free(pmon2[0])
    P.free(pam2[1]); P.free(pam2[0])
    P.free(am_sb); P.free(dg)
    P.free(csn_all)
    P.free(attn_fin); P.free(mem_sb)
    P.free(sigq)

    x1 = P.sb("x1", [128, NC_OWN, D], F32)
    wo_b = [P.sb(f"wo_b{i}", [128, ND, 1024], BF16) for i in range(2)]
    xre = [P.sb(f"xre{i}", [128, 2048], F32) for i in range(2)]
    ln2buf = [P.sb(f"ln2buf{i}", [128, D], BF16) for i in range(2)]
    pw1 = P.ps("pw1", [128, 2048])
    pw2 = P.ps("pw2", [128, 2048])
    pw_rot = [pw1, pw2]
    for half in range(2):
        wob = wo_b[half]
        for j in range(ND):
            P.dma("g", wob[:, j, :],
                  woT[128 * j:128 * (j + 1), 1024 * half:1024 * (half + 1)],
                  writes=[(wob.name, j)])
    for tc in range(NC_OWN):
        pw = pw_rot[tc % 2]
        xr_b = xre[tc % 2]
        P.dma("y", xr_b[:], xo[128 * tc:128 * (tc + 1), :],
              writes=[(xr_b.name, 0)])
        for half in range(2):
            wob = wo_b[half]
            for j in range(ND):
                for n in range(2):
                    P.op("t", lambda e, tc=tc, j=j, n=n, half=half, pw=pw, wob=wob: e.matmul(
                        pw[:, 1024 * half + 512 * n:1024 * half + 512 * (n + 1)],
                        aT[:, j, 128 * tc:128 * (tc + 1)],
                        wob[:, j, 512 * n:512 * (n + 1)],
                        start=(j == 0), stop=(j == ND - 1)),
                        reads=[("aT", (j, tc)), (wob.name, j)],
                        writes=[(pw.name, 2 * half + n)], dbg=f"wo{tc}")
        P.op("v", lambda e, tc=tc, pw=pw, xr_b=xr_b: e.tensor_tensor(
            x1[:, tc, :], pw[:], xr_b[:], ALU.add),
            reads=[(pw.name, i) for i in range(4)] + [(xr_b.name, 0)],
            writes=[("x1", tc)], dbg=f"res1_{tc}")
        lb = ln2buf[tc % 2]
        for j in range(4):
            P.op("v", lambda e, j=j, tc=tc: e.bn_stats(
                stat[:, 6 * j:6 * j + 6], x1[:, tc, 512 * j:512 * (j + 1)]),
                reads=[("x1", tc)], writes=[("stat", j)])
        P.op("v", lambda e: e.bn_aggr(mv[:], stat[:]),
             reads=[("stat", j) for j in range(4)], writes=[("mv", 0)])
        P.op("s", lambda e: e.activation(rstd[:], mv[:, 1:2], AF.Sqrt, bias=eps_sb[:]),
             reads=[("mv", 0), ("eps_sb", 0)], writes=[("rstd", 0)])
        P.op("v", lambda e: e.reciprocal(rstd[:], rstd[:]),
             reads=[("rstd", 0)], writes=[("rstd", 0)])
        P.op("v", lambda e: e.tensor_tensor(nmean[:], mv[:, 0:1], rstd[:], ALU.mult),
             reads=[("mv", 0), ("rstd", 0)], writes=[("nmean", 0)])
        P.op("v", lambda e: e.tensor_scalar_mul(nmean[:], nmean[:], -1.0),
             reads=[("nmean", 0)], writes=[("nmean", 0)])
        P.op("s", lambda e, tc=tc, lb=lb: e.activation(
            lb[:], x1[:, tc, :], AF.Identity, bias=nmean[:], scale=rstd[:]),
            reads=[("x1", tc), ("rstd", 0), ("nmean", 0)], writes=[(lb.name, 0)])
        P.dma("s", ln2_dram[128 * tc:128 * (tc + 1), :], lb[:],
              reads=[(lb.name, 0)], writes=[("ln2_dram", tc)])
        P.dma("y", x1_dram[128 * tc:128 * (tc + 1), :], x1[:, tc, :],
              reads=[("x1", tc)], writes=[("x1_dram", tc)])
    P.free(pw2); P.free(pw1)
    P.free(ln2buf[1]); P.free(ln2buf[0])
    P.free(xre[1]); P.free(xre[0]); P.free(wo_b[1]); P.free(wo_b[0])
    P.free(aT)
    P.free(x1)

    sh_basis = P.sb("sh_basis", [128, 16, T_OWN], BF16)
    uT = P.sb("uT", [128, 8, T_OWN], BF16, is_transpose_dest=True)
    c_sb = P.sb("c_sb", [128, NC_OWN, 8], F32)
    ln2T = P.sb("ln2T", [128, ND, T_OWN], BF16, is_transpose_dest=True)
    l2_all = [("ln2_dram", tc) for tc in range(NC_OWN)]
    for j in range(ND):
        P.dma("y", ln2T[:, j, :],
              ln2_dram[:, 128 * j:128 * (j + 1)],
              transpose=True, reads=l2_all, writes=[("ln2T", j)])

    wr_sb = P.sb("wr_sb", [128, ND, E_EXP], BF16)
    for j in range(ND):
        P.dma("g", wr_sb[:, j, :], wrt[128 * j:128 * (j + 1), :],
              writes=[("wr_sb", j)])
    amps_sb = P.sb("amps_sb", [128, 16], F32)
    esc_sb = P.sb("esc_sb", [128, 1], F32)
    mix_sb = P.sb("mix_sb", [128, 8], BF16)
    mixf = P.sb("mixf", [128, 9], F32)
    P.dma("y", amps_sb[:], amps, writes=[("amps_sb", 0)])
    P.dma("y", esc_sb[:], escale, writes=[("esc_sb", 0)])
    P.op("s", lambda e: e.activation(amps_sb[:], amps_sb[:], AF.Square),
         reads=[("amps_sb", 0)], writes=[("amps_sb", 0)])
    P.op("v", lambda e: e.tensor_tensor(
        mixf[:, 0:8],
        amps_sb[:].rearrange("p (k two) -> p k two", two=2)[:, :, 0],
        amps_sb[:].rearrange("p (k two) -> p k two", two=2)[:, :, 1],
        ALU.add),
        reads=[("amps_sb", 0)], writes=[("mixf", 0)])
    P.op("v", lambda e: e.reduce_sum(mixf[:, 8:9], mixf[:, 0:8], axis=AX.X),
         reads=[("mixf", 0)], writes=[("mixf", 1)])
    P.op("v", lambda e: e.tensor_scalar_add(mixf[:, 8:9], mixf[:, 8:9], FFN_EPS),
         reads=[("mixf", 1)], writes=[("mixf", 1)])
    P.op("v", lambda e: e.reciprocal(mixf[:, 8:9], mixf[:, 8:9]),
         reads=[("mixf", 1)], writes=[("mixf", 1)])
    P.op("v", lambda e: e.tensor_tensor(
        mixf[:, 8:9], mixf[:, 8:9], esc_sb[:], ALU.mult),
        reads=[("mixf", 1), ("esc_sb", 0)], writes=[("mixf", 1)])
    P.op("v", lambda e: e.tensor_scalar(
        mix_sb[:], mixf[:, 0:8], mixf[:, 8:9], None, ALU.mult),
        reads=[("mixf", 0), ("mixf", 1)], writes=[("mix_sb", 0)])

    probs2 = [P.sb(f"probs{i}", [128, 128], F32) for i in range(2)]
    rt12 = [P.sb(f"rt1_{i}", [128, 8], F32) for i in range(2)]
    rtv2 = [P.sb(f"rtv{i}", [128, 3], F32) for i in range(2)]
    wmask2 = [P.sb(f"wmask{i}", [128, 128], BF16) for i in range(2)]
    prt2 = [P.ps(f"prt{i}", [128, 128]) for i in range(3)]
    pc2 = [P.ps(f"pc{i}", [128, 8]) for i in range(2)]
    pwt2 = [P.ps(f"pwt{i}", [128, 128], BF16) for i in range(2)]

    def rt_front(tc):
        prt = prt2[tc % 3]
        probs = probs2[tc % 2]
        rt1 = rt12[tc % 2]
        rtv = rtv2[tc % 2]
        wmask = wmask2[tc % 2]
        for j in range(ND):
            P.op("t", lambda e, tc=tc, j=j, prt=prt: e.matmul(
                prt[:, 0:128], ln2T[:, j, 128 * tc:128 * (tc + 1)], wr_sb[:, j, :],
                start=(j == 0), stop=(j == ND - 1)),
                reads=[("ln2T", j), ("wr_sb", j)], writes=[(prt.name, 0)],
                dbg=f"rt{tc}")
        P.op("v", lambda e, prt=prt, rtv=rtv: e.reduce_max(
            rtv[:, 0:1], prt[:, 0:128], axis=AX.X, negate=True),
            reads=[(prt.name, 0)], writes=[(rtv.name, 0)])
        P.op("s", lambda e, prt=prt, probs=probs, rtv=rtv: e.activation(
            probs[:], prt[:, 0:128], AF.Exp, bias=rtv[:, 0:1], accum_out=rtv[:, 1:2]),
            reads=[(prt.name, 0), (rtv.name, 0)],
            writes=[(probs.name, 0), (rtv.name, 1)])
        P.op("v", lambda e, probs=probs, rt1=rt1: e.max(out=rt1[:], in_=probs[:]),
             reads=[(probs.name, 0)], writes=[(rt1.name, 0)])
        P.op("v", lambda e, probs=probs, rt1=rt1, wmask=wmask: e.scalar_tensor_tensor(
            wmask[:], probs[:], rt1[:, 3:4], probs[:], ALU.is_ge, ALU.mult),
            reads=[(probs.name, 0), (rt1.name, 0)], writes=[(wmask.name, 0)])
        P.op("v", lambda e, wmask=wmask, rtv=rtv: e.reduce_sum(
            rtv[:, 2:3], wmask[:], axis=AX.X),
            reads=[(wmask.name, 0)], writes=[(rtv.name, 2)])
        P.op("v", lambda e, rtv=rtv: e.scalar_tensor_tensor(
            rtv[:, 2:3], rtv[:, 1:2], FFN_EPS, rtv[:, 2:3], ALU.mult, ALU.add),
            reads=[(rtv.name, 1), (rtv.name, 2)], writes=[(rtv.name, 2)])
        P.op("v", lambda e, rtv=rtv: e.reciprocal(rtv[:, 2:3], rtv[:, 2:3]),
             reads=[(rtv.name, 2)], writes=[(rtv.name, 2)])

    def rt_back(tc):
        rtv = rtv2[tc % 2]
        wmask = wmask2[tc % 2]
        pwt = pwt2[tc % 2]
        pc = pc2[tc % 2]
        P.op("t", lambda e, pwt=pwt, wmask=wmask: e.transpose(
            pwt[:, 0:128], wmask[:], ident_bf[:]),
            reads=[(wmask.name, 0), ("ident_bf", 0)], writes=[(pwt.name, 0)])
        P.op("s", lambda e, pwt=pwt, wmask=wmask: e.copy(wmask[:], pwt[:, 0:128]),
             reads=[(pwt.name, 0)], writes=[(wmask.name, 0)])
        P.op("t", lambda e, pc=pc, wmask=wmask: e.matmul(
            pc[:, 0:8], wmask[:], mix_sb[:], start=True, stop=True),
            reads=[(wmask.name, 0), ("mix_sb", 0)], writes=[(pc.name, 0)])
        P.op("v", lambda e, tc=tc, pc=pc, rtv=rtv: e.tensor_scalar(
            c_sb[:, tc, :], pc[:, 0:8], rtv[:, 2:3], None, ALU.mult),
            reads=[(pc.name, 0), (rtv.name, 2)], writes=[("c_sb", tc)])

    rt_front(0)
    for tc in range(1, NC_OWN):
        rt_front(tc)
        rt_back(tc - 1)
    rt_back(NC_OWN - 1)
    P.free(pwt2[1]); P.free(pwt2[0]); P.free(pc2[1]); P.free(pc2[0])
    P.free(prt2[2]); P.free(prt2[1]); P.free(prt2[0])
    P.free(wmask2[1]); P.free(wmask2[0]); P.free(rtv2[1]); P.free(rtv2[0])
    P.free(rt12[1]); P.free(rt12[0]); P.free(probs2[1]); P.free(probs2[0])
    P.free(mixf); P.free(mix_sb); P.free(esc_sb); P.free(amps_sb)
    P.free(wr_sb)

    u_acc = P.sb("u_acc", [128, NC_OWN, I_DIM], F32)
    wgq = [P.sb(f"wgq{i}", [128, ND, 512], BF16) for i in range(2)]
    wuq = [P.sb(f"wuq{i}", [128, ND, 512], BF16) for i in range(2)]
    silu_g = P.sb("silu_g", [128, NC_OWN, 512], BF16)
    ut2 = [P.sb(f"ut{i}", [128, 512], F32) for i in range(2)]
    pg_rot = [P.ps(f"pg{i}", [128, 512]) for i in range(3)]
    pu_rot = [P.ps(f"pu{i}", [128, 512]) for i in range(3)]

    def ffn_loads(q):
        wgb = wgq[q % 2]
        wub = wuq[q % 2]
        for j in range(ND):
            P.dma("g", wgb[:, j, :],
                  wg[128 * j:128 * (j + 1), 512 * q:512 * (q + 1)],
                  writes=[(wgb.name, j)])
            P.dma("g", wub[:, j, :],
                  wu[128 * j:128 * (j + 1), 512 * q:512 * (q + 1)],
                  writes=[(wub.name, j)])

    ffn_loads(0)
    ffn_loads(1)
    for q in range(16):
        k_idx = q // 2
        ihalf = q % 2
        wgb = wgq[q % 2]
        wub = wuq[q % 2]
        for tc in range(NC_OWN):
            pg = pg_rot[tc % 3]
            pu = pu_rot[tc % 3]
            ut = ut2[tc % 2]
            for j in range(ND):
                P.op("t", lambda e, tc=tc, j=j, pg=pg, wgb=wgb: e.matmul(
                    pg[:], ln2T[:, j, 128 * tc:128 * (tc + 1)], wgb[:, j, :],
                    start=(j == 0), stop=(j == ND - 1)),
                    reads=[("ln2T", j), (wgb.name, j)],
                    writes=[(pg.name, 0)], dbg=f"g{q}t{tc}")
            P.op("s", lambda e, tc=tc, pg=pg: e.activation(
                silu_g[:, tc, :], pg[:], AF.Silu),
                reads=[(pg.name, 0)], writes=[("silu_g", tc)])
            for j in range(ND):
                P.op("t", lambda e, tc=tc, j=j, pu=pu, wub=wub: e.matmul(
                    pu[:], ln2T[:, j, 128 * tc:128 * (tc + 1)], wub[:, j, :],
                    start=(j == 0), stop=(j == ND - 1)),
                    reads=[("ln2T", j), (wub.name, j)],
                    writes=[(pu.name, 0)], dbg=f"u{q}t{tc}")
            P.op("v", lambda e, tc=tc, k_idx=k_idx, pu=pu, ut=ut: e.scalar_tensor_tensor(
                ut[:], pu[:], c_sb[:, tc, k_idx:k_idx + 1], silu_g[:, tc, :],
                ALU.mult, ALU.mult),
                reads=[(pu.name, 0), ("c_sb", tc), ("silu_g", tc)],
                writes=[(ut.name, 0)], dbg=f"ut{q}t{tc}")
            if k_idx == 0:
                P.op("v", lambda e, tc=tc, ihalf=ihalf, ut=ut: e.tensor_copy(
                    u_acc[:, tc, 512 * ihalf:512 * (ihalf + 1)], ut[:]),
                    reads=[(ut.name, 0)], writes=[("u_acc", (tc, ihalf))])
            else:
                P.op("v", lambda e, tc=tc, ihalf=ihalf, ut=ut: e.tensor_tensor(
                    u_acc[:, tc, 512 * ihalf:512 * (ihalf + 1)],
                    u_acc[:, tc, 512 * ihalf:512 * (ihalf + 1)], ut[:], ALU.add),
                    reads=[(ut.name, 0), ("u_acc", (tc, ihalf))],
                    writes=[("u_acc", (tc, ihalf))])
        if q + 2 < 16:
            ffn_loads(q + 2)
    for tc in range(NC_OWN):
        P.dma("g", u_dram[128 * tc:128 * (tc + 1), :], u_acc[:, tc, :],
              reads=[("u_acc", (tc, 0)), ("u_acc", (tc, 1))],
              writes=[("u_dram", tc)])
    for p in reversed(pu_rot + pg_rot):
        P.free(p)
    P.free(ut2[1]); P.free(ut2[0]); P.free(silu_g)
    P.free(wuq[1]); P.free(wuq[0]); P.free(wgq[1]); P.free(wgq[0])
    P.free(u_acc)

    u_all = [("u_dram", tc) for tc in range(NC_OWN)]
    for j in range(8):
        P.dma("y", uT[:, j, :],
              u_dram[:, 128 * j:128 * (j + 1)],
              transpose=True, reads=u_all, writes=[("uT", j)])

    swgq = [P.sb(f"swgq{i}", [128, ND, 512], BF16) for i in range(2)]
    swuq = [P.sb(f"swuq{i}", [128, ND, 512], BF16) for i in range(2)]
    ssilu = P.sb("ssilu", [128, T_OWN], BF16)
    psh = [P.ps(f"psh{i}", [128, 1024]) for i in range(4)]

    def sh_loads(q):
        sgb = swgq[q % 2]
        sub = swuq[q % 2]
        for j in range(ND):
            P.dma("g", sgb[:, j, :],
                  swg[128 * j:128 * (j + 1), 512 * q:512 * (q + 1)],
                  writes=[(sgb.name, j)])
            P.dma("g", sub[:, j, :],
                  swu[128 * j:128 * (j + 1), 512 * q:512 * (q + 1)],
                  writes=[(sub.name, j)])

    sh_loads(0)
    sh_loads(1)
    for q in range(4):
        sgb = swgq[q % 2]
        sub = swuq[q % 2]
        for fc in range(4):
            of = 4 * q + fc
            pga = psh[2 * (fc % 2)]
            pua = psh[2 * (fc % 2) + 1]
            for j in range(ND):
                for n in range(2):
                    P.op("t", lambda e, j=j, n=n, fc=fc, pga=pga, sgb=sgb: e.matmul(
                        pga[:, 512 * n:512 * (n + 1)],
                        sgb[:, j, 128 * fc:128 * (fc + 1)],
                        ln2T[:, j, 512 * n:512 * (n + 1)],
                        start=(j == 0), stop=(j == ND - 1)),
                        reads=[(sgb.name, j), ("ln2T", j)],
                        writes=[(pga.name, n)], dbg=f"sg{of}")
            P.op("s", lambda e, pga=pga: e.activation(ssilu[:], pga[:], AF.Silu),
                 reads=[(pga.name, 0), (pga.name, 1)], writes=[("ssilu", 0)])
            for j in range(ND):
                for n in range(2):
                    P.op("t", lambda e, j=j, n=n, fc=fc, pua=pua, sub=sub: e.matmul(
                        pua[:, 512 * n:512 * (n + 1)],
                        sub[:, j, 128 * fc:128 * (fc + 1)],
                        ln2T[:, j, 512 * n:512 * (n + 1)],
                        start=(j == 0), stop=(j == ND - 1)),
                        reads=[(sub.name, j), ("ln2T", j)],
                        writes=[(pua.name, n)], dbg=f"su{of}")
            P.op("v", lambda e, of=of, pua=pua: e.tensor_tensor(
                sh_basis[:, of, :], pua[:], ssilu[:], ALU.mult),
                reads=[(pua.name, 0), (pua.name, 1), ("ssilu", 0)],
                writes=[("sh_basis", of)], dbg=f"sb{of}")
        if q + 2 < 4:
            sh_loads(q + 2)
    for p in reversed(psh):
        P.free(p)
    P.free(ssilu)
    P.free(swuq[1]); P.free(swuq[0]); P.free(swgq[1]); P.free(swgq[0])
    P.free(ln2T)

    wd_sb = P.sb("wd_sb", [128, 8, D], BF16)
    for j in range(8):
        P.dma("g", wd_sb[:, j, :], wd[128 * j:128 * (j + 1), :],
              writes=[("wd_sb", j)])
    swd_sb = P.sb("swd_sb", [128, 16, D], BF16)
    for j in range(16):
        P.dma("g", swd_sb[:, j, :], swd[128 * j:128 * (j + 1), :],
              writes=[("swd_sb", j)])
    ybuf = [P.sb(f"ybuf{i}", [128, D], F32) for i in range(2)]
    x1b = [P.sb(f"x1b{i}", [128, D], F32) for i in range(2)]
    pf = [P.ps(f"pf{i}", [128, 2048]) for i in range(2)]

    for tc in range(NC_OWN):
        pfb = pf[tc % 2]
        yb = ybuf[tc % 2]
        xb = x1b[tc % 2]
        P.dma("y", xb[:], x1_dram[128 * tc:128 * (tc + 1), :],
              reads=[("x1_dram", tc)], writes=[(xb.name, 0)])
        for j in range(8):
            for n in range(4):
                P.op("t", lambda e, tc=tc, j=j, n=n, pfb=pfb: e.matmul(
                    pfb[:, 512 * n:512 * (n + 1)],
                    uT[:, j, 128 * tc:128 * (tc + 1)],
                    wd_sb[:, j, 512 * n:512 * (n + 1)],
                    start=(j == 0), stop=False),
                    reads=[("uT", j), ("wd_sb", j)],
                    writes=[(pfb.name, n)], dbg=f"dn{tc}")
        for j in range(16):
            for n in range(4):
                P.op("t", lambda e, tc=tc, j=j, n=n, pfb=pfb: e.matmul(
                    pfb[:, 512 * n:512 * (n + 1)],
                    sh_basis[:, j, 128 * tc:128 * (tc + 1)],
                    swd_sb[:, j, 512 * n:512 * (n + 1)],
                    start=False, stop=(j == 15)),
                    reads=[("sh_basis", j), ("swd_sb", j)],
                    writes=[(pfb.name, n)], dbg=f"sd{tc}")
        P.op("v", lambda e, pfb=pfb, yb=yb, xb=xb: e.tensor_tensor(
            yb[:], pfb[:], xb[:], ALU.add),
            reads=[(pfb.name, i) for i in range(4)] + [(xb.name, 0)],
            writes=[(yb.name, 0)], dbg=f"res2_{tc}")
        P.dma("y", yout[128 * tc:128 * (tc + 1), :], yb[:],
              reads=[(yb.name, 0)], writes=[("y", tc)])

    P.emit(final_outputs=[("y", tc) for tc in range(NC_OWN)])
    return nc


_NC_CACHE = [None]


def kernel(L0_attn_hidden_states, L0_attn_memories, L0_attn_memory_norms,
           L0_attn_w_q, L0_attn_w_k, L0_attn_w_v, L0_attn_w_o, L0_attn_gate,
           L0_ffn_w_router, L0_ffn_w_gate, L0_ffn_w_up, L0_ffn_w_down,
           L0_ffn_amplitudes, L0_ffn_expert_scale, L0_ffn_shared_w_gate,
           L0_ffn_shared_w_up, L0_ffn_shared_w_down):
    ct = np.ascontiguousarray
    f32 = np.float32
    x = ct(np.asarray(L0_attn_hidden_states, f32))
    shared = dict(
        wqT=ct(np.asarray(L0_attn_w_q, f32).T),
        wkT=ct(np.asarray(L0_attn_w_k, f32).T),
        wvT=ct(np.asarray(L0_attn_w_v, f32).T),
        woT=ct(np.asarray(L0_attn_w_o, f32).T),
        memT=ct(np.asarray(L0_attn_memories, f32).transpose(2, 0, 1, 3)
                .reshape(128, M * H * 128)),
        mnT=ct(np.asarray(L0_attn_memory_norms, f32).transpose(2, 1, 0)
               .reshape(128, H * M)),
        gate1=ct(np.asarray(L0_attn_gate, f32).reshape(1, H)),
        wrt=ct(np.asarray(L0_ffn_w_router, f32)),
        wg=ct(np.asarray(L0_ffn_w_gate, f32)),
        wu=ct(np.asarray(L0_ffn_w_up, f32)),
        wd=ct(np.asarray(L0_ffn_w_down, f32)),
        amps=ct(np.asarray(L0_ffn_amplitudes, f32).reshape(128, 16)),
        escale=ct(np.asarray(L0_ffn_expert_scale, f32).reshape(128, 1)),
        swg=ct(np.asarray(L0_ffn_shared_w_gate, f32)),
        swu=ct(np.asarray(L0_ffn_shared_w_up, f32)),
        swd=ct(np.asarray(L0_ffn_shared_w_down, f32)),
    )
    if _NC_CACHE[0] is None:
        _NC_CACHE[0] = build_nc()
    nc = _NC_CACHE[0]

    in_maps = []
    for core in range(NCORES):
        seq, half = divmod(core, 2)
        m = dict(shared)
        m["xo"] = ct(x[seq, 1024 * half:1024 * (half + 1), :])
        m["bmask"] = np.full((128, 1), 1.0 if half == 1 else 0.0, f32)
        in_maps.append(m)

    res = run_bass_kernel_spmd(nc, in_maps, core_ids=list(range(NCORES)))
    out = np.empty((B, S, D), f32)
    for core in range(NCORES):
        seq, half = divmod(core, 2)
        out[seq, 1024 * half:1024 * (half + 1), :] = res.results[core]["y"]
    return out


# version 23
# speedup vs baseline: 1.3241x; 1.0164x over previous
import numpy as np
import concourse.bass as bass
import concourse.mybir as mybir
from concourse.bass_utils import run_bass_kernel_spmd

F32 = mybir.dt.float32
BF16 = mybir.dt.bfloat16
AF = mybir.ActivationFunctionType
ALU = mybir.AluOpType
AX = mybir.AxisListType

B, S, D = 4, 2048, 2048
H, HD = 16, 128
M = 4
NCORES = 8
T_OWN = 1024
NC_OWN = 8
NC_ALL = 16
ND = 16
SCALE = HD ** -0.5
CGROUPS = [[0, 1], [2, 3], [4, 5], [6, 7]]
LN_EPS = 1e-5
ATT_EPS = 1e-6
FFN_EPS = 1e-8
KI = 8192
I_DIM = 1024
E_EXP = 128


class Rec:
    __slots__ = ("eng", "fn", "is_dma", "is_coll", "deps", "marked", "val", "dbg")

    def __init__(self, eng, fn, is_dma, dbg, is_coll=False):
        self.eng = eng
        self.fn = fn
        self.is_dma = is_dma
        self.is_coll = is_coll
        self.deps = []
        self.marked = is_dma or is_coll
        self.val = None
        self.dbg = dbg


class Prog:
    CENG = ("t", "v", "s", "g", "y")

    def __init__(self, nc):
        self.nc = nc
        self.streams = {e: [] for e in self.CENG}
        self.regions = {}
        self.free_barrier = []
        self.birth = {}
        self.stacks = {"L": [], "R": [], "P": []}

    def sb(self, name, shape, dtype, side="left", **kw):
        g = self.nc.sbuf_tensor(name, shape, dtype, side=side, **kw)
        h = g.__enter__()
        self.stacks["R" if side == "right" else "L"].append([name, g, False])
        self.birth[name] = list(self.free_barrier)
        return h

    def ps(self, name, shape, dtype=F32):
        bankw = 512 if dtype == F32 else 1024
        shape = list(shape)
        shape[-1] = ((shape[-1] + bankw - 1) // bankw) * bankw
        g = self.nc.psum_tensor(name, shape, dtype)
        h = g.__enter__()
        self.stacks["P"].append([name, g, False])
        self.birth[name] = list(self.free_barrier)
        return h

    def free(self, h):
        name = h.name
        for stack in self.stacks.values():
            for ent in stack:
                if ent[0] == name:
                    ent[2] = True
                    for (n, k), st in list(self.regions.items()):
                        if n == name:
                            if st[0] is not None:
                                self.free_barrier.append(st[0])
                            self.free_barrier.extend(st[1])
                            del self.regions[(n, k)]
                    while stack and stack[-1][2]:
                        _, g, _ = stack.pop()
                        g.__exit__(None, None, None)
                    return
        raise KeyError(name)

    def _region(self, tensor_name, key):
        st = self.regions.get((tensor_name, key))
        if st is None:
            st = [None, []]
            self.regions[(tensor_name, key)] = st
        return st

    def op(self, eng, fn, reads=(), writes=(), dma=False, dbg="", coll=False):
        rec = Rec(eng, fn, dma, dbg, is_coll=coll)
        deps = []
        for t, k in reads:
            st = self._region(t, k)
            if st[0] is not None:
                deps.append(st[0])
        for t, k in writes:
            st = self._region(t, k)
            if st[0] is not None:
                deps.append(st[0])
            deps.extend(st[1])
        for t, k in writes:
            if self.regions.get((t, k), [None])[0] is None:
                bd = self.birth.get(t)
                if bd:
                    deps.extend(bd)
        seen = set()
        for d in deps:
            if id(d) in seen or d is rec:
                continue
            seen.add(id(d))
            if (d.eng == "t" and eng == "t" and not d.is_dma and not dma):
                continue
            rec.deps.append(d)
            d.marked = True
        for t, k in reads:
            self._region(t, k)[1].append(rec)
        for t, k in writes:
            st = self._region(t, k)
            st[0] = rec
            st[1] = []
        self.streams[eng].append(rec)
        return rec

    def dma(self, q, out, in_, reads=(), writes=(), transpose=False, dbg=""):
        def fn(e):
            return e.dma_start(out=out, in_=in_, transpose=transpose)
        return self.op(q, fn, reads, writes, dma=True, dbg=dbg)

    def emit(self, final_outputs=()):
        nc = self.nc
        sink_deps = []
        for t, k in final_outputs:
            st = self.regions.get((t, k))
            if st and st[0] is not None:
                sink_deps.append(st[0])
                st[0].marked = True

        NSLOT = 8
        NCOLL = 4
        csem = {e: nc.alloc_semaphore(name=f"prog_{e}") for e in self.CENG}
        dsem = {(e, s): nc.alloc_semaphore(name=f"dma_{e}{s}")
                for e in ("y", "g", "s") for s in range(NSLOT)}
        for s in range(NCOLL):
            dsem[("o", s)] = nc.alloc_semaphore(name=f"coll{s}")
        cnt = {e: 0 for e in self.CENG}
        dcnt = {e: 0 for e in ("y", "g", "s")}
        ocnt = [0]
        for e in self.CENG:
            for rec in self.streams[e]:
                if not rec.marked:
                    continue
                if rec.is_coll:
                    slot = ocnt[0] % NCOLL
                    use = ocnt[0] // NCOLL + 1
                    ocnt[0] += 1
                    rec.val = ("d", ("o", slot), use)
                elif rec.is_dma:
                    slot = dcnt[e] % NSLOT
                    use = dcnt[e] // NSLOT + 1
                    dcnt[e] += 1
                    rec.val = ("d", (e, slot), 16 * use)
                else:
                    cnt[e] += 1
                    rec.val = ("c", e, cnt[e])

        hiwater = {e: {} for e in self.CENG}

        def get_sem(kind, de):
            return dsem[de] if kind == "d" else csem[de]

        def do_wait(engobj, hw, kind, de, v):
            key = (kind, de)
            if hw.get(key, 0) >= v:
                return
            engobj.wait_ge(get_sem(kind, de), v)
            hw[key] = v

        def emit_stream(e, engobj):
            hw = hiwater[e]
            for rec in self.streams[e]:
                for d in rec.deps:
                    kind, de, v = d.val
                    do_wait(engobj, hw, kind, de, v)
                if rec.marked and (rec.is_dma or rec.is_coll):
                    kind, de, v = rec.val
                    step = 1 if rec.is_coll else 16
                    if v > step:
                        do_wait(engobj, hw, kind, de, v - step)
                ins = rec.fn(engobj)
                if rec.marked:
                    kind, de, v = rec.val
                    amt = 16 if (kind == "d" and not rec.is_coll) else 1
                    ins.then_inc(get_sem(kind, de), amt)
            if e == "g":
                for d in sink_deps:
                    kind, de, v = d.val
                    do_wait(engobj, hw, kind, de, v)

        with nc.Block() as block:
            @block.tensor
            def _(t):
                emit_stream("t", t)

            @block.vector
            def _(v):
                emit_stream("v", v)

            @block.scalar
            def _(s):
                emit_stream("s", s)

            @block.gpsimd
            def _(g):
                emit_stream("g", g)

            @block.sync
            def _(y):
                emit_stream("y", y)


def build_nc():
    nc = bass.Bass("TRN2", target_bir_lowering=False)
    P = Prog(nc)

    def din(name, shape):
        return nc.dram_tensor(name, shape, F32, kind="ExternalInput").ap()

    xo = din("xo", [T_OWN, D])
    bmask = din("bmask", [128, 1])
    wqT = din("wqT", [D, D])
    wkT = din("wkT", [D, D])
    wvT = din("wvT", [D, D])
    woT = din("woT", [D, D])
    memT = din("memT", [128, M * H * 128])
    mnT = din("mnT", [128, H * M])
    gate1 = din("gate1", [1, H])
    wrt = din("wrt", [D, E_EXP])
    wg = din("wg", [D, KI])
    wu = din("wu", [D, KI])
    wd = din("wd", [I_DIM, D])
    amps = din("amps", [128, 16])
    escale = din("escale", [128, 1])
    swg = din("swg", [D, 2 * I_DIM])
    swu = din("swu", [D, 2 * I_DIM])
    swd = din("swd", [2 * I_DIM, D])

    yout = nc.dram_tensor("y", [T_OWN, D], F32, kind="ExternalOutput").ap()
    ln1_dram = nc.dram_tensor("ln1_dram", [T_OWN, D], BF16, kind="Internal").ap()
    cx_in = nc.dram_tensor("cx_in", [128, 16], F32, kind="Internal")
    cx_sum = nc.dram_tensor("cx_sum", [128, 16], F32, kind="Internal")
    cs_in = nc.dram_tensor("cs_in", [8, 128, 258], F32, kind="Internal")
    cs_sum = nc.dram_tensor("cs_sum", [8, 128, 258], F32, kind="Internal")
    attn_dram = nc.dram_tensor("attn_dram", [T_OWN, D], BF16, kind="Internal").ap()
    ln2_dram = nc.dram_tensor("ln2_dram", [T_OWN, D], BF16, kind="Internal").ap()
    u_dram = nc.dram_tensor("u_dram", [T_OWN, I_DIM], BF16, kind="Internal").ap()
    x1_dram = nc.dram_tensor("x1_dram", [T_OWN, D], F32, kind="Internal").ap()

    R = dict(side="right")
    ones_col = P.sb("ones_col", [128, 1], BF16, **R)
    ones_row = P.sb("ones_row", [1, 128], BF16, **R)
    ident_bf = P.sb("ident_bf", [128, 128], BF16, **R)
    mask01 = P.sb("mask01", [128, 256], BF16, **R)
    bmask_sb = P.sb("bmask_sb", [128, 1], F32, **R)
    stat = P.sb("stat", [128, 24], F32, **R)
    mv = P.sb("mv", [128, 2], F32, **R)
    rstd = P.sb("rstd", [128, 1], F32, **R)
    nmean = P.sb("nmean", [128, 1], F32, **R)
    xbar = P.sb("xbar", [128, ND], BF16, **R)
    xbf = P.sb("xbf", [128, 1], F32, **R)
    qbar_sb = P.sb("qbar_sb", [128, 16], BF16, **R)
    mn_sb = P.sb("mn_sb", [128, H * M], BF16, **R)
    sc1 = P.sb("sc1", [1, 256], F32, **R)
    sc1b = P.sb("sc1b", [1, 80], BF16, **R)
    eps_sb = P.sb("eps_sb", [128, 1], F32, **R)
    csb = P.sb("csb", [128, 80], F32, **R)

    P.op("g", lambda e: e.memset(ones_col[:], 1.0), writes=[("ones_col", 0)])
    P.op("g", lambda e: e.memset(ones_row[:], 1.0), writes=[("ones_row", 0)])
    P.op("g", lambda e: e.memset(ident_bf[:], 0.0), writes=[("ident_bf", 0)])
    P.op("g", lambda e: e.affine_select(
        out=ident_bf[:], in_=ident_bf[:], compare_op=ALU.not_equal, fill=1.0,
        base=0, pattern=[[-1, 128]], channel_multiplier=1),
        reads=[("ident_bf", 0)], writes=[("ident_bf", 0)])
    P.op("g", lambda e: e.memset(mask01[:], 1.0), writes=[("mask01", 0)])
    P.op("g", lambda e: e.affine_select(
        out=mask01[:].rearrange("p (q i) -> p q i", i=128),
        in_=mask01[:].rearrange("p (q i) -> p q i", i=128),
        compare_op=ALU.is_ge, fill=0.0,
        base=0, pattern=[[0, 2], [1, 128]], channel_multiplier=-1),
        reads=[("mask01", 0)], writes=[("mask01", 0)])
    P.dma("y", bmask_sb[:], bmask, writes=[("bmask_sb", 0)])
    P.op("g", lambda e: e.memset(eps_sb[:], LN_EPS), writes=[("eps_sb", 0)])
    P.dma("g", mn_sb[:], mnT, writes=[("mn_sb", 0)])

    aT = P.sb("aT", [128, ND, T_OWN], BF16)
    sigq = P.sb("sigq", [128, H, T_OWN], BF16)
    attn_fin = P.sb("attn_fin", [128, NC_OWN, D], BF16)
    ln1T = P.sb("ln1T", [128, ND, T_OWN], BF16, is_transpose_dest=True)
    elu_t2 = [P.sb(f"elu_t{i}", [128, 512], BF16) for i in range(2)]
    elu_e2 = [P.sb(f"elu_e{i}", [128, 512], BF16) for i in range(2)]
    elu_ctr = [0]

    def emit_elu(src_psum, src_regs, dst_ap, dst_regs, n):
        i = elu_ctr[0] % 2
        elu_ctr[0] += 1
        elu_t = elu_t2[i]
        elu_e = elu_e2[i]
        P.op("v", lambda e, elu_t=elu_t: e.tensor_scalar_min(elu_t[:, :n], src_psum, 0.0),
             reads=src_regs, writes=[(elu_t.name, 0)])
        P.op("s", lambda e, elu_t=elu_t, elu_e=elu_e: e.activation(elu_e[:, :n], elu_t[:, :n], AF.Exp),
             reads=[(elu_t.name, 0)], writes=[(elu_e.name, 0)])
        P.op("v", lambda e, elu_e=elu_e: e.scalar_tensor_tensor(
            dst_ap, src_psum, 1.0, elu_e[:, :n], ALU.add, ALU.max),
            reads=list(src_regs) + [(elu_e.name, 0)], writes=dst_regs)


    xbuf = [P.sb(f"xbuf{i}", [128, D], F32) for i in range(2)]
    lnbuf = [P.sb(f"lnbuf{i}", [128, D], BF16) for i in range(2)]

    def ln_tile(src_ap, xb, lb, store_ap, wkey, xsrc_key):
        P.dma("y", xb[:], src_ap, reads=[xsrc_key] if xsrc_key else [],
              writes=[(xb.name, 0)])
        for j in range(4):
            P.op("v", lambda e, j=j, xb=xb: e.bn_stats(
                stat[:, 6 * j:6 * j + 6], xb[:, 512 * j:512 * (j + 1)]),
                reads=[(xb.name, 0)], writes=[("stat", j)])
        P.op("v", lambda e: e.bn_aggr(mv[:], stat[:]),
             reads=[("stat", j) for j in range(4)], writes=[("mv", 0)])
        P.op("s", lambda e: e.activation(rstd[:], mv[:, 1:2], AF.Sqrt, bias=eps_sb[:]),
             reads=[("mv", 0), ("eps_sb", 0)], writes=[("rstd", 0)])
        P.op("v", lambda e: e.reciprocal(rstd[:], rstd[:]),
             reads=[("rstd", 0)], writes=[("rstd", 0)])
        P.op("v", lambda e: e.tensor_tensor(nmean[:], mv[:, 0:1], rstd[:], ALU.mult),
             reads=[("mv", 0), ("rstd", 0)], writes=[("nmean", 0)])
        P.op("v", lambda e: e.tensor_scalar_mul(nmean[:], nmean[:], -1.0),
             reads=[("nmean", 0)], writes=[("nmean", 0)])
        P.op("s", lambda e, xb=xb, lb=lb: e.activation(
            lb[:], xb[:], AF.Identity, bias=nmean[:], scale=rstd[:]),
            reads=[(xb.name, 0), ("rstd", 0), ("nmean", 0)], writes=[(lb.name, 0)])
        P.dma("s", store_ap, lb[:], reads=[(lb.name, 0)], writes=[wkey])

    for i in range(NC_OWN):
        ln_tile(xo[128 * i:128 * (i + 1), :], xbuf[i % 2], lnbuf[i % 2],
                ln1_dram[128 * i:128 * (i + 1), :], ("ln1_dram", i), None)
    for b in xbuf + lnbuf:
        P.free(b)

    ln_own = [("ln1_dram", i) for i in range(NC_OWN)]
    for j in range(ND):
        P.dma("y", ln1T[:, j, :],
              ln1_dram[:, 128 * j:128 * (j + 1)],
              transpose=True, reads=ln_own, writes=[("ln1T", (j, 0))])
    xpart = P.sb("xpart", [128, ND], F32, side="right")
    for j in range(ND):
        P.op("v", lambda e, j=j: e.reduce_sum(xbf[:], ln1T[:, j, :], axis=AX.X),
             reads=[("ln1T", (j, 0))], writes=[("xbf", 0)])
        P.op("v", lambda e, j=j: e.tensor_scalar(
            xpart[:, j:j + 1], xbf[:], SCALE / (2.0 * T_OWN), None, ALU.mult),
            reads=[("xbf", 0)], writes=[("xpart", j)])
    P.dma("y", cx_in.ap(), xpart[:],
          reads=[("xpart", j) for j in range(ND)], writes=[("cx_in", 0)])
    P.op("g", lambda e: e.collective_compute(
        "AllReduce", ALU.add, replica_groups=CGROUPS,
        ins=[cx_in.ap().opt()], outs=[cx_sum.ap().opt()]),
        reads=[("cx_in", 0)], writes=[("cx_sum", 0)], coll=True, dbg="cx")
    P.dma("y", xpart[:], cx_sum.ap(),
          reads=[("cx_sum", 0)], writes=[("xpart", j) for j in range(ND)])
    P.op("v", lambda e: e.tensor_copy(xbar[:], xpart[:]),
         reads=[("xpart", j) for j in range(ND)],
         writes=[("xbar", j) for j in range(ND)])

    HGW = 2
    NHG = H // HGW
    BW = HGW * 128

    wk_d = [P.sb("wk_b0", [128, ND, BW], BF16)]
    wv_d = [P.sb("wv_b0", [128, ND, BW], BF16)]

    def k_loads(hg):
        wk_b = wk_d[0]
        for j in range(ND):
            P.dma("g", wk_b[:, j, :],
                  wkT[128 * j:128 * (j + 1), BW * hg:BW * (hg + 1)],
                  writes=[(wk_b.name, j)])

    def v_loads(hg):
        wv_b = wv_d[0]
        for j in range(ND):
            P.dma("g", wv_b[:, j, :],
                  wvT[128 * j:128 * (j + 1), BW * hg:BW * (hg + 1)],
                  writes=[(wv_b.name, j)])
    kp_hg = P.sb("kp_hg", [128, NC_OWN, BW], BF16)
    kpT_g = [P.sb(f"kpT_g{i}", [128, NC_OWN, BW], BF16) for i in range(2)]
    v_g = [P.sb(f"v_g{i}", [128, NC_OWN, BW], BF16) for i in range(2)]
    spall_g = [P.sb(f"spall{i}", [128, NC_OWN, HGW, 132], BF16) for i in range(2)]
    exch_g = [P.sb(f"exch{i}", [128, 258], F32) for i in range(2)]
    recv_g = [P.sb(f"recv{i}", [128, 258], F32) for i in range(2)]
    sbase_g = [P.sb(f"sbase{i}", [128, HGW, 132], BF16) for i in range(2)]
    scm = P.sb("scm", [128, BW], BF16)
    den_sb = P.sb("den_sb", [128, HGW], F32)

    pk1 = P.ps("pk1", [128, BW])
    pk2 = P.ps("pk2", [128, BW])
    pk3 = P.ps("pk3", [128, BW])
    pkv = P.ps("pkv", [128, 272])
    psc = P.ps("psc", [128, BW])
    pnum = P.ps("pnum", [128, 272])
    pkt2 = [P.ps(f"pkt{i}", [128, 128], BF16) for i in range(2)]
    pk_rot = [pk1, pk2, pk3]

    def produce_hg(hg, mid=None):
        g3 = hg % 2
        kpT_hg = kpT_g[g3]
        v_hg = v_g[g3]
        spall = spall_g[g3]
        exch = exch_g[g3]

        def state_copies(tc):
            for hh in range(HGW):
                if tc > 0:
                    P.op("s", lambda e, tc=tc, hh=hh: e.copy(
                        spall[:, tc, hh, 0:128], pkv[:, 128 * hh:128 * (hh + 1)]),
                        reads=[("pkv", 0)], writes=[(spall.name, (tc, hh))])
                    P.op("s", lambda e, tc=tc, hh=hh: e.copy(
                        spall[:, tc, hh, 128:129], pkv[:, 256 + hh:257 + hh]),
                        reads=[("pkv", 0)], writes=[(spall.name, (tc, hh))])
                else:
                    P.op("s", lambda e, hh=hh: e.memzero(
                        spall[:, 0, hh, 0:130]),
                        writes=[(spall.name, (0, hh))])

        def state_mms(tc):
            for hh in range(HGW):
                P.op("t", lambda e, tc=tc, hh=hh: e.matmul(
                    pkv[:, 128 * hh:128 * (hh + 1)],
                    kp_hg[:, tc, 128 * hh:128 * (hh + 1)],
                    v_hg[:, tc, 128 * hh:128 * (hh + 1)],
                    start=(tc == 0 and hh == 0), stop=True,
                    skip_group_check=(tc != 0 or hh != 0)),
                    reads=[("kp_hg", tc), (v_hg.name, tc)],
                    writes=[("pkv", 0)], dbg=f"upd{hg}")
                P.op("t", lambda e, tc=tc, hh=hh: e.matmul(
                    pkv[:, 256 + hh:257 + hh],
                    kp_hg[:, tc, 128 * hh:128 * (hh + 1)],
                    ones_col[:, 0:1],
                    start=False, stop=True, skip_group_check=True),
                    reads=[("kp_hg", tc), ("ones_col", 0)],
                    writes=[("pkv", 0)], dbg=f"updz{hg}")
        wk_b = wk_d[0]
        wv_b = wv_d[0]
        for tc in range(NC_OWN):
            pk = pk_rot[tc % 3]
            for j in range(ND):
                P.op("t", lambda e, tc=tc, j=j, pk=pk, wk_b=wk_b: e.matmul(
                    pk[:, 0:BW], ln1T[:, j, 128 * tc:128 * (tc + 1)], wk_b[:, j, :],
                    start=(j == 0), stop=(j == ND - 1)),
                    reads=[("ln1T", (j, 0)), (wk_b.name, j)],
                    writes=[(pk.name, 0)], dbg=f"k{hg}t{tc}")
            emit_elu(pk[:, 0:BW], [(pk.name, 0)], kp_hg[:, tc, :], [("kp_hg", tc)], BW)
        if mid is not None:
            mid()
        for tc in range(NC_OWN):
            if tc > 0:
                state_copies(tc - 1)
            pk = pk_rot[tc % 3]
            for j in range(ND):
                P.op("t", lambda e, tc=tc, j=j, pk=pk, wv_b=wv_b: e.matmul(
                    pk[:, 0:BW], ln1T[:, j, 128 * tc:128 * (tc + 1)], wv_b[:, j, :],
                    start=(j == 0), stop=(j == ND - 1)),
                    reads=[("ln1T", (j, 0)), (wv_b.name, j)],
                    writes=[(pk.name, 0)], dbg=f"v{hg}t{tc}")
            P.op("s", lambda e, tc=tc, pk=pk, v_hg=v_hg: e.copy(v_hg[:, tc, :], pk[:, 0:BW]),
                 reads=[(pk.name, 0)], writes=[(v_hg.name, tc)])
            for hh in range(HGW):
                pkt = pkt2[hh]
                P.op("t", lambda e, tc=tc, hh=hh, pkt=pkt: e.transpose(
                    pkt[:, 0:128], kp_hg[:, tc, 128 * hh:128 * (hh + 1)],
                    ident_bf[:]),
                    reads=[("kp_hg", tc), ("ident_bf", 0)],
                    writes=[(pkt.name, 0)], dbg=f"kpt{hg}")
                P.op("s", lambda e, tc=tc, hh=hh, pkt=pkt, kpT_hg=kpT_hg: e.copy(
                    kpT_hg[:, tc, 128 * hh:128 * (hh + 1)], pkt[:, 0:128]),
                    reads=[(pkt.name, 0)], writes=[(kpT_hg.name, tc)])
            if tc > 0:
                state_mms(tc - 1)
            if tc == NC_OWN - 1:
                state_copies(tc)
                state_mms(tc)
        for hh in range(HGW):
            P.op("s", lambda e, hh=hh, exch=exch: e.copy(
                exch[:, 129 * hh:129 * hh + 128], pkv[:, 128 * hh:128 * (hh + 1)]),
                reads=[("pkv", 0)], writes=[(exch.name, hh)])
            P.op("s", lambda e, hh=hh, exch=exch: e.copy(
                exch[:, 129 * hh + 128:129 * hh + 129], pkv[:, 256 + hh:257 + hh]),
                reads=[("pkv", 0)], writes=[(exch.name, hh)])
        P.dma("y", cs_in.ap()[hg], exch[:, 0:258],
              reads=[(exch.name, hh) for hh in range(HGW)],
              writes=[("cs_in", hg)])
        P.op("g", lambda e, hg=hg: e.collective_compute(
            "AllReduce", ALU.add, replica_groups=CGROUPS,
            ins=[cs_in.ap()[hg].opt()], outs=[cs_sum.ap()[hg].opt()]),
            reads=[("cs_in", hg)], writes=[("cs_sum", hg)], coll=True,
            dbg=f"cs{hg}")

    def prep_hg(hg):
        g3 = hg % 2
        exch = exch_g[g3]
        recv = recv_g[g3]
        sbase = sbase_g[g3]
        P.dma("y", recv[:, 0:258], cs_sum.ap()[hg],
              reads=[("cs_sum", hg)], writes=[(recv.name, 0)])
        P.op("v", lambda e, recv=recv, exch=exch: e.tensor_tensor(
            recv[:, 0:258], recv[:, 0:258], exch[:, 0:258], ALU.subtract),
            reads=[(recv.name, 0)] + [(exch.name, hh) for hh in range(HGW)],
            writes=[(recv.name, 0)])
        for hh in range(HGW):
            P.op("s", lambda e, hh=hh, recv=recv, sbase=sbase: e.activation(
                sbase[:, hh, 0:129], recv[:, 129 * hh:129 * hh + 129],
                AF.Identity, scale=bmask_sb[:]),
                reads=[(recv.name, 0), ("bmask_sb", 0)],
                writes=[(sbase.name, hh)])

    def local_hg(hg):
        g3 = hg % 2
        kpT_hg = kpT_g[g3]
        v_hg = v_g[g3]
        spall = spall_g[g3]
        sbase = sbase_g[g3]
        for tc in range(NC_OWN):
            for hh in range(HGW):
                h = HGW * hg + hh
                P.op("t", lambda e, tc=tc, h=h, hh=hh, kpT_hg=kpT_hg: e.matmul(
                    psc[:, 128 * hh:128 * (hh + 1)],
                    kpT_hg[:, tc, 128 * hh:128 * (hh + 1)],
                    sigq[:, h, 128 * tc:128 * (tc + 1)], start=True, stop=True),
                    reads=[(kpT_hg.name, tc), ("sigq", h)],
                    writes=[("psc", hh)] + ([("psc", "qb")] if (hg == 0 and tc == 0 and hh == 0) else []),
                    dbg=f"sc{h}")
            P.op("v", lambda e: e.tensor_tensor(scm[:], psc[:, 0:BW], mask01[:], ALU.mult),
                 reads=[("psc", hh2) for hh2 in range(HGW)] + [("mask01", 0)],
                 writes=[("scm", 0)], dbg=f"mask{hg}t{tc}")
            nmm = [0]
            NTOT = HGW * 6

            def flg():
                nmm[0] += 1
                return dict(start=(nmm[0] == 1), stop=(nmm[0] == NTOT))
            for hh in range(HGW):
                h = HGW * hg + hh
                P.op("t", lambda e, tc=tc, h=h, hh=hh, spall=spall, f=flg(): e.matmul(
                    pnum[:, 128 * hh:128 * (hh + 1)],
                    sigq[:, h, 128 * tc:128 * (tc + 1)],
                    spall[:, tc, hh, 0:128], **f),
                    reads=[("sigq", h), (spall.name, (tc, hh))],
                    writes=[("pnum", 0)] + ([("pnum", "rel")] if (hg == 0 and tc == 0 and hh == 0) else []),
                    dbg=f"num{h}")
                P.op("t", lambda e, tc=tc, h=h, hh=hh, sbase=sbase, f=flg(): e.matmul(
                    pnum[:, 128 * hh:128 * (hh + 1)],
                    sigq[:, h, 128 * tc:128 * (tc + 1)],
                    sbase[:, hh, 0:128], **f),
                    reads=[("sigq", h), (sbase.name, hh)],
                    writes=[("pnum", 0)], dbg=f"numb{h}")
                P.op("t", lambda e, tc=tc, h=h, hh=hh, spall=spall, f=flg(): e.matmul(
                    pnum[:, 256 + hh:257 + hh],
                    sigq[:, h, 128 * tc:128 * (tc + 1)],
                    spall[:, tc, hh, 128:129], **f),
                    reads=[("sigq", h), (spall.name, (tc, hh))],
                    writes=[("pnum", 0)], dbg=f"den{h}")
                P.op("t", lambda e, tc=tc, h=h, hh=hh, sbase=sbase, f=flg(): e.matmul(
                    pnum[:, 256 + hh:257 + hh],
                    sigq[:, h, 128 * tc:128 * (tc + 1)],
                    sbase[:, hh, 128:129], **f),
                    reads=[("sigq", h), (sbase.name, hh)],
                    writes=[("pnum", 0)], dbg=f"denb{h}")
            for hh in range(HGW):
                h = HGW * hg + hh
                P.op("t", lambda e, tc=tc, hh=hh, v_hg=v_hg, f=flg(): e.matmul(
                    pnum[:, 128 * hh:128 * (hh + 1)],
                    scm[:, 128 * hh:128 * (hh + 1)],
                    v_hg[:, tc, 128 * hh:128 * (hh + 1)], **f),
                    reads=[("scm", 0), (v_hg.name, tc)],
                    writes=[("pnum", 0)], dbg=f"numi{h}")
                P.op("t", lambda e, tc=tc, hh=hh, f=flg(): e.matmul(
                    pnum[:, 256 + hh:257 + hh],
                    scm[:, 128 * hh:128 * (hh + 1)],
                    ones_col[:, 0:1], **f),
                    reads=[("scm", 0), ("ones_col", 0)],
                    writes=[("pnum", 0)], dbg=f"deni{h}")
            P.op("v", lambda e: e.tensor_scalar_max(
                den_sb[:], pnum[:, 256:256 + HGW], ATT_EPS),
                reads=[("pnum", 0)], writes=[("den_sb", 0)])
            P.op("v", lambda e: e.reciprocal(den_sb[:], den_sb[:]),
                 reads=[("den_sb", 0)], writes=[("den_sb", 0)])
            P.op("v", lambda e, hg=hg: e.tensor_tensor(
                den_sb[:], den_sb[:],
                csb[:, 64 + HGW * hg:64 + HGW * (hg + 1)], ALU.mult),
                reads=[("den_sb", 0), ("csb", 0)], writes=[("den_sb", 0)])
            P.op("v", lambda e, tc=tc, hg=hg: e.tensor_tensor(
                attn_fin[:, tc, 256 * hg:256 * (hg + 1)]
                .rearrange("p (q e) -> p q e", e=128),
                pnum[:, 0:256].rearrange("p (q e) -> p q e", e=128),
                den_sb[:, 0:HGW].rearrange("p q -> p q ()")
                .to_broadcast([128, HGW, 128]), ALU.mult),
                reads=[("pnum", 0), ("den_sb", 0)],
                writes=[("attn_fin", (tc, HGW * hg)),
                        ("attn_fin", (tc, HGW * hg + 1))], dbg=f"loc{hg}")

    k_loads(0)
    v_loads(0)
    produce_hg(0, mid=lambda: k_loads(1))
    v_loads(1)
    produce_hg(1)

    wq_q2 = [P.sb(f"wq_q{i}", [128, ND, 256], BF16) for i in range(2)]
    qctr = [0]
    for j in range(ND):
        P.dma("g", wq_q2[0][:, j, :], wqT[128 * j:128 * (j + 1), 0:256],
              writes=[(wq_q2[0].name, j)])
    for qq in range(8):
        wq_q = wq_q2[qq % 2]
        if qq + 1 < 8:
            wn = wq_q2[(qq + 1) % 2]
            for j in range(ND):
                P.dma("g", wn[:, j, :],
                      wqT[128 * j:128 * (j + 1), 256 * (qq + 1):256 * (qq + 2)],
                      writes=[(wn.name, j)])
        for hh in range(2):
            h = 2 * qq + hh
            for t2 in range(2):
                pk = pk_rot[qctr[0] % 3]
                qctr[0] += 1
                for j in range(ND):
                    P.op("t", lambda e, h=h, hh=hh, j=j, t2=t2, pk=pk, wq_q=wq_q: e.matmul(
                        pk[:, 0:512],
                        wq_q[:, j, 128 * hh:128 * (hh + 1)],
                        ln1T[:, j, 512 * t2:512 * (t2 + 1)],
                        start=(j == 0), stop=(j == ND - 1)),
                        reads=[(wq_q.name, j), ("ln1T", (j, 0))],
                        writes=[(pk.name, 0)], dbg=f"q{h}")
                emit_elu(pk[:, 0:512], [(pk.name, 0)],
                         sigq[:, h, 512 * t2:512 * (t2 + 1)], [("sigq", h)], 512)
            for j in range(ND):
                P.op("t", lambda e, h=h, hh=hh, j=j, wq_q=wq_q: e.matmul(
                    psc[0:128, h:h + 1], wq_q[:, j, 128 * hh:128 * (hh + 1)],
                    xbar[:, j:j + 1], start=(j == 0), stop=(j == ND - 1)),
                    reads=[(wq_q.name, j), ("xbar", j)],
                    writes=[("psc", "qb")])
    P.op("s", lambda e: e.copy(qbar_sb[:], psc[:, 0:16]),
         reads=[("psc", "qb")], writes=[("qbar_sb", 0)])
    P.free(wq_q2[1]); P.free(wq_q2[0])

    for h in range(H):
        P.op("t", lambda e, h=h: e.matmul(
            pnum[0:1, 4 * h:4 * h + 4], qbar_sb[:, h:h + 1],
            mn_sb[:, 4 * h:4 * h + 4], start=True, stop=True),
            reads=[("qbar_sb", 0), ("mn_sb", 0)], writes=[("pnum", "rel")])
    P.op("t", lambda e: e.matmul(pnum[0:1, 64:128], ones_col[:, 0:1], mn_sb[:],
                                 start=True, stop=True),
         reads=[("ones_col", 0), ("mn_sb", 0)], writes=[("pnum", "rel")])

    P.op("v", lambda e: e.tensor_reduce(
        sc1[0:1, 112:116], pnum[0:1, 64:128].rearrange("p (m h) -> p m h", h=16),
        axis=AX.X, op=ALU.add),
        reads=[("pnum", "rel")], writes=[("sc1", "sm")])
    P.op("v", lambda e: e.tensor_scalar(
        sc1[0:1, 121:125], sc1[0:1, 112:116], 1e-6, None, ALU.is_lt),
        reads=[("sc1", "sm")], writes=[("sc1", "empty")])
    P.op("v", lambda e: e.scalar_tensor_tensor(
        sc1[0:1, 0:64].rearrange("p (h m) -> p h m", m=4),
        sc1[0:1, 121:125].rearrange("p m -> p () m").to_broadcast([1, 16, 4]),
        -1e30, pnum[0:1, 0:64].rearrange("p (h m) -> p h m", m=4),
        ALU.mult, ALU.add),
        reads=[("sc1", "empty"), ("pnum", "rel")], writes=[("sc1", "rel")])
    P.op("v", lambda e: e.tensor_reduce(
        sc1[0:1, 64:80], sc1[0:1, 0:64].rearrange("p (h m) -> p h m", m=4),
        axis=AX.X, op=ALU.max),
        reads=[("sc1", "rel")], writes=[("sc1", "hmax")])
    P.op("v", lambda e: e.tensor_tensor(
        sc1[0:1, 0:64].rearrange("p (h m) -> p h m", m=4),
        sc1[0:1, 0:64].rearrange("p (h m) -> p h m", m=4),
        sc1[0:1, 64:80].rearrange("p h -> p h ()").to_broadcast([1, 16, 4]),
        ALU.subtract),
        reads=[("sc1", "rel"), ("sc1", "hmax")], writes=[("sc1", "rel")])
    P.op("s", lambda e: e.activation(sc1[0:1, 0:64], sc1[0:1, 0:64], AF.Exp),
         reads=[("sc1", "rel")], writes=[("sc1", "rel")])
    P.op("v", lambda e: e.tensor_reduce(
        sc1[0:1, 80:96], sc1[0:1, 0:64].rearrange("p (h m) -> p h m", m=4),
        axis=AX.X, op=ALU.add),
        reads=[("sc1", "rel")], writes=[("sc1", "hsum")])
    P.op("v", lambda e: e.reciprocal(sc1[0:1, 80:96], sc1[0:1, 80:96]),
         reads=[("sc1", "hsum")], writes=[("sc1", "hsum")])
    P.op("v", lambda e: e.tensor_tensor(
        sc1[0:1, 0:64].rearrange("p (h m) -> p h m", m=4),
        sc1[0:1, 0:64].rearrange("p (h m) -> p h m", m=4),
        sc1[0:1, 80:96].rearrange("p h -> p h ()").to_broadcast([1, 16, 4]),
        ALU.mult),
        reads=[("sc1", "rel"), ("sc1", "hsum")], writes=[("sc1", "rel")])
    P.dma("y", sc1[0:1, 96:112], gate1, writes=[("sc1", "g")])
    P.op("s", lambda e: e.activation(sc1[0:1, 96:112], sc1[0:1, 96:112], AF.Sigmoid),
         reads=[("sc1", "g")], writes=[("sc1", "g")])
    P.op("v", lambda e: e.tensor_tensor(
        sc1[0:1, 120:121], sc1[0:1, 121:122], sc1[0:1, 122:123], ALU.mult),
        reads=[("sc1", "empty")], writes=[("sc1", "nae")])
    P.op("v", lambda e: e.tensor_tensor(
        sc1[0:1, 120:121], sc1[0:1, 120:121], sc1[0:1, 123:124], ALU.mult),
        reads=[("sc1", "empty"), ("sc1", "nae")], writes=[("sc1", "nae")])
    P.op("v", lambda e: e.tensor_tensor(
        sc1[0:1, 120:121], sc1[0:1, 120:121], sc1[0:1, 124:125], ALU.mult),
        reads=[("sc1", "empty"), ("sc1", "nae")], writes=[("sc1", "nae")])
    P.op("v", lambda e: e.tensor_scalar(
        sc1[0:1, 120:121], sc1[0:1, 120:121], -1.0, 1.0, ALU.mult, ALU.add),
        reads=[("sc1", "nae")], writes=[("sc1", "nae")])
    P.op("v", lambda e: e.tensor_scalar(
        sc1[0:1, 116:120], sc1[0:1, 121:125], -1.0, 1.0, ALU.mult, ALU.add),
        reads=[("sc1", "empty")], writes=[("sc1", "nem")])
    P.op("v", lambda e: e.tensor_tensor(
        sc1[0:1, 128:192].rearrange("p (h m) -> p h m", m=4),
        sc1[0:1, 0:64].rearrange("p (h m) -> p h m", m=4),
        sc1[0:1, 96:112].rearrange("p h -> p h ()").to_broadcast([1, 16, 4]),
        ALU.mult),
        reads=[("sc1", "rel"), ("sc1", "g")], writes=[("sc1", "wgg")])
    P.op("v", lambda e: e.tensor_tensor(
        sc1[0:1, 128:192].rearrange("p (h m) -> p h m", m=4),
        sc1[0:1, 128:192].rearrange("p (h m) -> p h m", m=4),
        sc1[0:1, 116:120].rearrange("p m -> p () m").to_broadcast([1, 16, 4]),
        ALU.mult),
        reads=[("sc1", "wgg"), ("sc1", "nem")], writes=[("sc1", "wgg")])
    P.op("v", lambda e: e.tensor_tensor(
        sc1[0:1, 128:192], sc1[0:1, 128:192],
        sc1[0:1, 120:121].to_broadcast([1, 64]), ALU.mult),
        reads=[("sc1", "wgg"), ("sc1", "nae")], writes=[("sc1", "wgg")])
    P.op("v", lambda e: e.tensor_scalar(
        sc1[0:1, 192:208], sc1[0:1, 96:112], -1.0, 1.0, ALU.mult, ALU.add),
        reads=[("sc1", "g")], writes=[("sc1", "gloc")])
    P.op("v", lambda e: e.tensor_copy(sc1b[0:1, 0:64], sc1[0:1, 128:192]),
         reads=[("sc1", "wgg")], writes=[("sc1b", 0)])
    P.op("v", lambda e: e.tensor_copy(sc1b[0:1, 64:80], sc1[0:1, 192:208]),
         reads=[("sc1", "gloc")], writes=[("sc1b", 0)])
    P.op("t", lambda e: e.matmul(psc[:, 0:80], ones_row[0:1, :], sc1b[0:1, :],
                                 start=True, stop=True),
         reads=[("ones_row", 0), ("sc1b", 0)], writes=[("psc", "qb")])
    P.op("s", lambda e: e.copy(csb[:], psc[:, 0:80]),
         reads=[("psc", "qb")], writes=[("csb", 0)])

    prep_hg(0)
    k_loads(2)
    v_loads(2)
    local_hg(0)
    for hg in range(2, NHG):
        if hg + 1 < NHG:
            produce_hg(hg, mid=lambda hgm=hg - 1, hgn=hg + 1:
                       (prep_hg(hgm), k_loads(hgn)))
            v_loads(hg + 1)
        else:
            produce_hg(hg, mid=lambda hgm=hg - 1: prep_hg(hgm))
        local_hg(hg - 1)
    prep_hg(NHG - 1)
    local_hg(NHG - 1)

    P.free(pkt2[1]); P.free(pkt2[0])
    P.free(pnum); P.free(psc); P.free(pkv)
    P.free(pk3); P.free(pk2); P.free(pk1)
    P.free(den_sb); P.free(scm)
    for b in reversed(sbase_g + recv_g + exch_g + spall_g + v_g + kpT_g):
        P.free(b)
    P.free(kp_hg)
    P.free(wv_d[0]); P.free(wk_d[0])
    P.free(elu_e2[1]); P.free(elu_e2[0]); P.free(elu_t2[1]); P.free(elu_t2[0])
    P.free(ln1T)

    mem_sb = P.sb("mem_sb", [128, M * H * 128], BF16)
    P.dma("g", mem_sb[:], memT, writes=[("mem_sb", 0)])
    pnall = P.ps("pnall", [128, 512])
    nmm2 = [0]
    for tc in range(NC_OWN):
        for h in range(H):
            nmm2[0] += 1
            P.op("t", lambda e, tc=tc, h=h,
                 f=dict(start=(nmm2[0] == 1), stop=(nmm2[0] == 128)): e.matmul(
                pnall[:, 64 * tc + 4 * h:64 * tc + 4 * (h + 1)],
                sigq[:, h, 128 * tc:128 * (tc + 1)],
                mn_sb[:, 4 * h:4 * (h + 1)], **f),
                reads=[("sigq", h), ("mn_sb", 0)],
                writes=[("pnall", 0)], dbg="nall")
    csn_all = P.sb("csn_all", [128, 512], F32)
    P.op("v", lambda e: e.tensor_scalar_max(csn_all[:], pnall[:, 0:512], ATT_EPS),
         reads=[("pnall", 0)], writes=[("csn_all", 0)])
    P.op("v", lambda e: e.reciprocal(csn_all[:], csn_all[:]),
         reads=[("csn_all", 0)], writes=[("csn_all", 0)])
    P.op("v", lambda e: e.tensor_tensor(
        csn_all[:].rearrange("p (tc q) -> p tc q", q=64),
        csn_all[:].rearrange("p (tc q) -> p tc q", q=64),
        csb[:, 0:64].rearrange("p q -> p () q").to_broadcast([128, 8, 64]),
        ALU.mult),
        reads=[("csn_all", 0), ("csb", 0)], writes=[("csn_all", 0)])
    P.free(pnall)

    dg = P.sb("dg", [128, 3, 8, 128], BF16)
    am_sb = P.sb("am_sb", [128, 2, 2, 4, 128], BF16)
    pam2 = [P.ps(f"pamb{i}", [128, 1024]) for i in range(2)]
    pmon2 = [P.ps(f"pmon{i}", [128, 264]) for i in range(2)]
    pat2 = [P.ps(f"pat{i}", [128, 128], BF16) for i in range(2)]
    pairs = [(tc, pp) for tc in range(NC_OWN) for pp in range(8)]

    def dg_emit(i):
        tc, pp = pairs[i]
        P.op("g", lambda e, tc=tc, pp=pp, st=i % 3: e.affine_select(
            out=dg[:, st, :, :],
            in_=csn_all[:, 64 * tc + 8 * pp:64 * tc + 8 * (pp + 1)]
                .rearrange("p q -> p q ()").to_broadcast([128, 8, 128]),
            compare_op=ALU.is_equal, fill=0.0,
            base=0, pattern=[[0, 8], [1, 128]], channel_multiplier=-1),
            reads=[("csn_all", 0)], writes=[("dg", i % 3)], dbg=f"dg{i}")

    def mem_front(i):
        tc, pp = pairs[i]
        st_i = i % 2
        pa = pam2[st_i]
        for hh in range(2):
            h = 2 * pp + hh
            for m in range(M):
                P.op("t", lambda e, tc=tc, h=h, hh=hh, m=m, pa=pa: e.matmul(
                    pa[:, 512 * hh + 128 * m:512 * hh + 128 * (m + 1)],
                    sigq[:, h, 128 * tc:128 * (tc + 1)],
                    mem_sb[:, 128 * (m * H + h):128 * (m * H + h + 1)],
                    start=(m == 0), stop=(m == M - 1)),
                    reads=[("sigq", h), ("mem_sb", 0)],
                    writes=[(pa.name, 0)], dbg=f"am{h}m{m}")
        P.op("s", lambda e, st_i=st_i, pa=pa: e.copy(
            am_sb[:, st_i, :, :, :], pa[:, 0:1024]),
            reads=[(pa.name, 0)], writes=[("am_sb", st_i)])

    def mem_back1(i):
        tc, pp = pairs[i]
        st_i = i % 2
        dst = i % 3
        pmon = pmon2[st_i]
        for hh in range(2):
            h = 2 * pp + hh
            for m in range(M):
                P.op("t", lambda e, st_i=st_i, dst=dst, hh=hh, m=m, pmon=pmon: e.matmul(
                    pmon[:, 128 * hh:128 * (hh + 1)],
                    dg[:, dst, 4 * hh + m, :],
                    am_sb[:, st_i, hh, m, :],
                    start=(m == 0), stop=(m == M - 1)),
                    reads=[("dg", dst), ("am_sb", st_i)],
                    writes=[(pmon.name, 0)], dbg=f"dmm{h}")
        P.op("v", lambda e, tc=tc, pp=pp, pmon=pmon: e.tensor_tensor(
            attn_fin[:, tc, 256 * pp:256 * (pp + 1)],
            pmon[:, 0:256],
            attn_fin[:, tc, 256 * pp:256 * (pp + 1)], ALU.add),
            reads=[(pmon.name, 0), ("attn_fin", (tc, 2 * pp)),
                   ("attn_fin", (tc, 2 * pp + 1))],
            writes=[("attn_fin", (tc, 2 * pp)), ("attn_fin", (tc, 2 * pp + 1))],
            dbg=f"mfin{pp}")

    def mem_back2(i):
        tc, pp = pairs[i]
        for hh in range(2):
            h = 2 * pp + hh
            pat = pat2[hh]
            P.op("t", lambda e, tc=tc, h=h, pat=pat: e.transpose(
                pat[:, 0:128], attn_fin[:, tc, 128 * h:128 * (h + 1)],
                ident_bf[:]),
                reads=[("attn_fin", (tc, h)), ("ident_bf", 0)],
                writes=[(pat.name, 0)], dbg=f"aT{h}")
            P.op("s", lambda e, tc=tc, h=h, pat=pat: e.copy(
                aT[:, h, 128 * tc:128 * (tc + 1)], pat[:, 0:128]),
                reads=[(pat.name, 0)], writes=[("aT", (h, tc))], dbg=f"aTc{h}")

    NP_ = len(pairs)
    dg_emit(0)
    dg_emit(1)
    mem_front(0)
    mem_front(1)
    mem_back1(0)
    for i in range(2, NP_):
        mem_front(i)
        dg_emit(i)
        mem_back1(i - 1)
        mem_back2(i - 2)
    mem_back1(NP_ - 1)
    mem_back2(NP_ - 2)
    mem_back2(NP_ - 1)

    P.free(pat2[1]); P.free(pat2[0])
    P.free(pmon2[1]); P.free(pmon2[0])
    P.free(pam2[1]); P.free(pam2[0])
    P.free(am_sb); P.free(dg)
    P.free(csn_all)
    P.free(attn_fin); P.free(mem_sb)
    P.free(sigq)

    x1 = P.sb("x1", [128, NC_OWN, D], F32)
    wo_b = [P.sb(f"wo_b{i}", [128, ND, 1024], BF16) for i in range(2)]
    xre = [P.sb(f"xre{i}", [128, 2048], F32) for i in range(2)]
    ln2buf = [P.sb(f"ln2buf{i}", [128, D], BF16) for i in range(2)]
    pw1 = P.ps("pw1", [128, 2048])
    pw2 = P.ps("pw2", [128, 2048])
    pw_rot = [pw1, pw2]
    for half in range(2):
        wob = wo_b[half]
        for j in range(ND):
            P.dma("g", wob[:, j, :],
                  woT[128 * j:128 * (j + 1), 1024 * half:1024 * (half + 1)],
                  writes=[(wob.name, j)])
    for tc in range(NC_OWN):
        pw = pw_rot[tc % 2]
        xr_b = xre[tc % 2]
        P.dma("y", xr_b[:], xo[128 * tc:128 * (tc + 1), :],
              writes=[(xr_b.name, 0)])
        for half in range(2):
            wob = wo_b[half]
            for j in range(ND):
                for n in range(2):
                    P.op("t", lambda e, tc=tc, j=j, n=n, half=half, pw=pw, wob=wob: e.matmul(
                        pw[:, 1024 * half + 512 * n:1024 * half + 512 * (n + 1)],
                        aT[:, j, 128 * tc:128 * (tc + 1)],
                        wob[:, j, 512 * n:512 * (n + 1)],
                        start=(j == 0), stop=(j == ND - 1)),
                        reads=[("aT", (j, tc)), (wob.name, j)],
                        writes=[(pw.name, 2 * half + n)], dbg=f"wo{tc}")
        P.op("v", lambda e, tc=tc, pw=pw, xr_b=xr_b: e.tensor_tensor(
            x1[:, tc, :], pw[:], xr_b[:], ALU.add),
            reads=[(pw.name, i) for i in range(4)] + [(xr_b.name, 0)],
            writes=[("x1", tc)], dbg=f"res1_{tc}")
        lb = ln2buf[tc % 2]
        for j in range(4):
            P.op("v", lambda e, j=j, tc=tc: e.bn_stats(
                stat[:, 6 * j:6 * j + 6], x1[:, tc, 512 * j:512 * (j + 1)]),
                reads=[("x1", tc)], writes=[("stat", j)])
        P.op("v", lambda e: e.bn_aggr(mv[:], stat[:]),
             reads=[("stat", j) for j in range(4)], writes=[("mv", 0)])
        P.op("s", lambda e: e.activation(rstd[:], mv[:, 1:2], AF.Sqrt, bias=eps_sb[:]),
             reads=[("mv", 0), ("eps_sb", 0)], writes=[("rstd", 0)])
        P.op("v", lambda e: e.reciprocal(rstd[:], rstd[:]),
             reads=[("rstd", 0)], writes=[("rstd", 0)])
        P.op("v", lambda e: e.tensor_tensor(nmean[:], mv[:, 0:1], rstd[:], ALU.mult),
             reads=[("mv", 0), ("rstd", 0)], writes=[("nmean", 0)])
        P.op("v", lambda e: e.tensor_scalar_mul(nmean[:], nmean[:], -1.0),
             reads=[("nmean", 0)], writes=[("nmean", 0)])
        P.op("s", lambda e, tc=tc, lb=lb: e.activation(
            lb[:], x1[:, tc, :], AF.Identity, bias=nmean[:], scale=rstd[:]),
            reads=[("x1", tc), ("rstd", 0), ("nmean", 0)], writes=[(lb.name, 0)])
        P.dma("s", ln2_dram[128 * tc:128 * (tc + 1), :], lb[:],
              reads=[(lb.name, 0)], writes=[("ln2_dram", tc)])
        P.dma("y", x1_dram[128 * tc:128 * (tc + 1), :], x1[:, tc, :],
              reads=[("x1", tc)], writes=[("x1_dram", tc)])
    P.free(pw2); P.free(pw1)
    P.free(ln2buf[1]); P.free(ln2buf[0])
    P.free(xre[1]); P.free(xre[0]); P.free(wo_b[1]); P.free(wo_b[0])
    P.free(aT)
    P.free(x1)

    sh_basis = P.sb("sh_basis", [128, 16, T_OWN], BF16)
    uT = P.sb("uT", [128, 8, T_OWN], BF16, is_transpose_dest=True)
    c_sb = P.sb("c_sb", [128, NC_OWN, 8], F32)
    ln2T = P.sb("ln2T", [128, ND, T_OWN], BF16, is_transpose_dest=True)
    l2_all = [("ln2_dram", tc) for tc in range(NC_OWN)]
    for j in range(ND):
        P.dma("y", ln2T[:, j, :],
              ln2_dram[:, 128 * j:128 * (j + 1)],
              transpose=True, reads=l2_all, writes=[("ln2T", j)])

    wr_sb = P.sb("wr_sb", [128, ND, E_EXP], BF16)
    for j in range(ND):
        P.dma("g", wr_sb[:, j, :], wrt[128 * j:128 * (j + 1), :],
              writes=[("wr_sb", j)])
    amps_sb = P.sb("amps_sb", [128, 16], F32)
    esc_sb = P.sb("esc_sb", [128, 1], F32)
    mix_sb = P.sb("mix_sb", [128, 8], BF16)
    mixf = P.sb("mixf", [128, 9], F32)
    P.dma("y", amps_sb[:], amps, writes=[("amps_sb", 0)])
    P.dma("y", esc_sb[:], escale, writes=[("esc_sb", 0)])
    P.op("s", lambda e: e.activation(amps_sb[:], amps_sb[:], AF.Square),
         reads=[("amps_sb", 0)], writes=[("amps_sb", 0)])
    P.op("v", lambda e: e.tensor_tensor(
        mixf[:, 0:8],
        amps_sb[:].rearrange("p (k two) -> p k two", two=2)[:, :, 0],
        amps_sb[:].rearrange("p (k two) -> p k two", two=2)[:, :, 1],
        ALU.add),
        reads=[("amps_sb", 0)], writes=[("mixf", 0)])
    P.op("v", lambda e: e.reduce_sum(mixf[:, 8:9], mixf[:, 0:8], axis=AX.X),
         reads=[("mixf", 0)], writes=[("mixf", 1)])
    P.op("v", lambda e: e.tensor_scalar_add(mixf[:, 8:9], mixf[:, 8:9], FFN_EPS),
         reads=[("mixf", 1)], writes=[("mixf", 1)])
    P.op("v", lambda e: e.reciprocal(mixf[:, 8:9], mixf[:, 8:9]),
         reads=[("mixf", 1)], writes=[("mixf", 1)])
    P.op("v", lambda e: e.tensor_tensor(
        mixf[:, 8:9], mixf[:, 8:9], esc_sb[:], ALU.mult),
        reads=[("mixf", 1), ("esc_sb", 0)], writes=[("mixf", 1)])
    P.op("v", lambda e: e.tensor_scalar(
        mix_sb[:], mixf[:, 0:8], mixf[:, 8:9], None, ALU.mult),
        reads=[("mixf", 0), ("mixf", 1)], writes=[("mix_sb", 0)])

    probs2 = [P.sb(f"probs{i}", [128, 128], F32) for i in range(2)]
    rt12 = [P.sb(f"rt1_{i}", [128, 8], F32) for i in range(2)]
    rtv2 = [P.sb(f"rtv{i}", [128, 3], F32) for i in range(2)]
    wmask2 = [P.sb(f"wmask{i}", [128, 128], BF16) for i in range(2)]
    prt2 = [P.ps(f"prt{i}", [128, 128]) for i in range(3)]
    pc2 = [P.ps(f"pc{i}", [128, 8]) for i in range(2)]
    pwt2 = [P.ps(f"pwt{i}", [128, 128], BF16) for i in range(2)]

    def rt_front(tc):
        prt = prt2[tc % 3]
        probs = probs2[tc % 2]
        rt1 = rt12[tc % 2]
        rtv = rtv2[tc % 2]
        wmask = wmask2[tc % 2]
        for j in range(ND):
            P.op("t", lambda e, tc=tc, j=j, prt=prt: e.matmul(
                prt[:, 0:128], ln2T[:, j, 128 * tc:128 * (tc + 1)], wr_sb[:, j, :],
                start=(j == 0), stop=(j == ND - 1)),
                reads=[("ln2T", j), ("wr_sb", j)], writes=[(prt.name, 0)],
                dbg=f"rt{tc}")
        P.op("v", lambda e, prt=prt, rtv=rtv: e.reduce_max(
            rtv[:, 0:1], prt[:, 0:128], axis=AX.X, negate=True),
            reads=[(prt.name, 0)], writes=[(rtv.name, 0)])
        P.op("s", lambda e, prt=prt, probs=probs, rtv=rtv: e.activation(
            probs[:], prt[:, 0:128], AF.Exp, bias=rtv[:, 0:1], accum_out=rtv[:, 1:2]),
            reads=[(prt.name, 0), (rtv.name, 0)],
            writes=[(probs.name, 0), (rtv.name, 1)])
        P.op("v", lambda e, probs=probs, rt1=rt1: e.max(out=rt1[:], in_=probs[:]),
             reads=[(probs.name, 0)], writes=[(rt1.name, 0)])
        P.op("v", lambda e, probs=probs, rt1=rt1, wmask=wmask: e.scalar_tensor_tensor(
            wmask[:], probs[:], rt1[:, 3:4], probs[:], ALU.is_ge, ALU.mult),
            reads=[(probs.name, 0), (rt1.name, 0)], writes=[(wmask.name, 0)])
        P.op("v", lambda e, wmask=wmask, rtv=rtv: e.reduce_sum(
            rtv[:, 2:3], wmask[:], axis=AX.X),
            reads=[(wmask.name, 0)], writes=[(rtv.name, 2)])
        P.op("v", lambda e, rtv=rtv: e.scalar_tensor_tensor(
            rtv[:, 2:3], rtv[:, 1:2], FFN_EPS, rtv[:, 2:3], ALU.mult, ALU.add),
            reads=[(rtv.name, 1), (rtv.name, 2)], writes=[(rtv.name, 2)])
        P.op("v", lambda e, rtv=rtv: e.reciprocal(rtv[:, 2:3], rtv[:, 2:3]),
             reads=[(rtv.name, 2)], writes=[(rtv.name, 2)])

    def rt_back(tc):
        rtv = rtv2[tc % 2]
        wmask = wmask2[tc % 2]
        pwt = pwt2[tc % 2]
        pc = pc2[tc % 2]
        P.op("t", lambda e, pwt=pwt, wmask=wmask: e.transpose(
            pwt[:, 0:128], wmask[:], ident_bf[:]),
            reads=[(wmask.name, 0), ("ident_bf", 0)], writes=[(pwt.name, 0)])
        P.op("s", lambda e, pwt=pwt, wmask=wmask: e.copy(wmask[:], pwt[:, 0:128]),
             reads=[(pwt.name, 0)], writes=[(wmask.name, 0)])
        P.op("t", lambda e, pc=pc, wmask=wmask: e.matmul(
            pc[:, 0:8], wmask[:], mix_sb[:], start=True, stop=True),
            reads=[(wmask.name, 0), ("mix_sb", 0)], writes=[(pc.name, 0)])
        P.op("v", lambda e, tc=tc, pc=pc, rtv=rtv: e.tensor_scalar(
            c_sb[:, tc, :], pc[:, 0:8], rtv[:, 2:3], None, ALU.mult),
            reads=[(pc.name, 0), (rtv.name, 2)], writes=[("c_sb", tc)])

    rt_front(0)
    for tc in range(1, NC_OWN):
        rt_front(tc)
        rt_back(tc - 1)
    rt_back(NC_OWN - 1)
    P.free(pwt2[1]); P.free(pwt2[0]); P.free(pc2[1]); P.free(pc2[0])
    P.free(prt2[2]); P.free(prt2[1]); P.free(prt2[0])
    P.free(wmask2[1]); P.free(wmask2[0]); P.free(rtv2[1]); P.free(rtv2[0])
    P.free(rt12[1]); P.free(rt12[0]); P.free(probs2[1]); P.free(probs2[0])
    P.free(mixf); P.free(mix_sb); P.free(esc_sb); P.free(amps_sb)
    P.free(wr_sb)

    u_acc = P.sb("u_acc", [128, NC_OWN, I_DIM], F32)
    wgq = [P.sb(f"wgq{i}", [128, ND, 512], BF16) for i in range(2)]
    wuq = [P.sb(f"wuq{i}", [128, ND, 512], BF16) for i in range(2)]
    silu_g = P.sb("silu_g", [128, NC_OWN, 512], BF16)
    ut2 = [P.sb(f"ut{i}", [128, 512], F32) for i in range(2)]
    pg_rot = [P.ps(f"pg{i}", [128, 512]) for i in range(3)]
    pu_rot = [P.ps(f"pu{i}", [128, 512]) for i in range(3)]

    def ffn_loads(q):
        wgb = wgq[q % 2]
        wub = wuq[q % 2]
        for j in range(ND):
            P.dma("g", wgb[:, j, :],
                  wg[128 * j:128 * (j + 1), 512 * q:512 * (q + 1)],
                  writes=[(wgb.name, j)])
            P.dma("g", wub[:, j, :],
                  wu[128 * j:128 * (j + 1), 512 * q:512 * (q + 1)],
                  writes=[(wub.name, j)])

    ffn_loads(0)
    ffn_loads(1)
    for q in range(16):
        k_idx = q // 2
        ihalf = q % 2
        wgb = wgq[q % 2]
        wub = wuq[q % 2]
        for tc in range(NC_OWN):
            pg = pg_rot[tc % 3]
            pu = pu_rot[tc % 3]
            ut = ut2[tc % 2]
            for j in range(ND):
                P.op("t", lambda e, tc=tc, j=j, pg=pg, wgb=wgb: e.matmul(
                    pg[:], ln2T[:, j, 128 * tc:128 * (tc + 1)], wgb[:, j, :],
                    start=(j == 0), stop=(j == ND - 1)),
                    reads=[("ln2T", j), (wgb.name, j)],
                    writes=[(pg.name, 0)], dbg=f"g{q}t{tc}")
            P.op("s", lambda e, tc=tc, pg=pg: e.activation(
                silu_g[:, tc, :], pg[:], AF.Silu),
                reads=[(pg.name, 0)], writes=[("silu_g", tc)])
            for j in range(ND):
                P.op("t", lambda e, tc=tc, j=j, pu=pu, wub=wub: e.matmul(
                    pu[:], ln2T[:, j, 128 * tc:128 * (tc + 1)], wub[:, j, :],
                    start=(j == 0), stop=(j == ND - 1)),
                    reads=[("ln2T", j), (wub.name, j)],
                    writes=[(pu.name, 0)], dbg=f"u{q}t{tc}")
            P.op("v", lambda e, tc=tc, k_idx=k_idx, pu=pu, ut=ut: e.scalar_tensor_tensor(
                ut[:], pu[:], c_sb[:, tc, k_idx:k_idx + 1], silu_g[:, tc, :],
                ALU.mult, ALU.mult),
                reads=[(pu.name, 0), ("c_sb", tc), ("silu_g", tc)],
                writes=[(ut.name, 0)], dbg=f"ut{q}t{tc}")
            if k_idx == 0:
                P.op("v", lambda e, tc=tc, ihalf=ihalf, ut=ut: e.tensor_copy(
                    u_acc[:, tc, 512 * ihalf:512 * (ihalf + 1)], ut[:]),
                    reads=[(ut.name, 0)], writes=[("u_acc", (tc, ihalf))])
            else:
                P.op("v", lambda e, tc=tc, ihalf=ihalf, ut=ut: e.tensor_tensor(
                    u_acc[:, tc, 512 * ihalf:512 * (ihalf + 1)],
                    u_acc[:, tc, 512 * ihalf:512 * (ihalf + 1)], ut[:], ALU.add),
                    reads=[(ut.name, 0), ("u_acc", (tc, ihalf))],
                    writes=[("u_acc", (tc, ihalf))])
        if q + 2 < 16:
            ffn_loads(q + 2)
    for tc in range(NC_OWN):
        P.dma("g", u_dram[128 * tc:128 * (tc + 1), :], u_acc[:, tc, :],
              reads=[("u_acc", (tc, 0)), ("u_acc", (tc, 1))],
              writes=[("u_dram", tc)])
    for p in reversed(pu_rot + pg_rot):
        P.free(p)
    P.free(ut2[1]); P.free(ut2[0]); P.free(silu_g)
    P.free(wuq[1]); P.free(wuq[0]); P.free(wgq[1]); P.free(wgq[0])
    P.free(u_acc)

    u_all = [("u_dram", tc) for tc in range(NC_OWN)]
    for j in range(8):
        P.dma("y", uT[:, j, :],
              u_dram[:, 128 * j:128 * (j + 1)],
              transpose=True, reads=u_all, writes=[("uT", j)])

    swgq = [P.sb(f"swgq{i}", [128, ND, 512], BF16) for i in range(2)]
    swuq = [P.sb(f"swuq{i}", [128, ND, 512], BF16) for i in range(2)]
    ssilu = P.sb("ssilu", [128, T_OWN], BF16)
    psh = [P.ps(f"psh{i}", [128, 1024]) for i in range(4)]

    def sh_loads(q):
        sgb = swgq[q % 2]
        sub = swuq[q % 2]
        for j in range(ND):
            P.dma("g", sgb[:, j, :],
                  swg[128 * j:128 * (j + 1), 512 * q:512 * (q + 1)],
                  writes=[(sgb.name, j)])
            P.dma("g", sub[:, j, :],
                  swu[128 * j:128 * (j + 1), 512 * q:512 * (q + 1)],
                  writes=[(sub.name, j)])

    sh_loads(0)
    sh_loads(1)
    for q in range(4):
        sgb = swgq[q % 2]
        sub = swuq[q % 2]
        for fc in range(4):
            of = 4 * q + fc
            pga = psh[2 * (fc % 2)]
            pua = psh[2 * (fc % 2) + 1]
            for j in range(ND):
                for n in range(2):
                    P.op("t", lambda e, j=j, n=n, fc=fc, pga=pga, sgb=sgb: e.matmul(
                        pga[:, 512 * n:512 * (n + 1)],
                        sgb[:, j, 128 * fc:128 * (fc + 1)],
                        ln2T[:, j, 512 * n:512 * (n + 1)],
                        start=(j == 0), stop=(j == ND - 1)),
                        reads=[(sgb.name, j), ("ln2T", j)],
                        writes=[(pga.name, n)], dbg=f"sg{of}")
            P.op("s", lambda e, pga=pga: e.activation(ssilu[:], pga[:], AF.Silu),
                 reads=[(pga.name, 0), (pga.name, 1)], writes=[("ssilu", 0)])
            for j in range(ND):
                for n in range(2):
                    P.op("t", lambda e, j=j, n=n, fc=fc, pua=pua, sub=sub: e.matmul(
                        pua[:, 512 * n:512 * (n + 1)],
                        sub[:, j, 128 * fc:128 * (fc + 1)],
                        ln2T[:, j, 512 * n:512 * (n + 1)],
                        start=(j == 0), stop=(j == ND - 1)),
                        reads=[(sub.name, j), ("ln2T", j)],
                        writes=[(pua.name, n)], dbg=f"su{of}")
            P.op("v", lambda e, of=of, pua=pua: e.tensor_tensor(
                sh_basis[:, of, :], pua[:], ssilu[:], ALU.mult),
                reads=[(pua.name, 0), (pua.name, 1), ("ssilu", 0)],
                writes=[("sh_basis", of)], dbg=f"sb{of}")
        if q + 2 < 4:
            sh_loads(q + 2)
    for p in reversed(psh):
        P.free(p)
    P.free(ssilu)
    P.free(swuq[1]); P.free(swuq[0]); P.free(swgq[1]); P.free(swgq[0])
    P.free(ln2T)

    wd_sb = P.sb("wd_sb", [128, 8, D], BF16)
    for j in range(8):
        P.dma("g", wd_sb[:, j, :], wd[128 * j:128 * (j + 1), :],
              writes=[("wd_sb", j)])
    swd_sb = P.sb("swd_sb", [128, 16, D], BF16)
    for j in range(16):
        P.dma("g", swd_sb[:, j, :], swd[128 * j:128 * (j + 1), :],
              writes=[("swd_sb", j)])
    ybuf = [P.sb(f"ybuf{i}", [128, D], F32) for i in range(2)]
    x1b = [P.sb(f"x1b{i}", [128, D], F32) for i in range(2)]
    pf = [P.ps(f"pf{i}", [128, 2048]) for i in range(2)]

    for tc in range(NC_OWN):
        pfb = pf[tc % 2]
        yb = ybuf[tc % 2]
        xb = x1b[tc % 2]
        P.dma("y", xb[:], x1_dram[128 * tc:128 * (tc + 1), :],
              reads=[("x1_dram", tc)], writes=[(xb.name, 0)])
        for j in range(8):
            for n in range(4):
                P.op("t", lambda e, tc=tc, j=j, n=n, pfb=pfb: e.matmul(
                    pfb[:, 512 * n:512 * (n + 1)],
                    uT[:, j, 128 * tc:128 * (tc + 1)],
                    wd_sb[:, j, 512 * n:512 * (n + 1)],
                    start=(j == 0), stop=False),
                    reads=[("uT", j), ("wd_sb", j)],
                    writes=[(pfb.name, n)], dbg=f"dn{tc}")
        for j in range(16):
            for n in range(4):
                P.op("t", lambda e, tc=tc, j=j, n=n, pfb=pfb: e.matmul(
                    pfb[:, 512 * n:512 * (n + 1)],
                    sh_basis[:, j, 128 * tc:128 * (tc + 1)],
                    swd_sb[:, j, 512 * n:512 * (n + 1)],
                    start=False, stop=(j == 15)),
                    reads=[("sh_basis", j), ("swd_sb", j)],
                    writes=[(pfb.name, n)], dbg=f"sd{tc}")
        P.op("v", lambda e, pfb=pfb, yb=yb, xb=xb: e.tensor_tensor(
            yb[:], pfb[:], xb[:], ALU.add),
            reads=[(pfb.name, i) for i in range(4)] + [(xb.name, 0)],
            writes=[(yb.name, 0)], dbg=f"res2_{tc}")
        P.dma("y", yout[128 * tc:128 * (tc + 1), :], yb[:],
              reads=[(yb.name, 0)], writes=[("y", tc)])

    P.emit(final_outputs=[("y", tc) for tc in range(NC_OWN)])
    return nc


_NC_CACHE = [None]


def kernel(L0_attn_hidden_states, L0_attn_memories, L0_attn_memory_norms,
           L0_attn_w_q, L0_attn_w_k, L0_attn_w_v, L0_attn_w_o, L0_attn_gate,
           L0_ffn_w_router, L0_ffn_w_gate, L0_ffn_w_up, L0_ffn_w_down,
           L0_ffn_amplitudes, L0_ffn_expert_scale, L0_ffn_shared_w_gate,
           L0_ffn_shared_w_up, L0_ffn_shared_w_down):
    ct = np.ascontiguousarray
    f32 = np.float32
    x = ct(np.asarray(L0_attn_hidden_states, f32))
    shared = dict(
        wqT=ct(np.asarray(L0_attn_w_q, f32).T),
        wkT=ct(np.asarray(L0_attn_w_k, f32).T),
        wvT=ct(np.asarray(L0_attn_w_v, f32).T),
        woT=ct(np.asarray(L0_attn_w_o, f32).T),
        memT=ct(np.asarray(L0_attn_memories, f32).transpose(2, 0, 1, 3)
                .reshape(128, M * H * 128)),
        mnT=ct(np.asarray(L0_attn_memory_norms, f32).transpose(2, 1, 0)
               .reshape(128, H * M)),
        gate1=ct(np.asarray(L0_attn_gate, f32).reshape(1, H)),
        wrt=ct(np.asarray(L0_ffn_w_router, f32)),
        wg=ct(np.asarray(L0_ffn_w_gate, f32)),
        wu=ct(np.asarray(L0_ffn_w_up, f32)),
        wd=ct(np.asarray(L0_ffn_w_down, f32)),
        amps=ct(np.asarray(L0_ffn_amplitudes, f32).reshape(128, 16)),
        escale=ct(np.asarray(L0_ffn_expert_scale, f32).reshape(128, 1)),
        swg=ct(np.asarray(L0_ffn_shared_w_gate, f32)),
        swu=ct(np.asarray(L0_ffn_shared_w_up, f32)),
        swd=ct(np.asarray(L0_ffn_shared_w_down, f32)),
    )
    if _NC_CACHE[0] is None:
        _NC_CACHE[0] = build_nc()
    nc = _NC_CACHE[0]

    in_maps = []
    for core in range(NCORES):
        seq, half = divmod(core, 2)
        m = dict(shared)
        m["xo"] = ct(x[seq, 1024 * half:1024 * (half + 1), :])
        m["bmask"] = np.full((128, 1), 1.0 if half == 1 else 0.0, f32)
        in_maps.append(m)

    res = run_bass_kernel_spmd(nc, in_maps, core_ids=list(range(NCORES)))
    out = np.empty((B, S, D), f32)
    for core in range(NCORES):
        seq, half = divmod(core, 2)
        out[seq, 1024 * half:1024 * (half + 1), :] = res.results[core]["y"]
    return out


# version 24
# speedup vs baseline: 1.3536x; 1.0222x over previous
import numpy as np
import concourse.bass as bass
import concourse.mybir as mybir
from concourse.bass_utils import run_bass_kernel_spmd

F32 = mybir.dt.float32
BF16 = mybir.dt.bfloat16
AF = mybir.ActivationFunctionType
ALU = mybir.AluOpType
AX = mybir.AxisListType

B, S, D = 4, 2048, 2048
H, HD = 16, 128
M = 4
NCORES = 8
T_OWN = 1024
NC_OWN = 8
NC_ALL = 16
ND = 16
SCALE = HD ** -0.5
CGROUPS = [[0, 1], [2, 3], [4, 5], [6, 7]]
LN_EPS = 1e-5
ATT_EPS = 1e-6
FFN_EPS = 1e-8
KI = 8192
I_DIM = 1024
E_EXP = 128


class Rec:
    __slots__ = ("eng", "fn", "is_dma", "is_coll", "deps", "marked", "val", "dbg")

    def __init__(self, eng, fn, is_dma, dbg, is_coll=False):
        self.eng = eng
        self.fn = fn
        self.is_dma = is_dma
        self.is_coll = is_coll
        self.deps = []
        self.marked = is_dma or is_coll
        self.val = None
        self.dbg = dbg


class Prog:
    CENG = ("t", "v", "s", "g", "y")

    def __init__(self, nc):
        self.nc = nc
        self.streams = {e: [] for e in self.CENG}
        self.regions = {}
        self.free_barrier = []
        self.birth = {}
        self.stacks = {"L": [], "R": [], "P": []}

    def sb(self, name, shape, dtype, side="left", **kw):
        g = self.nc.sbuf_tensor(name, shape, dtype, side=side, **kw)
        h = g.__enter__()
        self.stacks["R" if side == "right" else "L"].append([name, g, False])
        self.birth[name] = list(self.free_barrier)
        return h

    def ps(self, name, shape, dtype=F32):
        bankw = 512 if dtype == F32 else 1024
        shape = list(shape)
        shape[-1] = ((shape[-1] + bankw - 1) // bankw) * bankw
        g = self.nc.psum_tensor(name, shape, dtype)
        h = g.__enter__()
        self.stacks["P"].append([name, g, False])
        self.birth[name] = list(self.free_barrier)
        return h

    def free(self, h):
        name = h.name
        for stack in self.stacks.values():
            for ent in stack:
                if ent[0] == name:
                    ent[2] = True
                    for (n, k), st in list(self.regions.items()):
                        if n == name:
                            if st[0] is not None:
                                self.free_barrier.append(st[0])
                            self.free_barrier.extend(st[1])
                            del self.regions[(n, k)]
                    while stack and stack[-1][2]:
                        _, g, _ = stack.pop()
                        g.__exit__(None, None, None)
                    return
        raise KeyError(name)

    def _region(self, tensor_name, key):
        st = self.regions.get((tensor_name, key))
        if st is None:
            st = [None, []]
            self.regions[(tensor_name, key)] = st
        return st

    def op(self, eng, fn, reads=(), writes=(), dma=False, dbg="", coll=False):
        rec = Rec(eng, fn, dma, dbg, is_coll=coll)
        deps = []
        for t, k in reads:
            st = self._region(t, k)
            if st[0] is not None:
                deps.append(st[0])
        for t, k in writes:
            st = self._region(t, k)
            if st[0] is not None:
                deps.append(st[0])
            deps.extend(st[1])
        for t, k in writes:
            if self.regions.get((t, k), [None])[0] is None:
                bd = self.birth.get(t)
                if bd:
                    deps.extend(bd)
        seen = set()
        for d in deps:
            if id(d) in seen or d is rec:
                continue
            seen.add(id(d))
            if (d.eng == "t" and eng == "t" and not d.is_dma and not dma):
                continue
            rec.deps.append(d)
            d.marked = True
        for t, k in reads:
            self._region(t, k)[1].append(rec)
        for t, k in writes:
            st = self._region(t, k)
            st[0] = rec
            st[1] = []
        self.streams[eng].append(rec)
        return rec

    def dma(self, q, out, in_, reads=(), writes=(), transpose=False, dbg=""):
        def fn(e):
            return e.dma_start(out=out, in_=in_, transpose=transpose)
        return self.op(q, fn, reads, writes, dma=True, dbg=dbg)

    def emit(self, final_outputs=()):
        nc = self.nc
        sink_deps = []
        for t, k in final_outputs:
            st = self.regions.get((t, k))
            if st and st[0] is not None:
                sink_deps.append(st[0])
                st[0].marked = True

        NSLOT = 8
        NCOLL = 4
        csem = {e: nc.alloc_semaphore(name=f"prog_{e}") for e in self.CENG}
        dsem = {(e, s): nc.alloc_semaphore(name=f"dma_{e}{s}")
                for e in ("y", "g", "s") for s in range(NSLOT)}
        for s in range(NCOLL):
            dsem[("o", s)] = nc.alloc_semaphore(name=f"coll{s}")
        cnt = {e: 0 for e in self.CENG}
        dcnt = {e: 0 for e in ("y", "g", "s")}
        ocnt = [0]
        for e in self.CENG:
            for rec in self.streams[e]:
                if not rec.marked:
                    continue
                if rec.is_coll:
                    slot = ocnt[0] % NCOLL
                    use = ocnt[0] // NCOLL + 1
                    ocnt[0] += 1
                    rec.val = ("d", ("o", slot), use)
                elif rec.is_dma:
                    slot = dcnt[e] % NSLOT
                    use = dcnt[e] // NSLOT + 1
                    dcnt[e] += 1
                    rec.val = ("d", (e, slot), 16 * use)
                else:
                    cnt[e] += 1
                    rec.val = ("c", e, cnt[e])

        hiwater = {e: {} for e in self.CENG}

        def get_sem(kind, de):
            return dsem[de] if kind == "d" else csem[de]

        def do_wait(engobj, hw, kind, de, v):
            key = (kind, de)
            if hw.get(key, 0) >= v:
                return
            engobj.wait_ge(get_sem(kind, de), v)
            hw[key] = v

        def emit_stream(e, engobj):
            hw = hiwater[e]
            for rec in self.streams[e]:
                for d in rec.deps:
                    kind, de, v = d.val
                    do_wait(engobj, hw, kind, de, v)
                if rec.marked and (rec.is_dma or rec.is_coll):
                    kind, de, v = rec.val
                    step = 1 if rec.is_coll else 16
                    if v > step:
                        do_wait(engobj, hw, kind, de, v - step)
                ins = rec.fn(engobj)
                if rec.marked:
                    kind, de, v = rec.val
                    amt = 16 if (kind == "d" and not rec.is_coll) else 1
                    ins.then_inc(get_sem(kind, de), amt)
            if e == "g":
                for d in sink_deps:
                    kind, de, v = d.val
                    do_wait(engobj, hw, kind, de, v)

        with nc.Block() as block:
            @block.tensor
            def _(t):
                emit_stream("t", t)

            @block.vector
            def _(v):
                emit_stream("v", v)

            @block.scalar
            def _(s):
                emit_stream("s", s)

            @block.gpsimd
            def _(g):
                emit_stream("g", g)

            @block.sync
            def _(y):
                emit_stream("y", y)


def build_nc():
    nc = bass.Bass("TRN2", target_bir_lowering=False)
    P = Prog(nc)

    def din(name, shape):
        return nc.dram_tensor(name, shape, F32, kind="ExternalInput").ap()

    xo = din("xo", [T_OWN, D])
    bmask = din("bmask", [128, 1])
    wqT = din("wqT", [D, D])
    wkT = din("wkT", [D, D])
    wvT = din("wvT", [D, D])
    woT = din("woT", [D, D])
    memT = din("memT", [128, M * H * 128])
    mnT = din("mnT", [128, H * M])
    gate1 = din("gate1", [1, H])
    wrt = din("wrt", [D, E_EXP])
    wg = din("wg", [D, KI])
    wu = din("wu", [D, KI])
    wd = din("wd", [I_DIM, D])
    amps = din("amps", [128, 16])
    escale = din("escale", [128, 1])
    swg = din("swg", [D, 2 * I_DIM])
    swu = din("swu", [D, 2 * I_DIM])
    swd = din("swd", [2 * I_DIM, D])

    yout = nc.dram_tensor("y", [T_OWN, D], F32, kind="ExternalOutput").ap()
    ln1_dram = nc.dram_tensor("ln1_dram", [T_OWN, D], BF16, kind="Internal").ap()
    cx_in = nc.dram_tensor("cx_in", [128, 16], F32, kind="Internal")
    cx_sum = nc.dram_tensor("cx_sum", [128, 16], F32, kind="Internal")
    cs_in = nc.dram_tensor("cs_in", [8, 128, 258], F32, kind="Internal")
    cs_sum = nc.dram_tensor("cs_sum", [8, 128, 258], F32, kind="Internal")
    attn_dram = nc.dram_tensor("attn_dram", [T_OWN, D], BF16, kind="Internal").ap()
    ln2_dram = nc.dram_tensor("ln2_dram", [T_OWN, D], BF16, kind="Internal").ap()
    u_dram = nc.dram_tensor("u_dram", [T_OWN, I_DIM], BF16, kind="Internal").ap()
    x1_dram = nc.dram_tensor("x1_dram", [T_OWN, D], F32, kind="Internal").ap()

    R = dict(side="right")
    ones_col = P.sb("ones_col", [128, 1], BF16, **R)
    ones_row = P.sb("ones_row", [1, 128], BF16, **R)
    ident_bf = P.sb("ident_bf", [128, 128], BF16, **R)
    mask01 = P.sb("mask01", [128, 256], BF16, **R)
    bmask_sb = P.sb("bmask_sb", [128, 1], F32, **R)
    stat = P.sb("stat", [128, 24], F32, **R)
    mv = P.sb("mv", [128, 2], F32, **R)
    rstd = P.sb("rstd", [128, 1], F32, **R)
    nmean = P.sb("nmean", [128, 1], F32, **R)
    xbar = P.sb("xbar", [128, ND], BF16, **R)
    xbf = P.sb("xbf", [128, 1], F32, **R)
    qbar_sb = P.sb("qbar_sb", [128, 16], BF16, **R)
    mn_sb = P.sb("mn_sb", [128, H * M], BF16, **R)
    sc1 = P.sb("sc1", [1, 256], F32, **R)
    sc1b = P.sb("sc1b", [1, 80], BF16, **R)
    eps_sb = P.sb("eps_sb", [128, 1], F32, **R)
    csb = P.sb("csb", [128, 80], F32, **R)

    P.op("g", lambda e: e.memset(ones_col[:], 1.0), writes=[("ones_col", 0)])
    P.op("g", lambda e: e.memset(ones_row[:], 1.0), writes=[("ones_row", 0)])
    P.op("g", lambda e: e.memset(ident_bf[:], 0.0), writes=[("ident_bf", 0)])
    P.op("g", lambda e: e.affine_select(
        out=ident_bf[:], in_=ident_bf[:], compare_op=ALU.not_equal, fill=1.0,
        base=0, pattern=[[-1, 128]], channel_multiplier=1),
        reads=[("ident_bf", 0)], writes=[("ident_bf", 0)])
    P.op("g", lambda e: e.memset(mask01[:], 1.0), writes=[("mask01", 0)])
    P.op("g", lambda e: e.affine_select(
        out=mask01[:].rearrange("p (q i) -> p q i", i=128),
        in_=mask01[:].rearrange("p (q i) -> p q i", i=128),
        compare_op=ALU.is_ge, fill=0.0,
        base=0, pattern=[[0, 2], [1, 128]], channel_multiplier=-1),
        reads=[("mask01", 0)], writes=[("mask01", 0)])
    P.dma("y", bmask_sb[:], bmask, writes=[("bmask_sb", 0)])
    P.op("g", lambda e: e.memset(eps_sb[:], LN_EPS), writes=[("eps_sb", 0)])
    P.dma("g", mn_sb[:], mnT, writes=[("mn_sb", 0)])

    aT = P.sb("aT", [128, ND, T_OWN], BF16)
    sigq = P.sb("sigq", [128, H, T_OWN], BF16)
    attn_fin = P.sb("attn_fin", [128, NC_OWN, D], BF16)
    ln1T = P.sb("ln1T", [128, ND, T_OWN], BF16, is_transpose_dest=True)
    elu_t2 = [P.sb(f"elu_t{i}", [128, 512], BF16) for i in range(2)]
    elu_e2 = [P.sb(f"elu_e{i}", [128, 512], BF16) for i in range(2)]
    elu_ctr = [0]

    def emit_elu(src_psum, src_regs, dst_ap, dst_regs, n):
        i = elu_ctr[0] % 2
        elu_ctr[0] += 1
        elu_t = elu_t2[i]
        elu_e = elu_e2[i]
        P.op("v", lambda e, elu_t=elu_t: e.tensor_scalar_min(elu_t[:, :n], src_psum, 0.0),
             reads=src_regs, writes=[(elu_t.name, 0)])
        P.op("s", lambda e, elu_t=elu_t, elu_e=elu_e: e.activation(elu_e[:, :n], elu_t[:, :n], AF.Exp),
             reads=[(elu_t.name, 0)], writes=[(elu_e.name, 0)])
        P.op("v", lambda e, elu_e=elu_e: e.scalar_tensor_tensor(
            dst_ap, src_psum, 1.0, elu_e[:, :n], ALU.add, ALU.max),
            reads=list(src_regs) + [(elu_e.name, 0)], writes=dst_regs)


    xbuf = [P.sb(f"xbuf{i}", [128, D], F32) for i in range(2)]
    lnbuf = [P.sb(f"lnbuf{i}", [128, D], BF16) for i in range(2)]

    def ln_tile(src_ap, xb, lb, store_ap, wkey, xsrc_key):
        P.dma("y", xb[:], src_ap, reads=[xsrc_key] if xsrc_key else [],
              writes=[(xb.name, 0)])
        for j in range(4):
            P.op("v", lambda e, j=j, xb=xb: e.bn_stats(
                stat[:, 6 * j:6 * j + 6], xb[:, 512 * j:512 * (j + 1)]),
                reads=[(xb.name, 0)], writes=[("stat", j)])
        P.op("v", lambda e: e.bn_aggr(mv[:], stat[:]),
             reads=[("stat", j) for j in range(4)], writes=[("mv", 0)])
        P.op("s", lambda e: e.activation(rstd[:], mv[:, 1:2], AF.Sqrt, bias=eps_sb[:]),
             reads=[("mv", 0), ("eps_sb", 0)], writes=[("rstd", 0)])
        P.op("v", lambda e: e.reciprocal(rstd[:], rstd[:]),
             reads=[("rstd", 0)], writes=[("rstd", 0)])
        P.op("v", lambda e: e.tensor_tensor(nmean[:], mv[:, 0:1], rstd[:], ALU.mult),
             reads=[("mv", 0), ("rstd", 0)], writes=[("nmean", 0)])
        P.op("v", lambda e: e.tensor_scalar_mul(nmean[:], nmean[:], -1.0),
             reads=[("nmean", 0)], writes=[("nmean", 0)])
        P.op("s", lambda e, xb=xb, lb=lb: e.activation(
            lb[:], xb[:], AF.Identity, bias=nmean[:], scale=rstd[:]),
            reads=[(xb.name, 0), ("rstd", 0), ("nmean", 0)], writes=[(lb.name, 0)])
        P.dma("s", store_ap, lb[:], reads=[(lb.name, 0)], writes=[wkey])

    for i in range(NC_OWN):
        ln_tile(xo[128 * i:128 * (i + 1), :], xbuf[i % 2], lnbuf[i % 2],
                ln1_dram[128 * i:128 * (i + 1), :], ("ln1_dram", i), None)
    for b in xbuf + lnbuf:
        P.free(b)

    ln_own = [("ln1_dram", i) for i in range(NC_OWN)]
    for j in range(ND):
        P.dma("y", ln1T[:, j, :],
              ln1_dram[:, 128 * j:128 * (j + 1)],
              transpose=True, reads=ln_own, writes=[("ln1T", (j, 0))])
    xpart = P.sb("xpart", [128, ND], F32, side="right")
    for j in range(ND):
        P.op("v", lambda e, j=j: e.reduce_sum(xbf[:], ln1T[:, j, :], axis=AX.X),
             reads=[("ln1T", (j, 0))], writes=[("xbf", 0)])
        P.op("v", lambda e, j=j: e.tensor_scalar(
            xpart[:, j:j + 1], xbf[:], SCALE / (2.0 * T_OWN), None, ALU.mult),
            reads=[("xbf", 0)], writes=[("xpart", j)])
    P.dma("y", cx_in.ap(), xpart[:],
          reads=[("xpart", j) for j in range(ND)], writes=[("cx_in", 0)])
    P.op("g", lambda e: e.collective_compute(
        "AllReduce", ALU.add, replica_groups=CGROUPS,
        ins=[cx_in.ap().opt()], outs=[cx_sum.ap().opt()]),
        reads=[("cx_in", 0)], writes=[("cx_sum", 0)], coll=True, dbg="cx")
    P.dma("y", xpart[:], cx_sum.ap(),
          reads=[("cx_sum", 0)], writes=[("xpart", j) for j in range(ND)])
    P.op("v", lambda e: e.tensor_copy(xbar[:], xpart[:]),
         reads=[("xpart", j) for j in range(ND)],
         writes=[("xbar", j) for j in range(ND)])

    HGW = 2
    NHG = H // HGW
    BW = HGW * 128

    wk_d = [P.sb("wk_b0", [128, ND, BW], BF16)]
    wv_d = [P.sb("wv_b0", [128, ND, BW], BF16)]

    def k_loads(hg):
        wk_b = wk_d[0]
        for j in range(ND):
            P.dma("g", wk_b[:, j, :],
                  wkT[128 * j:128 * (j + 1), BW * hg:BW * (hg + 1)],
                  writes=[(wk_b.name, j)])

    def v_loads(hg):
        wv_b = wv_d[0]
        for j in range(ND):
            P.dma("g", wv_b[:, j, :],
                  wvT[128 * j:128 * (j + 1), BW * hg:BW * (hg + 1)],
                  writes=[(wv_b.name, j)])
    kp_hg = P.sb("kp_hg", [128, NC_OWN, BW], BF16)
    kpT_g = [P.sb(f"kpT_g{i}", [128, NC_OWN, BW], BF16) for i in range(2)]
    v_g = [P.sb(f"v_g{i}", [128, NC_OWN, BW], BF16) for i in range(2)]
    spall_g = [P.sb(f"spall{i}", [128, NC_OWN, HGW, 132], BF16) for i in range(2)]
    exch_g = [P.sb(f"exch{i}", [128, 258], F32) for i in range(2)]
    recv_g = [P.sb(f"recv{i}", [128, 258], F32) for i in range(2)]
    sbase_g = [P.sb(f"sbase{i}", [128, HGW, 132], BF16) for i in range(2)]
    scm = P.sb("scm", [128, BW], BF16)
    den_sb = P.sb("den_sb", [128, HGW], F32)

    pk1 = P.ps("pk1", [128, BW])
    pk2 = P.ps("pk2", [128, BW])
    pkv = P.ps("pkv", [128, 272])
    psc = P.ps("psc", [128, BW])
    pnd2 = [P.ps("pnum0", [128, 272]), P.ps("pnum1", [128, 272])]
    pnum = pnd2[0]
    pkt2 = [P.ps(f"pkt{i}", [128, 128], BF16) for i in range(2)]
    pk_rot = [pk1, pk2]

    def produce_hg(hg, mid=None):
        g3 = hg % 2
        kpT_hg = kpT_g[g3]
        v_hg = v_g[g3]
        spall = spall_g[g3]
        exch = exch_g[g3]

        def state_copies(tc):
            for hh in range(HGW):
                if tc > 0:
                    P.op("s", lambda e, tc=tc, hh=hh: e.copy(
                        spall[:, tc, hh, 0:128], pkv[:, 128 * hh:128 * (hh + 1)]),
                        reads=[("pkv", 0)], writes=[(spall.name, (tc, hh))])
                    P.op("s", lambda e, tc=tc, hh=hh: e.copy(
                        spall[:, tc, hh, 128:129], pkv[:, 256 + hh:257 + hh]),
                        reads=[("pkv", 0)], writes=[(spall.name, (tc, hh))])
                else:
                    P.op("s", lambda e, hh=hh: e.memzero(
                        spall[:, 0, hh, 0:130]),
                        writes=[(spall.name, (0, hh))])

        def state_mms(tc):
            for hh in range(HGW):
                P.op("t", lambda e, tc=tc, hh=hh: e.matmul(
                    pkv[:, 128 * hh:128 * (hh + 1)],
                    kp_hg[:, tc, 128 * hh:128 * (hh + 1)],
                    v_hg[:, tc, 128 * hh:128 * (hh + 1)],
                    start=(tc == 0 and hh == 0), stop=True,
                    skip_group_check=(tc != 0 or hh != 0)),
                    reads=[("kp_hg", tc), (v_hg.name, tc)],
                    writes=[("pkv", 0)], dbg=f"upd{hg}")
                P.op("t", lambda e, tc=tc, hh=hh: e.matmul(
                    pkv[:, 256 + hh:257 + hh],
                    kp_hg[:, tc, 128 * hh:128 * (hh + 1)],
                    ones_col[:, 0:1],
                    start=False, stop=True, skip_group_check=True),
                    reads=[("kp_hg", tc), ("ones_col", 0)],
                    writes=[("pkv", 0)], dbg=f"updz{hg}")
        wk_b = wk_d[0]
        wv_b = wv_d[0]
        for tc in range(NC_OWN):
            pk = pk_rot[tc % 2]
            for j in range(ND):
                P.op("t", lambda e, tc=tc, j=j, pk=pk, wk_b=wk_b: e.matmul(
                    pk[:, 0:BW], ln1T[:, j, 128 * tc:128 * (tc + 1)], wk_b[:, j, :],
                    start=(j == 0), stop=(j == ND - 1)),
                    reads=[("ln1T", (j, 0)), (wk_b.name, j)],
                    writes=[(pk.name, 0)], dbg=f"k{hg}t{tc}")
            emit_elu(pk[:, 0:BW], [(pk.name, 0)], kp_hg[:, tc, :], [("kp_hg", tc)], BW)
        if mid is not None:
            mid()
        for tc in range(NC_OWN):
            if tc > 0:
                state_copies(tc - 1)
            pk = pk_rot[tc % 2]
            for j in range(ND):
                P.op("t", lambda e, tc=tc, j=j, pk=pk, wv_b=wv_b: e.matmul(
                    pk[:, 0:BW], ln1T[:, j, 128 * tc:128 * (tc + 1)], wv_b[:, j, :],
                    start=(j == 0), stop=(j == ND - 1)),
                    reads=[("ln1T", (j, 0)), (wv_b.name, j)],
                    writes=[(pk.name, 0)], dbg=f"v{hg}t{tc}")
            P.op("s", lambda e, tc=tc, pk=pk, v_hg=v_hg: e.copy(v_hg[:, tc, :], pk[:, 0:BW]),
                 reads=[(pk.name, 0)], writes=[(v_hg.name, tc)])
            for hh in range(HGW):
                pkt = pkt2[hh]
                P.op("t", lambda e, tc=tc, hh=hh, pkt=pkt: e.transpose(
                    pkt[:, 0:128], kp_hg[:, tc, 128 * hh:128 * (hh + 1)],
                    ident_bf[:]),
                    reads=[("kp_hg", tc), ("ident_bf", 0)],
                    writes=[(pkt.name, 0)], dbg=f"kpt{hg}")
                P.op("s", lambda e, tc=tc, hh=hh, pkt=pkt, kpT_hg=kpT_hg: e.copy(
                    kpT_hg[:, tc, 128 * hh:128 * (hh + 1)], pkt[:, 0:128]),
                    reads=[(pkt.name, 0)], writes=[(kpT_hg.name, tc)])
            if tc > 0:
                state_mms(tc - 1)
            if tc == NC_OWN - 1:
                state_copies(tc)
                state_mms(tc)
        for hh in range(HGW):
            P.op("s", lambda e, hh=hh, exch=exch: e.copy(
                exch[:, 129 * hh:129 * hh + 128], pkv[:, 128 * hh:128 * (hh + 1)]),
                reads=[("pkv", 0)], writes=[(exch.name, hh)])
            P.op("s", lambda e, hh=hh, exch=exch: e.copy(
                exch[:, 129 * hh + 128:129 * hh + 129], pkv[:, 256 + hh:257 + hh]),
                reads=[("pkv", 0)], writes=[(exch.name, hh)])
        P.dma("y", cs_in.ap()[hg], exch[:, 0:258],
              reads=[(exch.name, hh) for hh in range(HGW)],
              writes=[("cs_in", hg)])
        P.op("g", lambda e, hg=hg: e.collective_compute(
            "AllReduce", ALU.add, replica_groups=CGROUPS,
            ins=[cs_in.ap()[hg].opt()], outs=[cs_sum.ap()[hg].opt()]),
            reads=[("cs_in", hg)], writes=[("cs_sum", hg)], coll=True,
            dbg=f"cs{hg}")

    def prep_hg(hg):
        g3 = hg % 2
        exch = exch_g[g3]
        recv = recv_g[g3]
        sbase = sbase_g[g3]
        P.dma("y", recv[:, 0:258], cs_sum.ap()[hg],
              reads=[("cs_sum", hg)], writes=[(recv.name, 0)])
        P.op("v", lambda e, recv=recv, exch=exch: e.tensor_tensor(
            recv[:, 0:258], recv[:, 0:258], exch[:, 0:258], ALU.subtract),
            reads=[(recv.name, 0)] + [(exch.name, hh) for hh in range(HGW)],
            writes=[(recv.name, 0)])
        for hh in range(HGW):
            P.op("s", lambda e, hh=hh, recv=recv, sbase=sbase: e.activation(
                sbase[:, hh, 0:129], recv[:, 129 * hh:129 * hh + 129],
                AF.Identity, scale=bmask_sb[:]),
                reads=[(recv.name, 0), ("bmask_sb", 0)],
                writes=[(sbase.name, hh)])

    def local_hg(hg):
        g3 = hg % 2
        kpT_hg = kpT_g[g3]
        v_hg = v_g[g3]
        spall = spall_g[g3]
        sbase = sbase_g[g3]
        for tc in range(NC_OWN):
            pnd = pnd2[tc % 2]
            for hh in range(HGW):
                h = HGW * hg + hh
                P.op("t", lambda e, tc=tc, h=h, hh=hh, kpT_hg=kpT_hg: e.matmul(
                    psc[:, 128 * hh:128 * (hh + 1)],
                    kpT_hg[:, tc, 128 * hh:128 * (hh + 1)],
                    sigq[:, h, 128 * tc:128 * (tc + 1)], start=True, stop=True),
                    reads=[(kpT_hg.name, tc), ("sigq", h)],
                    writes=[("psc", hh)] + ([("psc", "qb")] if (hg == 0 and tc == 0 and hh == 0) else []),
                    dbg=f"sc{h}")
            P.op("v", lambda e: e.tensor_tensor(scm[:], psc[:, 0:BW], mask01[:], ALU.mult),
                 reads=[("psc", hh2) for hh2 in range(HGW)] + [("mask01", 0)],
                 writes=[("scm", 0)], dbg=f"mask{hg}t{tc}")
            nmm = [0]
            NTOT = HGW * 6

            def flg():
                nmm[0] += 1
                return dict(start=(nmm[0] == 1), stop=(nmm[0] == NTOT))
            for hh in range(HGW):
                h = HGW * hg + hh
                P.op("t", lambda e, tc=tc, h=h, hh=hh, spall=spall, pnd=pnd, f=flg(): e.matmul(
                    pnd[:, 128 * hh:128 * (hh + 1)],
                    sigq[:, h, 128 * tc:128 * (tc + 1)],
                    spall[:, tc, hh, 0:128], **f),
                    reads=[("sigq", h), (spall.name, (tc, hh))],
                    writes=[(pnd.name, 0)] + ([("pnum0", "rel")] if (hg == 0 and tc == 0 and hh == 0) else []),
                    dbg=f"num{h}")
                P.op("t", lambda e, tc=tc, h=h, hh=hh, sbase=sbase, pnd=pnd, f=flg(): e.matmul(
                    pnd[:, 128 * hh:128 * (hh + 1)],
                    sigq[:, h, 128 * tc:128 * (tc + 1)],
                    sbase[:, hh, 0:128], **f),
                    reads=[("sigq", h), (sbase.name, hh)],
                    writes=[(pnd.name, 0)], dbg=f"numb{h}")
                P.op("t", lambda e, tc=tc, h=h, hh=hh, spall=spall, pnd=pnd, f=flg(): e.matmul(
                    pnd[:, 256 + hh:257 + hh],
                    sigq[:, h, 128 * tc:128 * (tc + 1)],
                    spall[:, tc, hh, 128:129], **f),
                    reads=[("sigq", h), (spall.name, (tc, hh))],
                    writes=[(pnd.name, 0)], dbg=f"den{h}")
                P.op("t", lambda e, tc=tc, h=h, hh=hh, sbase=sbase, pnd=pnd, f=flg(): e.matmul(
                    pnd[:, 256 + hh:257 + hh],
                    sigq[:, h, 128 * tc:128 * (tc + 1)],
                    sbase[:, hh, 128:129], **f),
                    reads=[("sigq", h), (sbase.name, hh)],
                    writes=[(pnd.name, 0)], dbg=f"denb{h}")
            for hh in range(HGW):
                h = HGW * hg + hh
                P.op("t", lambda e, tc=tc, hh=hh, v_hg=v_hg, pnd=pnd, f=flg(): e.matmul(
                    pnd[:, 128 * hh:128 * (hh + 1)],
                    scm[:, 128 * hh:128 * (hh + 1)],
                    v_hg[:, tc, 128 * hh:128 * (hh + 1)], **f),
                    reads=[("scm", 0), (v_hg.name, tc)],
                    writes=[(pnd.name, 0)], dbg=f"numi{h}")
                P.op("t", lambda e, tc=tc, hh=hh, pnd=pnd, f=flg(): e.matmul(
                    pnd[:, 256 + hh:257 + hh],
                    scm[:, 128 * hh:128 * (hh + 1)],
                    ones_col[:, 0:1], **f),
                    reads=[("scm", 0), ("ones_col", 0)],
                    writes=[(pnd.name, 0)], dbg=f"deni{h}")
            P.op("v", lambda e, pnd=pnd: e.tensor_scalar_max(
                den_sb[:], pnd[:, 256:256 + HGW], ATT_EPS),
                reads=[(pnd.name, 0)], writes=[("den_sb", 0)])
            P.op("v", lambda e: e.reciprocal(den_sb[:], den_sb[:]),
                 reads=[("den_sb", 0)], writes=[("den_sb", 0)])
            P.op("v", lambda e, hg=hg: e.tensor_tensor(
                den_sb[:], den_sb[:],
                csb[:, 64 + HGW * hg:64 + HGW * (hg + 1)], ALU.mult),
                reads=[("den_sb", 0), ("csb", 0)], writes=[("den_sb", 0)])
            P.op("v", lambda e, tc=tc, hg=hg, pnd=pnd: e.tensor_tensor(
                attn_fin[:, tc, 256 * hg:256 * (hg + 1)]
                .rearrange("p (q e) -> p q e", e=128),
                pnd[:, 0:256].rearrange("p (q e) -> p q e", e=128),
                den_sb[:, 0:HGW].rearrange("p q -> p q ()")
                .to_broadcast([128, HGW, 128]), ALU.mult),
                reads=[(pnd.name, 0), ("den_sb", 0)],
                writes=[("attn_fin", (tc, HGW * hg)),
                        ("attn_fin", (tc, HGW * hg + 1))], dbg=f"loc{hg}")

    k_loads(0)
    v_loads(0)
    produce_hg(0, mid=lambda: k_loads(1))
    v_loads(1)
    produce_hg(1)

    wq_q2 = [P.sb(f"wq_q{i}", [128, ND, 256], BF16) for i in range(2)]
    qctr = [0]
    for j in range(ND):
        P.dma("g", wq_q2[0][:, j, :], wqT[128 * j:128 * (j + 1), 0:256],
              writes=[(wq_q2[0].name, j)])
    for qq in range(8):
        wq_q = wq_q2[qq % 2]
        if qq + 1 < 8:
            wn = wq_q2[(qq + 1) % 2]
            for j in range(ND):
                P.dma("g", wn[:, j, :],
                      wqT[128 * j:128 * (j + 1), 256 * (qq + 1):256 * (qq + 2)],
                      writes=[(wn.name, j)])
        for hh in range(2):
            h = 2 * qq + hh
            for t2 in range(2):
                pk = pk_rot[qctr[0] % 2]
                qctr[0] += 1
                for j in range(ND):
                    P.op("t", lambda e, h=h, hh=hh, j=j, t2=t2, pk=pk, wq_q=wq_q: e.matmul(
                        pk[:, 0:512],
                        wq_q[:, j, 128 * hh:128 * (hh + 1)],
                        ln1T[:, j, 512 * t2:512 * (t2 + 1)],
                        start=(j == 0), stop=(j == ND - 1)),
                        reads=[(wq_q.name, j), ("ln1T", (j, 0))],
                        writes=[(pk.name, 0)], dbg=f"q{h}")
                emit_elu(pk[:, 0:512], [(pk.name, 0)],
                         sigq[:, h, 512 * t2:512 * (t2 + 1)], [("sigq", h)], 512)
            for j in range(ND):
                P.op("t", lambda e, h=h, hh=hh, j=j, wq_q=wq_q: e.matmul(
                    psc[0:128, h:h + 1], wq_q[:, j, 128 * hh:128 * (hh + 1)],
                    xbar[:, j:j + 1], start=(j == 0), stop=(j == ND - 1)),
                    reads=[(wq_q.name, j), ("xbar", j)],
                    writes=[("psc", "qb")])
    P.op("s", lambda e: e.copy(qbar_sb[:], psc[:, 0:16]),
         reads=[("psc", "qb")], writes=[("qbar_sb", 0)])
    P.free(wq_q2[1]); P.free(wq_q2[0])

    for h in range(H):
        P.op("t", lambda e, h=h: e.matmul(
            pnum[0:1, 4 * h:4 * h + 4], qbar_sb[:, h:h + 1],
            mn_sb[:, 4 * h:4 * h + 4], start=True, stop=True),
            reads=[("qbar_sb", 0), ("mn_sb", 0)], writes=[("pnum0", "rel")])
    P.op("t", lambda e: e.matmul(pnum[0:1, 64:128], ones_col[:, 0:1], mn_sb[:],
                                 start=True, stop=True),
         reads=[("ones_col", 0), ("mn_sb", 0)], writes=[("pnum0", "rel")])

    P.op("v", lambda e: e.tensor_reduce(
        sc1[0:1, 112:116], pnum[0:1, 64:128].rearrange("p (m h) -> p m h", h=16),
        axis=AX.X, op=ALU.add),
        reads=[("pnum0", "rel")], writes=[("sc1", "sm")])
    P.op("v", lambda e: e.tensor_scalar(
        sc1[0:1, 121:125], sc1[0:1, 112:116], 1e-6, None, ALU.is_lt),
        reads=[("sc1", "sm")], writes=[("sc1", "empty")])
    P.op("v", lambda e: e.scalar_tensor_tensor(
        sc1[0:1, 0:64].rearrange("p (h m) -> p h m", m=4),
        sc1[0:1, 121:125].rearrange("p m -> p () m").to_broadcast([1, 16, 4]),
        -1e30, pnum[0:1, 0:64].rearrange("p (h m) -> p h m", m=4),
        ALU.mult, ALU.add),
        reads=[("sc1", "empty"), ("pnum0", "rel")], writes=[("sc1", "rel")])
    P.op("v", lambda e: e.tensor_reduce(
        sc1[0:1, 64:80], sc1[0:1, 0:64].rearrange("p (h m) -> p h m", m=4),
        axis=AX.X, op=ALU.max),
        reads=[("sc1", "rel")], writes=[("sc1", "hmax")])
    P.op("v", lambda e: e.tensor_tensor(
        sc1[0:1, 0:64].rearrange("p (h m) -> p h m", m=4),
        sc1[0:1, 0:64].rearrange("p (h m) -> p h m", m=4),
        sc1[0:1, 64:80].rearrange("p h -> p h ()").to_broadcast([1, 16, 4]),
        ALU.subtract),
        reads=[("sc1", "rel"), ("sc1", "hmax")], writes=[("sc1", "rel")])
    P.op("s", lambda e: e.activation(sc1[0:1, 0:64], sc1[0:1, 0:64], AF.Exp),
         reads=[("sc1", "rel")], writes=[("sc1", "rel")])
    P.op("v", lambda e: e.tensor_reduce(
        sc1[0:1, 80:96], sc1[0:1, 0:64].rearrange("p (h m) -> p h m", m=4),
        axis=AX.X, op=ALU.add),
        reads=[("sc1", "rel")], writes=[("sc1", "hsum")])
    P.op("v", lambda e: e.reciprocal(sc1[0:1, 80:96], sc1[0:1, 80:96]),
         reads=[("sc1", "hsum")], writes=[("sc1", "hsum")])
    P.op("v", lambda e: e.tensor_tensor(
        sc1[0:1, 0:64].rearrange("p (h m) -> p h m", m=4),
        sc1[0:1, 0:64].rearrange("p (h m) -> p h m", m=4),
        sc1[0:1, 80:96].rearrange("p h -> p h ()").to_broadcast([1, 16, 4]),
        ALU.mult),
        reads=[("sc1", "rel"), ("sc1", "hsum")], writes=[("sc1", "rel")])
    P.dma("y", sc1[0:1, 96:112], gate1, writes=[("sc1", "g")])
    P.op("s", lambda e: e.activation(sc1[0:1, 96:112], sc1[0:1, 96:112], AF.Sigmoid),
         reads=[("sc1", "g")], writes=[("sc1", "g")])
    P.op("v", lambda e: e.tensor_tensor(
        sc1[0:1, 120:121], sc1[0:1, 121:122], sc1[0:1, 122:123], ALU.mult),
        reads=[("sc1", "empty")], writes=[("sc1", "nae")])
    P.op("v", lambda e: e.tensor_tensor(
        sc1[0:1, 120:121], sc1[0:1, 120:121], sc1[0:1, 123:124], ALU.mult),
        reads=[("sc1", "empty"), ("sc1", "nae")], writes=[("sc1", "nae")])
    P.op("v", lambda e: e.tensor_tensor(
        sc1[0:1, 120:121], sc1[0:1, 120:121], sc1[0:1, 124:125], ALU.mult),
        reads=[("sc1", "empty"), ("sc1", "nae")], writes=[("sc1", "nae")])
    P.op("v", lambda e: e.tensor_scalar(
        sc1[0:1, 120:121], sc1[0:1, 120:121], -1.0, 1.0, ALU.mult, ALU.add),
        reads=[("sc1", "nae")], writes=[("sc1", "nae")])
    P.op("v", lambda e: e.tensor_scalar(
        sc1[0:1, 116:120], sc1[0:1, 121:125], -1.0, 1.0, ALU.mult, ALU.add),
        reads=[("sc1", "empty")], writes=[("sc1", "nem")])
    P.op("v", lambda e: e.tensor_tensor(
        sc1[0:1, 128:192].rearrange("p (h m) -> p h m", m=4),
        sc1[0:1, 0:64].rearrange("p (h m) -> p h m", m=4),
        sc1[0:1, 96:112].rearrange("p h -> p h ()").to_broadcast([1, 16, 4]),
        ALU.mult),
        reads=[("sc1", "rel"), ("sc1", "g")], writes=[("sc1", "wgg")])
    P.op("v", lambda e: e.tensor_tensor(
        sc1[0:1, 128:192].rearrange("p (h m) -> p h m", m=4),
        sc1[0:1, 128:192].rearrange("p (h m) -> p h m", m=4),
        sc1[0:1, 116:120].rearrange("p m -> p () m").to_broadcast([1, 16, 4]),
        ALU.mult),
        reads=[("sc1", "wgg"), ("sc1", "nem")], writes=[("sc1", "wgg")])
    P.op("v", lambda e: e.tensor_tensor(
        sc1[0:1, 128:192], sc1[0:1, 128:192],
        sc1[0:1, 120:121].to_broadcast([1, 64]), ALU.mult),
        reads=[("sc1", "wgg"), ("sc1", "nae")], writes=[("sc1", "wgg")])
    P.op("v", lambda e: e.tensor_scalar(
        sc1[0:1, 192:208], sc1[0:1, 96:112], -1.0, 1.0, ALU.mult, ALU.add),
        reads=[("sc1", "g")], writes=[("sc1", "gloc")])
    P.op("v", lambda e: e.tensor_copy(sc1b[0:1, 0:64], sc1[0:1, 128:192]),
         reads=[("sc1", "wgg")], writes=[("sc1b", 0)])
    P.op("v", lambda e: e.tensor_copy(sc1b[0:1, 64:80], sc1[0:1, 192:208]),
         reads=[("sc1", "gloc")], writes=[("sc1b", 0)])
    P.op("t", lambda e: e.matmul(psc[:, 0:80], ones_row[0:1, :], sc1b[0:1, :],
                                 start=True, stop=True),
         reads=[("ones_row", 0), ("sc1b", 0)], writes=[("psc", "qb")])
    P.op("s", lambda e: e.copy(csb[:], psc[:, 0:80]),
         reads=[("psc", "qb")], writes=[("csb", 0)])

    prep_hg(0)
    k_loads(2)
    v_loads(2)
    local_hg(0)
    for hg in range(2, NHG):
        if hg + 1 < NHG:
            produce_hg(hg, mid=lambda hgm=hg - 1, hgn=hg + 1:
                       (prep_hg(hgm), k_loads(hgn)))
            v_loads(hg + 1)
        else:
            produce_hg(hg, mid=lambda hgm=hg - 1: prep_hg(hgm))
        local_hg(hg - 1)
    prep_hg(NHG - 1)
    local_hg(NHG - 1)

    P.free(pkt2[1]); P.free(pkt2[0])
    P.free(pnd2[1]); P.free(pnd2[0]); P.free(psc); P.free(pkv)
    P.free(pk2); P.free(pk1)
    P.free(den_sb); P.free(scm)
    for b in reversed(sbase_g + recv_g + exch_g + spall_g + v_g + kpT_g):
        P.free(b)
    P.free(kp_hg)
    P.free(wv_d[0]); P.free(wk_d[0])
    P.free(elu_e2[1]); P.free(elu_e2[0]); P.free(elu_t2[1]); P.free(elu_t2[0])
    P.free(ln1T)

    mem_sb = P.sb("mem_sb", [128, M * H * 128], BF16)
    P.dma("g", mem_sb[:], memT, writes=[("mem_sb", 0)])
    pnall = P.ps("pnall", [128, 512])
    nmm2 = [0]
    for tc in range(NC_OWN):
        for h in range(H):
            nmm2[0] += 1
            P.op("t", lambda e, tc=tc, h=h,
                 f=dict(start=(nmm2[0] == 1), stop=(nmm2[0] == 128)): e.matmul(
                pnall[:, 64 * tc + 4 * h:64 * tc + 4 * (h + 1)],
                sigq[:, h, 128 * tc:128 * (tc + 1)],
                mn_sb[:, 4 * h:4 * (h + 1)], **f),
                reads=[("sigq", h), ("mn_sb", 0)],
                writes=[("pnall", 0)], dbg="nall")
    csn_all = P.sb("csn_all", [128, 512], F32)
    P.op("v", lambda e: e.tensor_scalar_max(csn_all[:], pnall[:, 0:512], ATT_EPS),
         reads=[("pnall", 0)], writes=[("csn_all", 0)])
    P.op("v", lambda e: e.reciprocal(csn_all[:], csn_all[:]),
         reads=[("csn_all", 0)], writes=[("csn_all", 0)])
    P.op("v", lambda e: e.tensor_tensor(
        csn_all[:].rearrange("p (tc q) -> p tc q", q=64),
        csn_all[:].rearrange("p (tc q) -> p tc q", q=64),
        csb[:, 0:64].rearrange("p q -> p () q").to_broadcast([128, 8, 64]),
        ALU.mult),
        reads=[("csn_all", 0), ("csb", 0)], writes=[("csn_all", 0)])
    P.free(pnall)

    dg = P.sb("dg", [128, 3, 8, 128], BF16)
    am_sb = P.sb("am_sb", [128, 2, 2, 4, 128], BF16)
    pam2 = [P.ps(f"pamb{i}", [128, 1024]) for i in range(2)]
    pmon2 = [P.ps(f"pmon{i}", [128, 264]) for i in range(2)]
    pat2 = [P.ps(f"pat{i}", [128, 128], BF16) for i in range(2)]
    pairs = [(tc, pp) for tc in range(NC_OWN) for pp in range(8)]

    def dg_emit(i):
        tc, pp = pairs[i]
        P.op("g", lambda e, tc=tc, pp=pp, st=i % 3: e.affine_select(
            out=dg[:, st, :, :],
            in_=csn_all[:, 64 * tc + 8 * pp:64 * tc + 8 * (pp + 1)]
                .rearrange("p q -> p q ()").to_broadcast([128, 8, 128]),
            compare_op=ALU.is_equal, fill=0.0,
            base=0, pattern=[[0, 8], [1, 128]], channel_multiplier=-1),
            reads=[("csn_all", 0)], writes=[("dg", i % 3)], dbg=f"dg{i}")

    def mem_front(i):
        tc, pp = pairs[i]
        st_i = i % 2
        pa = pam2[st_i]
        for hh in range(2):
            h = 2 * pp + hh
            for m in range(M):
                P.op("t", lambda e, tc=tc, h=h, hh=hh, m=m, pa=pa: e.matmul(
                    pa[:, 512 * hh + 128 * m:512 * hh + 128 * (m + 1)],
                    sigq[:, h, 128 * tc:128 * (tc + 1)],
                    mem_sb[:, 128 * (m * H + h):128 * (m * H + h + 1)],
                    start=(m == 0), stop=(m == M - 1)),
                    reads=[("sigq", h), ("mem_sb", 0)],
                    writes=[(pa.name, 0)], dbg=f"am{h}m{m}")
        P.op("s", lambda e, st_i=st_i, pa=pa: e.copy(
            am_sb[:, st_i, :, :, :], pa[:, 0:1024]),
            reads=[(pa.name, 0)], writes=[("am_sb", st_i)])

    def mem_back1(i):
        tc, pp = pairs[i]
        st_i = i % 2
        dst = i % 3
        pmon = pmon2[st_i]
        for hh in range(2):
            h = 2 * pp + hh
            for m in range(M):
                P.op("t", lambda e, st_i=st_i, dst=dst, hh=hh, m=m, pmon=pmon: e.matmul(
                    pmon[:, 128 * hh:128 * (hh + 1)],
                    dg[:, dst, 4 * hh + m, :],
                    am_sb[:, st_i, hh, m, :],
                    start=(m == 0), stop=(m == M - 1)),
                    reads=[("dg", dst), ("am_sb", st_i)],
                    writes=[(pmon.name, 0)], dbg=f"dmm{h}")
        P.op("v", lambda e, tc=tc, pp=pp, pmon=pmon: e.tensor_tensor(
            attn_fin[:, tc, 256 * pp:256 * (pp + 1)],
            pmon[:, 0:256],
            attn_fin[:, tc, 256 * pp:256 * (pp + 1)], ALU.add),
            reads=[(pmon.name, 0), ("attn_fin", (tc, 2 * pp)),
                   ("attn_fin", (tc, 2 * pp + 1))],
            writes=[("attn_fin", (tc, 2 * pp)), ("attn_fin", (tc, 2 * pp + 1))],
            dbg=f"mfin{pp}")

    def mem_back2(i):
        tc, pp = pairs[i]
        for hh in range(2):
            h = 2 * pp + hh
            pat = pat2[hh]
            P.op("t", lambda e, tc=tc, h=h, pat=pat: e.transpose(
                pat[:, 0:128], attn_fin[:, tc, 128 * h:128 * (h + 1)],
                ident_bf[:]),
                reads=[("attn_fin", (tc, h)), ("ident_bf", 0)],
                writes=[(pat.name, 0)], dbg=f"aT{h}")
            P.op("s", lambda e, tc=tc, h=h, pat=pat: e.copy(
                aT[:, h, 128 * tc:128 * (tc + 1)], pat[:, 0:128]),
                reads=[(pat.name, 0)], writes=[("aT", (h, tc))], dbg=f"aTc{h}")

    NP_ = len(pairs)
    dg_emit(0)
    dg_emit(1)
    mem_front(0)
    mem_front(1)
    mem_back1(0)
    for i in range(2, NP_):
        mem_front(i)
        dg_emit(i)
        mem_back1(i - 1)
        mem_back2(i - 2)
    mem_back1(NP_ - 1)
    mem_back2(NP_ - 2)
    mem_back2(NP_ - 1)

    P.free(pat2[1]); P.free(pat2[0])
    P.free(pmon2[1]); P.free(pmon2[0])
    P.free(pam2[1]); P.free(pam2[0])
    P.free(am_sb); P.free(dg)
    P.free(csn_all)
    P.free(attn_fin); P.free(mem_sb)
    P.free(sigq)

    x1 = P.sb("x1", [128, NC_OWN, D], F32)
    wo_b = [P.sb(f"wo_b{i}", [128, ND, 1024], BF16) for i in range(2)]
    xre = [P.sb(f"xre{i}", [128, 2048], F32) for i in range(2)]
    ln2buf = [P.sb(f"ln2buf{i}", [128, D], BF16) for i in range(2)]
    pw1 = P.ps("pw1", [128, 2048])
    pw2 = P.ps("pw2", [128, 2048])
    pw_rot = [pw1, pw2]
    for half in range(2):
        wob = wo_b[half]
        for j in range(ND):
            P.dma("g", wob[:, j, :],
                  woT[128 * j:128 * (j + 1), 1024 * half:1024 * (half + 1)],
                  writes=[(wob.name, j)])
    for tc in range(NC_OWN):
        pw = pw_rot[tc % 2]
        xr_b = xre[tc % 2]
        P.dma("y", xr_b[:], xo[128 * tc:128 * (tc + 1), :],
              writes=[(xr_b.name, 0)])
        for half in range(2):
            wob = wo_b[half]
            for j in range(ND):
                for n in range(2):
                    P.op("t", lambda e, tc=tc, j=j, n=n, half=half, pw=pw, wob=wob: e.matmul(
                        pw[:, 1024 * half + 512 * n:1024 * half + 512 * (n + 1)],
                        aT[:, j, 128 * tc:128 * (tc + 1)],
                        wob[:, j, 512 * n:512 * (n + 1)],
                        start=(j == 0), stop=(j == ND - 1)),
                        reads=[("aT", (j, tc)), (wob.name, j)],
                        writes=[(pw.name, 2 * half + n)], dbg=f"wo{tc}")
        P.op("v", lambda e, tc=tc, pw=pw, xr_b=xr_b: e.tensor_tensor(
            x1[:, tc, :], pw[:], xr_b[:], ALU.add),
            reads=[(pw.name, i) for i in range(4)] + [(xr_b.name, 0)],
            writes=[("x1", tc)], dbg=f"res1_{tc}")
        lb = ln2buf[tc % 2]
        for j in range(4):
            P.op("v", lambda e, j=j, tc=tc: e.bn_stats(
                stat[:, 6 * j:6 * j + 6], x1[:, tc, 512 * j:512 * (j + 1)]),
                reads=[("x1", tc)], writes=[("stat", j)])
        P.op("v", lambda e: e.bn_aggr(mv[:], stat[:]),
             reads=[("stat", j) for j in range(4)], writes=[("mv", 0)])
        P.op("s", lambda e: e.activation(rstd[:], mv[:, 1:2], AF.Sqrt, bias=eps_sb[:]),
             reads=[("mv", 0), ("eps_sb", 0)], writes=[("rstd", 0)])
        P.op("v", lambda e: e.reciprocal(rstd[:], rstd[:]),
             reads=[("rstd", 0)], writes=[("rstd", 0)])
        P.op("v", lambda e: e.tensor_tensor(nmean[:], mv[:, 0:1], rstd[:], ALU.mult),
             reads=[("mv", 0), ("rstd", 0)], writes=[("nmean", 0)])
        P.op("v", lambda e: e.tensor_scalar_mul(nmean[:], nmean[:], -1.0),
             reads=[("nmean", 0)], writes=[("nmean", 0)])
        P.op("s", lambda e, tc=tc, lb=lb: e.activation(
            lb[:], x1[:, tc, :], AF.Identity, bias=nmean[:], scale=rstd[:]),
            reads=[("x1", tc), ("rstd", 0), ("nmean", 0)], writes=[(lb.name, 0)])
        P.dma("s", ln2_dram[128 * tc:128 * (tc + 1), :], lb[:],
              reads=[(lb.name, 0)], writes=[("ln2_dram", tc)])
        P.dma("y", x1_dram[128 * tc:128 * (tc + 1), :], x1[:, tc, :],
              reads=[("x1", tc)], writes=[("x1_dram", tc)])
    P.free(pw2); P.free(pw1)
    P.free(ln2buf[1]); P.free(ln2buf[0])
    P.free(xre[1]); P.free(xre[0]); P.free(wo_b[1]); P.free(wo_b[0])
    P.free(aT)
    P.free(x1)

    sh_basis = P.sb("sh_basis", [128, 16, T_OWN], BF16)
    uT = P.sb("uT", [128, 8, T_OWN], BF16, is_transpose_dest=True)
    c_sb = P.sb("c_sb", [128, NC_OWN, 8], F32)
    ln2T = P.sb("ln2T", [128, ND, T_OWN], BF16, is_transpose_dest=True)
    l2_all = [("ln2_dram", tc) for tc in range(NC_OWN)]
    for j in range(ND):
        P.dma("y", ln2T[:, j, :],
              ln2_dram[:, 128 * j:128 * (j + 1)],
              transpose=True, reads=l2_all, writes=[("ln2T", j)])

    wr_sb = P.sb("wr_sb", [128, ND, E_EXP], BF16)
    for j in range(ND):
        P.dma("g", wr_sb[:, j, :], wrt[128 * j:128 * (j + 1), :],
              writes=[("wr_sb", j)])
    amps_sb = P.sb("amps_sb", [128, 16], F32)
    esc_sb = P.sb("esc_sb", [128, 1], F32)
    mix_sb = P.sb("mix_sb", [128, 8], BF16)
    mixf = P.sb("mixf", [128, 9], F32)
    P.dma("y", amps_sb[:], amps, writes=[("amps_sb", 0)])
    P.dma("y", esc_sb[:], escale, writes=[("esc_sb", 0)])
    P.op("s", lambda e: e.activation(amps_sb[:], amps_sb[:], AF.Square),
         reads=[("amps_sb", 0)], writes=[("amps_sb", 0)])
    P.op("v", lambda e: e.tensor_tensor(
        mixf[:, 0:8],
        amps_sb[:].rearrange("p (k two) -> p k two", two=2)[:, :, 0],
        amps_sb[:].rearrange("p (k two) -> p k two", two=2)[:, :, 1],
        ALU.add),
        reads=[("amps_sb", 0)], writes=[("mixf", 0)])
    P.op("v", lambda e: e.reduce_sum(mixf[:, 8:9], mixf[:, 0:8], axis=AX.X),
         reads=[("mixf", 0)], writes=[("mixf", 1)])
    P.op("v", lambda e: e.tensor_scalar_add(mixf[:, 8:9], mixf[:, 8:9], FFN_EPS),
         reads=[("mixf", 1)], writes=[("mixf", 1)])
    P.op("v", lambda e: e.reciprocal(mixf[:, 8:9], mixf[:, 8:9]),
         reads=[("mixf", 1)], writes=[("mixf", 1)])
    P.op("v", lambda e: e.tensor_tensor(
        mixf[:, 8:9], mixf[:, 8:9], esc_sb[:], ALU.mult),
        reads=[("mixf", 1), ("esc_sb", 0)], writes=[("mixf", 1)])
    P.op("v", lambda e: e.tensor_scalar(
        mix_sb[:], mixf[:, 0:8], mixf[:, 8:9], None, ALU.mult),
        reads=[("mixf", 0), ("mixf", 1)], writes=[("mix_sb", 0)])

    probs2 = [P.sb(f"probs{i}", [128, 128], F32) for i in range(2)]
    rt12 = [P.sb(f"rt1_{i}", [128, 8], F32) for i in range(2)]
    rtv2 = [P.sb(f"rtv{i}", [128, 3], F32) for i in range(2)]
    wmask2 = [P.sb(f"wmask{i}", [128, 128], BF16) for i in range(2)]
    prt2 = [P.ps(f"prt{i}", [128, 128]) for i in range(3)]
    pc2 = [P.ps(f"pc{i}", [128, 8]) for i in range(2)]
    pwt2 = [P.ps(f"pwt{i}", [128, 128], BF16) for i in range(2)]

    def rt_front(tc):
        prt = prt2[tc % 3]
        probs = probs2[tc % 2]
        rt1 = rt12[tc % 2]
        rtv = rtv2[tc % 2]
        wmask = wmask2[tc % 2]
        for j in range(ND):
            P.op("t", lambda e, tc=tc, j=j, prt=prt: e.matmul(
                prt[:, 0:128], ln2T[:, j, 128 * tc:128 * (tc + 1)], wr_sb[:, j, :],
                start=(j == 0), stop=(j == ND - 1)),
                reads=[("ln2T", j), ("wr_sb", j)], writes=[(prt.name, 0)],
                dbg=f"rt{tc}")
        P.op("v", lambda e, prt=prt, rtv=rtv: e.reduce_max(
            rtv[:, 0:1], prt[:, 0:128], axis=AX.X, negate=True),
            reads=[(prt.name, 0)], writes=[(rtv.name, 0)])
        P.op("s", lambda e, prt=prt, probs=probs, rtv=rtv: e.activation(
            probs[:], prt[:, 0:128], AF.Exp, bias=rtv[:, 0:1], accum_out=rtv[:, 1:2]),
            reads=[(prt.name, 0), (rtv.name, 0)],
            writes=[(probs.name, 0), (rtv.name, 1)])
        P.op("v", lambda e, probs=probs, rt1=rt1: e.max(out=rt1[:], in_=probs[:]),
             reads=[(probs.name, 0)], writes=[(rt1.name, 0)])
        P.op("v", lambda e, probs=probs, rt1=rt1, wmask=wmask: e.scalar_tensor_tensor(
            wmask[:], probs[:], rt1[:, 3:4], probs[:], ALU.is_ge, ALU.mult),
            reads=[(probs.name, 0), (rt1.name, 0)], writes=[(wmask.name, 0)])
        P.op("v", lambda e, wmask=wmask, rtv=rtv: e.reduce_sum(
            rtv[:, 2:3], wmask[:], axis=AX.X),
            reads=[(wmask.name, 0)], writes=[(rtv.name, 2)])
        P.op("v", lambda e, rtv=rtv: e.scalar_tensor_tensor(
            rtv[:, 2:3], rtv[:, 1:2], FFN_EPS, rtv[:, 2:3], ALU.mult, ALU.add),
            reads=[(rtv.name, 1), (rtv.name, 2)], writes=[(rtv.name, 2)])
        P.op("v", lambda e, rtv=rtv: e.reciprocal(rtv[:, 2:3], rtv[:, 2:3]),
             reads=[(rtv.name, 2)], writes=[(rtv.name, 2)])

    def rt_back(tc):
        rtv = rtv2[tc % 2]
        wmask = wmask2[tc % 2]
        pwt = pwt2[tc % 2]
        pc = pc2[tc % 2]
        P.op("t", lambda e, pwt=pwt, wmask=wmask: e.transpose(
            pwt[:, 0:128], wmask[:], ident_bf[:]),
            reads=[(wmask.name, 0), ("ident_bf", 0)], writes=[(pwt.name, 0)])
        P.op("s", lambda e, pwt=pwt, wmask=wmask: e.copy(wmask[:], pwt[:, 0:128]),
             reads=[(pwt.name, 0)], writes=[(wmask.name, 0)])
        P.op("t", lambda e, pc=pc, wmask=wmask: e.matmul(
            pc[:, 0:8], wmask[:], mix_sb[:], start=True, stop=True),
            reads=[(wmask.name, 0), ("mix_sb", 0)], writes=[(pc.name, 0)])
        P.op("v", lambda e, tc=tc, pc=pc, rtv=rtv: e.tensor_scalar(
            c_sb[:, tc, :], pc[:, 0:8], rtv[:, 2:3], None, ALU.mult),
            reads=[(pc.name, 0), (rtv.name, 2)], writes=[("c_sb", tc)])

    rt_front(0)
    for tc in range(1, NC_OWN):
        rt_front(tc)
        rt_back(tc - 1)
    rt_back(NC_OWN - 1)
    P.free(pwt2[1]); P.free(pwt2[0]); P.free(pc2[1]); P.free(pc2[0])
    P.free(prt2[2]); P.free(prt2[1]); P.free(prt2[0])
    P.free(wmask2[1]); P.free(wmask2[0]); P.free(rtv2[1]); P.free(rtv2[0])
    P.free(rt12[1]); P.free(rt12[0]); P.free(probs2[1]); P.free(probs2[0])
    P.free(mixf); P.free(mix_sb); P.free(esc_sb); P.free(amps_sb)
    P.free(wr_sb)

    u_acc = P.sb("u_acc", [128, NC_OWN, I_DIM], F32)
    wgq = [P.sb(f"wgq{i}", [128, ND, 512], BF16) for i in range(2)]
    wuq = [P.sb(f"wuq{i}", [128, ND, 512], BF16) for i in range(2)]
    silu_g = P.sb("silu_g", [128, NC_OWN, 512], BF16)
    ut2 = [P.sb(f"ut{i}", [128, 512], F32) for i in range(2)]
    pg_rot = [P.ps(f"pg{i}", [128, 512]) for i in range(3)]
    pu_rot = [P.ps(f"pu{i}", [128, 512]) for i in range(3)]

    def ffn_loads(q):
        wgb = wgq[q % 2]
        wub = wuq[q % 2]
        for j in range(ND):
            P.dma("g", wgb[:, j, :],
                  wg[128 * j:128 * (j + 1), 512 * q:512 * (q + 1)],
                  writes=[(wgb.name, j)])
            P.dma("g", wub[:, j, :],
                  wu[128 * j:128 * (j + 1), 512 * q:512 * (q + 1)],
                  writes=[(wub.name, j)])

    ffn_loads(0)
    ffn_loads(1)
    for q in range(16):
        k_idx = q // 2
        ihalf = q % 2
        wgb = wgq[q % 2]
        wub = wuq[q % 2]
        for tc in range(NC_OWN):
            pg = pg_rot[tc % 3]
            pu = pu_rot[tc % 3]
            ut = ut2[tc % 2]
            for j in range(ND):
                P.op("t", lambda e, tc=tc, j=j, pg=pg, wgb=wgb: e.matmul(
                    pg[:], ln2T[:, j, 128 * tc:128 * (tc + 1)], wgb[:, j, :],
                    start=(j == 0), stop=(j == ND - 1)),
                    reads=[("ln2T", j), (wgb.name, j)],
                    writes=[(pg.name, 0)], dbg=f"g{q}t{tc}")
            P.op("s", lambda e, tc=tc, pg=pg: e.activation(
                silu_g[:, tc, :], pg[:], AF.Silu),
                reads=[(pg.name, 0)], writes=[("silu_g", tc)])
            for j in range(ND):
                P.op("t", lambda e, tc=tc, j=j, pu=pu, wub=wub: e.matmul(
                    pu[:], ln2T[:, j, 128 * tc:128 * (tc + 1)], wub[:, j, :],
                    start=(j == 0), stop=(j == ND - 1)),
                    reads=[("ln2T", j), (wub.name, j)],
                    writes=[(pu.name, 0)], dbg=f"u{q}t{tc}")
            P.op("v", lambda e, tc=tc, k_idx=k_idx, pu=pu, ut=ut: e.scalar_tensor_tensor(
                ut[:], pu[:], c_sb[:, tc, k_idx:k_idx + 1], silu_g[:, tc, :],
                ALU.mult, ALU.mult),
                reads=[(pu.name, 0), ("c_sb", tc), ("silu_g", tc)],
                writes=[(ut.name, 0)], dbg=f"ut{q}t{tc}")
            if k_idx == 0:
                P.op("v", lambda e, tc=tc, ihalf=ihalf, ut=ut: e.tensor_copy(
                    u_acc[:, tc, 512 * ihalf:512 * (ihalf + 1)], ut[:]),
                    reads=[(ut.name, 0)], writes=[("u_acc", (tc, ihalf))])
            else:
                P.op("v", lambda e, tc=tc, ihalf=ihalf, ut=ut: e.tensor_tensor(
                    u_acc[:, tc, 512 * ihalf:512 * (ihalf + 1)],
                    u_acc[:, tc, 512 * ihalf:512 * (ihalf + 1)], ut[:], ALU.add),
                    reads=[(ut.name, 0), ("u_acc", (tc, ihalf))],
                    writes=[("u_acc", (tc, ihalf))])
        if q + 2 < 16:
            ffn_loads(q + 2)
    for tc in range(NC_OWN):
        P.dma("g", u_dram[128 * tc:128 * (tc + 1), :], u_acc[:, tc, :],
              reads=[("u_acc", (tc, 0)), ("u_acc", (tc, 1))],
              writes=[("u_dram", tc)])
    for p in reversed(pu_rot + pg_rot):
        P.free(p)
    P.free(ut2[1]); P.free(ut2[0]); P.free(silu_g)
    P.free(wuq[1]); P.free(wuq[0]); P.free(wgq[1]); P.free(wgq[0])
    P.free(u_acc)

    u_all = [("u_dram", tc) for tc in range(NC_OWN)]
    for j in range(8):
        P.dma("y", uT[:, j, :],
              u_dram[:, 128 * j:128 * (j + 1)],
              transpose=True, reads=u_all, writes=[("uT", j)])

    swgq = [P.sb(f"swgq{i}", [128, ND, 512], BF16) for i in range(2)]
    swuq = [P.sb(f"swuq{i}", [128, ND, 512], BF16) for i in range(2)]
    ssilu = P.sb("ssilu", [128, T_OWN], BF16)
    psh = [P.ps(f"psh{i}", [128, 1024]) for i in range(4)]

    def sh_loads(q):
        sgb = swgq[q % 2]
        sub = swuq[q % 2]
        for j in range(ND):
            P.dma("g", sgb[:, j, :],
                  swg[128 * j:128 * (j + 1), 512 * q:512 * (q + 1)],
                  writes=[(sgb.name, j)])
            P.dma("g", sub[:, j, :],
                  swu[128 * j:128 * (j + 1), 512 * q:512 * (q + 1)],
                  writes=[(sub.name, j)])

    sh_loads(0)
    sh_loads(1)
    for q in range(4):
        sgb = swgq[q % 2]
        sub = swuq[q % 2]
        for fc in range(4):
            of = 4 * q + fc
            pga = psh[2 * (fc % 2)]
            pua = psh[2 * (fc % 2) + 1]
            for j in range(ND):
                for n in range(2):
                    P.op("t", lambda e, j=j, n=n, fc=fc, pga=pga, sgb=sgb: e.matmul(
                        pga[:, 512 * n:512 * (n + 1)],
                        sgb[:, j, 128 * fc:128 * (fc + 1)],
                        ln2T[:, j, 512 * n:512 * (n + 1)],
                        start=(j == 0), stop=(j == ND - 1)),
                        reads=[(sgb.name, j), ("ln2T", j)],
                        writes=[(pga.name, n)], dbg=f"sg{of}")
            P.op("s", lambda e, pga=pga: e.activation(ssilu[:], pga[:], AF.Silu),
                 reads=[(pga.name, 0), (pga.name, 1)], writes=[("ssilu", 0)])
            for j in range(ND):
                for n in range(2):
                    P.op("t", lambda e, j=j, n=n, fc=fc, pua=pua, sub=sub: e.matmul(
                        pua[:, 512 * n:512 * (n + 1)],
                        sub[:, j, 128 * fc:128 * (fc + 1)],
                        ln2T[:, j, 512 * n:512 * (n + 1)],
                        start=(j == 0), stop=(j == ND - 1)),
                        reads=[(sub.name, j), ("ln2T", j)],
                        writes=[(pua.name, n)], dbg=f"su{of}")
            P.op("v", lambda e, of=of, pua=pua: e.tensor_tensor(
                sh_basis[:, of, :], pua[:], ssilu[:], ALU.mult),
                reads=[(pua.name, 0), (pua.name, 1), ("ssilu", 0)],
                writes=[("sh_basis", of)], dbg=f"sb{of}")
        if q + 2 < 4:
            sh_loads(q + 2)
    for p in reversed(psh):
        P.free(p)
    P.free(ssilu)
    P.free(swuq[1]); P.free(swuq[0]); P.free(swgq[1]); P.free(swgq[0])
    P.free(ln2T)

    wd_sb = P.sb("wd_sb", [128, 8, D], BF16)
    for j in range(8):
        P.dma("g", wd_sb[:, j, :], wd[128 * j:128 * (j + 1), :],
              writes=[("wd_sb", j)])
    swd_sb = P.sb("swd_sb", [128, 16, D], BF16)
    for j in range(16):
        P.dma("g", swd_sb[:, j, :], swd[128 * j:128 * (j + 1), :],
              writes=[("swd_sb", j)])
    ybuf = [P.sb(f"ybuf{i}", [128, D], F32) for i in range(2)]
    x1b = [P.sb(f"x1b{i}", [128, D], F32) for i in range(2)]
    pf = [P.ps(f"pf{i}", [128, 2048]) for i in range(2)]

    for tc in range(NC_OWN):
        pfb = pf[tc % 2]
        yb = ybuf[tc % 2]
        xb = x1b[tc % 2]
        P.dma("y", xb[:], x1_dram[128 * tc:128 * (tc + 1), :],
              reads=[("x1_dram", tc)], writes=[(xb.name, 0)])
        for j in range(8):
            for n in range(4):
                P.op("t", lambda e, tc=tc, j=j, n=n, pfb=pfb: e.matmul(
                    pfb[:, 512 * n:512 * (n + 1)],
                    uT[:, j, 128 * tc:128 * (tc + 1)],
                    wd_sb[:, j, 512 * n:512 * (n + 1)],
                    start=(j == 0), stop=False),
                    reads=[("uT", j), ("wd_sb", j)],
                    writes=[(pfb.name, n)], dbg=f"dn{tc}")
        for j in range(16):
            for n in range(4):
                P.op("t", lambda e, tc=tc, j=j, n=n, pfb=pfb: e.matmul(
                    pfb[:, 512 * n:512 * (n + 1)],
                    sh_basis[:, j, 128 * tc:128 * (tc + 1)],
                    swd_sb[:, j, 512 * n:512 * (n + 1)],
                    start=False, stop=(j == 15)),
                    reads=[("sh_basis", j), ("swd_sb", j)],
                    writes=[(pfb.name, n)], dbg=f"sd{tc}")
        P.op("v", lambda e, pfb=pfb, yb=yb, xb=xb: e.tensor_tensor(
            yb[:], pfb[:], xb[:], ALU.add),
            reads=[(pfb.name, i) for i in range(4)] + [(xb.name, 0)],
            writes=[(yb.name, 0)], dbg=f"res2_{tc}")
        P.dma("y", yout[128 * tc:128 * (tc + 1), :], yb[:],
              reads=[(yb.name, 0)], writes=[("y", tc)])

    P.emit(final_outputs=[("y", tc) for tc in range(NC_OWN)])
    return nc


_NC_CACHE = [None]


def kernel(L0_attn_hidden_states, L0_attn_memories, L0_attn_memory_norms,
           L0_attn_w_q, L0_attn_w_k, L0_attn_w_v, L0_attn_w_o, L0_attn_gate,
           L0_ffn_w_router, L0_ffn_w_gate, L0_ffn_w_up, L0_ffn_w_down,
           L0_ffn_amplitudes, L0_ffn_expert_scale, L0_ffn_shared_w_gate,
           L0_ffn_shared_w_up, L0_ffn_shared_w_down):
    ct = np.ascontiguousarray
    f32 = np.float32
    x = ct(np.asarray(L0_attn_hidden_states, f32))
    shared = dict(
        wqT=ct(np.asarray(L0_attn_w_q, f32).T),
        wkT=ct(np.asarray(L0_attn_w_k, f32).T),
        wvT=ct(np.asarray(L0_attn_w_v, f32).T),
        woT=ct(np.asarray(L0_attn_w_o, f32).T),
        memT=ct(np.asarray(L0_attn_memories, f32).transpose(2, 0, 1, 3)
                .reshape(128, M * H * 128)),
        mnT=ct(np.asarray(L0_attn_memory_norms, f32).transpose(2, 1, 0)
               .reshape(128, H * M)),
        gate1=ct(np.asarray(L0_attn_gate, f32).reshape(1, H)),
        wrt=ct(np.asarray(L0_ffn_w_router, f32)),
        wg=ct(np.asarray(L0_ffn_w_gate, f32)),
        wu=ct(np.asarray(L0_ffn_w_up, f32)),
        wd=ct(np.asarray(L0_ffn_w_down, f32)),
        amps=ct(np.asarray(L0_ffn_amplitudes, f32).reshape(128, 16)),
        escale=ct(np.asarray(L0_ffn_expert_scale, f32).reshape(128, 1)),
        swg=ct(np.asarray(L0_ffn_shared_w_gate, f32)),
        swu=ct(np.asarray(L0_ffn_shared_w_up, f32)),
        swd=ct(np.asarray(L0_ffn_shared_w_down, f32)),
    )
    if _NC_CACHE[0] is None:
        _NC_CACHE[0] = build_nc()
    nc = _NC_CACHE[0]

    in_maps = []
    for core in range(NCORES):
        seq, half = divmod(core, 2)
        m = dict(shared)
        m["xo"] = ct(x[seq, 1024 * half:1024 * (half + 1), :])
        m["bmask"] = np.full((128, 1), 1.0 if half == 1 else 0.0, f32)
        in_maps.append(m)

    res = run_bass_kernel_spmd(nc, in_maps, core_ids=list(range(NCORES)))
    out = np.empty((B, S, D), f32)
    for core in range(NCORES):
        seq, half = divmod(core, 2)
        out[seq, 1024 * half:1024 * (half + 1), :] = res.results[core]["y"]
    return out
